# version 07
# baseline (speedup 1.0000x reference)
import contextlib

import numpy as np
import ml_dtypes

import concourse.bass as bass
import concourse.mybir as mybir
from concourse.bass_utils import run_bass_kernel_spmd

F32 = mybir.dt.float32
BF16 = mybir.dt.bfloat16
AF = mybir.ActivationFunctionType
ALU = mybir.AluOpType
AX = mybir.AxisListType

T = 2048
D = 1024
H = 16
NB = 4
G = 8
NG = 32 // G
EXPERT_ORDER = [8, 0, 1, 2, 3, 4, 5, 6, 7]
NEG = len(EXPERT_ORDER) * NG


def _host_tables():
    slopes = 2.0 ** (-8.0 * np.arange(1, H + 1) / H)
    alibi = np.zeros((128, 1024), np.float32)
    for h in range(H):
        for b in range(NB):
            for kt in range(4 * (b + 1)):
                k = 128 * kt + np.arange(128)
                alibi[:, h * 64 + b * 16 + kt] = slopes[h] * (k - (512 * (b + 1) - 1)) / 8.0
    masks = np.zeros((4, 128, 512), np.float32)
    p = np.arange(128)[:, None]
    j = np.arange(512)[None, :]
    for r in range(4):
        masks[r] = (j >= p + 128 * r).astype(np.float32)
    ones = np.ones((128, 512), np.float32)
    ident = np.eye(128, dtype=np.float32)
    sel = np.zeros((9, 8 * 128), np.float32)
    for e in range(8):
        sel[e, e * 128:(e + 1) * 128] = 1.0
    return alibi, masks, ones, ident, sel


def build_program(phases=5, npair=8, debug=False, ffn_groups=NEG, ffn_parts=4):
    nc = bass.Bass("TRN2", target_bir_lowering=False)

    d_x = nc.dram_tensor("x", [D, T], F32, kind="ExternalInput")
    d_wq = nc.dram_tensor("wq", [D, D], F32, kind="ExternalInput")
    d_wk = nc.dram_tensor("wk", [D, D], F32, kind="ExternalInput")
    d_wv = nc.dram_tensor("wv", [D, D], F32, kind="ExternalInput")
    d_wp = nc.dram_tensor("wp", [D, D], F32, kind="ExternalInput")
    d_bp = nc.dram_tensor("bp", [1, D], F32, kind="ExternalInput")
    d_gw = nc.dram_tensor("gw", [D, 8], F32, kind="ExternalInput")
    d_gb = nc.dram_tensor("gb", [1, 8], F32, kind="ExternalInput")
    d_w1 = nc.dram_tensor("w1", [8, D, 4096], F32, kind="ExternalInput")
    d_w2 = nc.dram_tensor("w2", [8, 4096, D], F32, kind="ExternalInput")
    d_ws1 = nc.dram_tensor("ws1", [D, 4096], F32, kind="ExternalInput")
    d_ws2 = nc.dram_tensor("ws2", [4096, D], F32, kind="ExternalInput")
    d_b1t = nc.dram_tensor("b1t", [128, 288], F32, kind="ExternalInput")
    d_b2c = nc.dram_tensor("b2c", [9, D], F32, kind="ExternalInput")
    d_alibi = nc.dram_tensor("alibi", [128, 1024], F32, kind="ExternalInput")
    d_masks = nc.dram_tensor("masksd", [4, 128, 512], F32, kind="ExternalInput")
    d_ones = nc.dram_tensor("onesd", [128, 512], F32, kind="ExternalInput")
    d_ident = nc.dram_tensor("identd", [128, 128], F32, kind="ExternalInput")
    d_sel = nc.dram_tensor("seld", [9, 8 * 128], F32, kind="ExternalInput")
    d_out = nc.dram_tensor("out", [D, T], F32, kind="ExternalOutput")
    if debug:
        d_dbg = {nm: nc.dram_tensor("dbg_" + nm, shp, F32, kind="ExternalOutput")
                 for nm, shp in [("xln", [128, 8 * T]), ("Qp", [128, T]), ("Kp", [128, T]),
                                 ("Vp", [128, 16 * 130]), ("ctx0", [64, T]), ("ctx1", [64, T]),
                                 ("Pp", [128, 6 * 512]), ("lr", [1, 512]), ("rb", [64, 512]),
                                 ("wqs", [128, 1024]), ("wks", [128, 1024]), ("wvs", [128, 1024]),
                                 ("cfmb", [16, T]), ("gbb", [128, 8]), ("ctm", [128, 16 * 9]),
                                 ("g_lg", [128, 8]), ("g_m1", [128, 1]), ("g_m2", [128, 1]),
                                 ("g_msk", [128, 8]), ("g_eq1", [128, 8]), ("g_eq2", [128, 8]),
                                 ("g_d", [128, 1]), ("g_e", [128, 1]), ("g_r", [128, 1]),
                                 ("g_gw2", [128, 1]), ("g_c1", [128, 8]), ("g_c2", [128, 8])]}

    es = contextlib.ExitStack()
    with es:
        sbt = lambda name, shp, dt: es.enter_context(nc.sbuf_tensor("sb_" + name, shp, dt))
        resid = sbt("resid", [128, 8 * T], F32)
        xln = sbt("xln", [128, 8 * T], BF16)
        b1t = sbt("b1t", [128, 288], F32)
        b2c = sbt("b2c", [9, D], BF16)
        bp = sbt("bp", [1, D], BF16)
        gw = sbt("gw", [128, 8 * 8], BF16)
        gb_row = sbt("gb_row", [1, 8], F32)
        gb_b = sbt("gb_b", [128, 8], F32)
        onesf = sbt("onesf", [1, 128], F32)
        onesb = sbt("onesb", [128, 512], BF16)
        alibi = sbt("alibi", [128, 1024], F32)
        masks = sbt("masks", [128, 4 * 512], BF16)
        ident = sbt("ident", [128, 128], F32)
        sel = sbt("sel", [9, 8 * 128], BF16)
        ident_bf = sbt("ident_bf", [128, 128], BF16)
        tmp1 = sbt("tmp1", [128, 512], F32)
        tmp2 = sbt("tmp2", [128, 512], F32)
        tmpd = [tmp1, tmp2]
        xsq = [sbt(f"xsq{i}", [128, 512], BF16) for i in range(2)]
        srow = sbt("srow", [1, 2 * T], F32)
        msq = sbt("msq", [1, T], F32)

        pb = [es.enter_context(nc.psum_tensor(f"pb{i}", [128, 512], F32)) for i in range(7)]
        pb7 = es.enter_context(nc.psum_tensor("pb7", [128, 1024], BF16))
        s = {k: es.enter_context(nc.semaphore("s_" + k))
             for k in ("pe", "act", "dve", "ldx", "ldc", "ldgc",
                       "wpA", "wpB", "w1s", "w2s", "st")}

        ops = {"pe": [], "act": [], "dve": [], "pool": [], "sync": []}
        n = {k: 0 for k in ("pe", "act", "dve", "ldx", "ldc", "ldgc",
                            "wpA", "wpB", "w1s", "w2s", "st")}
        _esem = {"pe": "pe", "act": "act", "dve": "dve"}

        def emit(eng, fn, waits=None, inc=True, dma=None):
            w = {k: v for k, v in (waits or {}).items() if v > 0}
            inc_sem = dma if dma else (_esem.get(eng) if inc else None)
            amt = 16 if dma else 1
            ops[eng].append((w, fn, inc_sem, amt))
            if inc_sem:
                n[inc_sem] += amt
                return n[inc_sem]
            return None

        for ch in range(2):
            emit("sync", lambda e, ch=ch: e.dma_start(
                out=resid[:].rearrange("p (c t) -> p c t", c=8)[:, 4 * ch:4 * ch + 4],
                in_=d_x[:, :].rearrange("(c p) t -> p c t", p=128)[:, 4 * ch:4 * ch + 4]),
                dma="ldx")
        LDS_X = n["ldx"]
        for fn in (lambda e: e.dma_start(out=alibi[:, :], in_=d_alibi[:, :]),
                   lambda e: e.dma_start(out=ident[:, :], in_=d_ident[:, :]),
                   lambda e: e.dma_start(out=b1t[:, :], in_=d_b1t[:, :]),
                   lambda e: e.dma_start(out=onesf[:, :], in_=d_ones[0:1, 0:128]),
                   lambda e: e.dma_start(out=gb_row[:, :], in_=d_gb[:, :])):
            emit("sync", fn, dma="ldc")
        LDS_CONST = n["ldc"]

        for fn in (lambda e: e.dma_start(out=onesb[:, :], in_=d_ones[:, :]),
                   lambda e: e.dma_start(
                       out=masks[:].rearrange("p (r j) -> p r j", r=4),
                       in_=d_masks[:, :, :].rearrange("r p j -> p r j")),
                   lambda e: e.dma_start(out=bp[:, :], in_=d_bp[:, :]),
                   lambda e: e.dma_start(out=b2c[:, :], in_=d_b2c[:, :]),
                   lambda e: e.dma_start(out=sel[:, :], in_=d_sel[:, :]),
                   lambda e: e.dma_start(out=ident_bf[:, :], in_=d_ident[:, :]),
                   lambda e: e.dma_start(
                       out=gw[:].rearrange("p (c e) -> p c e", c=8),
                       in_=d_gw[:, :].rearrange("(c p) e -> p c e", p=128))):
            emit("pool", fn, dma="ldgc")
        LDG_CONST = n["ldgc"]

        def emit_layernorm(sums_ps, bc_ps, msq, pre=None):
            pre = dict(pre or {})
            cast_done = []
            for c in range(8):
                w = dict(pre) if c == 0 else {}
                w["ldx"] = 16 * (c // 4 + 1)
                w["ldc"] = LDS_CONST
                w["ldgc"] = LDG_CONST
                cast_done.append(emit("act", lambda e, c=c: e.activation(
                    xln[:, c * T:(c + 1) * T], resid[:, c * T:(c + 1) * T], AF.Copy),
                    waits=w))
            sums_free = [0, 0]
            ev = [0, 0]
            for nt in range(4):
                def f_sum(e, nt=nt):
                    for c in range(8):
                        last = e.matmul(
                            sums_ps[0][:, :], onesb[:, 0:1],
                            xln[:, c * T + nt * 512:c * T + nt * 512 + 512],
                            start=(c == 0), stop=(c == 7))
                    return last
                pe_c = emit("pe", f_sum, waits={"act": cast_done[7],
                                                "dve": sums_free[0],
                                                "ldc": LDS_CONST,
                                                "ldgc": LDG_CONST})
                sums_free[0] = ev[0] = emit("dve", lambda e, nt=nt: e.tensor_copy(
                    srow[:, nt * 512:nt * 512 + 512], sums_ps[0][:, :]),
                    waits={"pe": pe_c})
            prev_mm = [0, 0]
            for nt in range(4):
                for c in range(8):
                    par = c % 2
                    sq_c = emit("act", lambda e, par=par, c=c, nt=nt: e.activation(
                        xsq[par][:, :], xln[:, c * T + nt * 512:c * T + nt * 512 + 512],
                        AF.Square), waits={"act": cast_done[c], "pe": prev_mm[par]})
                    pe_c = emit("pe", lambda e, par=par, c=c: e.matmul(
                        sums_ps[1][:, :], onesb[:, 0:1], xsq[par][:, :],
                        start=(c == 0), stop=(c == 7)),
                        waits={"act": sq_c,
                               "dve": sums_free[1] if c == 0 else 0})
                    prev_mm[par] = pe_c
                sums_free[1] = ev[1] = emit("dve", lambda e, nt=nt: e.tensor_copy(
                    srow[:, T + nt * 512:T + nt * 512 + 512], sums_ps[1][:, :]),
                    waits={"pe": pe_c})

            c0 = emit("dve", lambda e: e.tensor_scalar(
                srow[:, 0:T], srow[:, 0:T], 1.0 / 1024, None, ALU.mult),
                waits={"dve": max(ev)})
            c1 = emit("dve", lambda e: e.tensor_scalar(
                srow[:, T:2 * T], srow[:, T:2 * T], 1.0 / 1024, None, ALU.mult),
                waits={"dve": c0})
            c2 = emit("dve", lambda e: e.tensor_tensor(
                msq[:, :], srow[:, 0:T], srow[:, 0:T], ALU.mult), waits={"dve": c1})
            c3 = emit("dve", lambda e: e.tensor_tensor(
                srow[:, T:2 * T], srow[:, T:2 * T], msq[:, :], ALU.subtract),
                waits={"dve": c2})
            dve_st = emit("dve", lambda e: e.tensor_scalar(
                srow[:, T:2 * T], srow[:, T:2 * T], 1e-5, None, ALU.add),
                waits={"dve": c3})

            a0 = emit("act", lambda e: e.activation(
                srow[:, T:2 * T], srow[:, T:2 * T], AF.Ln), waits={"dve": dve_st})
            act_r = emit("act", lambda e: e.activation(
                srow[:, T:2 * T], srow[:, T:2 * T], AF.Exp, scale=-0.5),
                waits={"act": a0})

            norm_nt = [0, 0, 0, 0]
            bc_free = [0, 0]
            sub_done = [0, 0]
            for nt in range(4):
                bpar = nt % 2
                def f_bc(e, nt=nt, bpar=bpar):
                    e.matmul(bc_ps[2 * bpar][:, :], onesf[:, :],
                             srow[0:1, nt * 512:nt * 512 + 512],
                             start=True, stop=True)
                    return e.matmul(bc_ps[2 * bpar + 1][:, :], onesf[:, :],
                                    srow[0:1, T + nt * 512:T + nt * 512 + 512],
                                    start=True, stop=True)
                pe_c = emit("pe", f_bc, waits={"act": act_r, "dve": bc_free[bpar]})
                for c in range(8):
                    tpar = c % 2
                    s0 = emit("dve", lambda e, c=c, nt=nt, tpar=tpar, bpar=bpar:
                              e.tensor_tensor(
                                  tmpd[tpar][:, :],
                                  resid[:, c * T + nt * 512:c * T + nt * 512 + 512],
                                  bc_ps[2 * bpar][:, :], ALU.subtract),
                              waits={"pe": pe_c if c == 0 else 0,
                                     "dve": sub_done[tpar]})
                    sub_done[tpar] = emit("dve", lambda e, c=c, nt=nt, tpar=tpar, bpar=bpar:
                                          e.tensor_tensor(
                                              xln[:, c * T + nt * 512:c * T + nt * 512 + 512],
                                              tmpd[tpar][:, :], bc_ps[2 * bpar + 1][:, :],
                                              ALU.mult), waits={"dve": s0})
                bc_free[bpar] = norm_nt[nt] = max(sub_done)
            return norm_nt, (ev[0], ev[1])

        sums_ps = [pb[0][0:1, :], pb[1][0:1, :]]
        bc_ps = [pb[2], pb[3], pb[4], pb[5]]
        ln1_nt, ln1_sums_last = emit_layernorm(sums_ps, bc_ps, msq)
        ln1_done = max(ln1_nt)

        if phases >= 2:
            aes = contextlib.ExitStack()
            asb = lambda name, shp, dt: aes.enter_context(nc.sbuf_tensor("sb_" + name, shp, dt))
            wq_s = [asb(f"wq_s{i}", [128, 1024], BF16) for i in range(2)]
            wk_s = [asb(f"wk_s{i}", [128, 1024], BF16) for i in range(2)]
            wv_s = [asb(f"wv_s{i}", [128, 1024], BF16) for i in range(2)]
            wp0 = [asb(f"wp0_{i}", [64, 1024], BF16) for i in range(2)]
            wp1 = [asb(f"wp1_{i}", [64, 1024], BF16) for i in range(2)]
            Qp = asb("Qp", [128, T], BF16)
            Kp = asb("Kp", [128, T], BF16)
            Vp = asb("Vp", [128, 16 * 130], BF16)
            Vf = asb("Vf", [128, T], BF16)
            NP = 6
            Ppool = asb("Ppool", [128, NP * 512], BF16)
            ctx0 = asb("ctx0", [64, T], BF16)
            ctx1 = asb("ctx1", [64, T], BF16)
            lrow = asb("lrow", [1, 512], F32)
            rrow = asb("rrow", [1, 512], BF16)
            rbb = asb("rbb", [64, 512], BF16)
            sc_ps = [pb[2], pb[3], pb[4]]
            qk_ps = [pb[0], pb[1]]
            ctx_ps = [pb[5], pb[6]]
            tr2_ps = [pb7]

            def f_vones(e):
                V4 = Vp[:].rearrange("p (kt h d) -> p kt h d", kt=16, h=2)
                e.memset(V4[:, :, 0:1, 64:65], 1.0)
                return e.memset(V4[:, :, 1:2, 64:65], 1.0)
            vones = emit("dve", f_vones)

            wbuf_free = [0, 0]
            wload = [0] * 8

            def emit_pair_load(hp):
                par = hp % 2
                sem = "wpA" if par == 0 else "wpB"
                emit("pool", lambda e, hp=hp, par=par: e.dma_start(
                    out=wq_s[par][:].rearrange("p (c m) -> p c m", c=8),
                    in_=d_wq[:, hp * 128:(hp + 1) * 128].rearrange("(c p) m -> p c m", p=128)),
                    waits={"pe": wbuf_free[par]}, dma=sem)
                emit("pool", lambda e, hp=hp, par=par: e.dma_start(
                    out=wk_s[par][:].rearrange("p (c m) -> p c m", c=8),
                    in_=d_wk[:, hp * 128:(hp + 1) * 128].rearrange("(c p) m -> p c m", p=128)), dma=sem)
                emit("pool", lambda e, hp=hp, par=par: e.dma_start(
                    out=wv_s[par][:].rearrange("p (c m) -> p c m", c=8),
                    in_=d_wv[:, hp * 128:(hp + 1) * 128].rearrange("(c p) m -> p c m", p=128)), dma=sem)
                emit("pool", lambda e, hp=hp, par=par: e.dma_start(
                    out=wp0[par][:, :], in_=d_wp[hp * 128:hp * 128 + 64, :]), dma=sem)
                wload[hp] = emit("pool", lambda e, hp=hp, par=par: e.dma_start(
                    out=wp1[par][:, :], in_=d_wp[hp * 128 + 64:(hp + 1) * 128, :]), dma=sem)

            emit_pair_load(0)
            emit_pair_load(1)

            qk_free = list(ln1_sums_last)
            pending_norm = [None]
            rb_pe = [0]
            pslot_free = [0] * 6
            pslot_rr = [0]
            sc_free = [0, 0, 0]
            ctx_free = [ln1_nt[3], 0]
            it = 0
            for hp in range(npair):
                par = hp % 2
                for which, (wslc, dst) in enumerate([(wq_s, Qp), (wk_s, Kp)]):
                    for ntl in range(4):
                        qpar = (which * 4 + ntl) % 2

                        def f(e, wslc=wslc, ntl=ntl, par=par, qpar=qpar):
                            for k in range(8):
                                last = e.matmul(
                                    qk_ps[qpar][:, :],
                                    wslc[par][:, k * 128:(k + 1) * 128],
                                    xln[:, k * T + ntl * 512:k * T + ntl * 512 + 512],
                                    start=(k == 0), stop=(k == 7))
                            return last
                        pe_c = emit("pe", f, waits={"dve": max(qk_free[qpar], ln1_nt[ntl]),
                                                    ("wpA" if par == 0 else "wpB"): wload[hp]})
                        qk_free[qpar] = emit("dve", lambda e, dst=dst, ntl=ntl, qpar=qpar:
                                             e.tensor_copy(dst[:, ntl * 512:ntl * 512 + 512],
                                                           qk_ps[qpar][:, :]),
                                             waits={"pe": pe_c})
                vf_done = [0, 0, 0, 0]
                for ntl in range(4):
                    qpar = ntl % 2

                    def f(e, ntl=ntl, par=par, qpar=qpar):
                        for k in range(8):
                            last = e.matmul(
                                qk_ps[qpar][:, :],
                                wv_s[par][:, k * 128:(k + 1) * 128],
                                xln[:, k * T + ntl * 512:k * T + ntl * 512 + 512],
                                start=(k == 0), stop=(k == 7))
                        return last
                    pe_c = emit("pe", f, waits={"dve": max(qk_free[qpar], ln1_nt[ntl])})
                    qk_free[qpar] = vf_done[ntl] = emit(
                        "dve", lambda e, ntl=ntl, qpar=qpar: e.tensor_copy(
                            Vf[:, ntl * 512:ntl * 512 + 512], qk_ps[qpar][:, :]),
                        waits={"pe": pe_c})
                tr_done = 0
                for kt in range(16):
                    pe_c = emit("pe", lambda e, kt=kt: e.transpose(
                        tr2_ps[0][:, 0:128], Vf[:, kt * 128:(kt + 1) * 128], ident_bf[:, :]),
                        waits={"dve": max(vf_done[kt // 4], tr_done),
                               "ldc": LDS_CONST})
                    tr_done = emit("dve", lambda e, kt=kt: e.tensor_copy(
                        Vp[:, kt * 130:(kt + 1) * 130].rearrange("p (h d) -> p h d", h=2)[:, :, 0:64],
                        tr2_ps[0][:, 0:128].rearrange("p (h d) -> p h d", h=2)),
                        waits={"pe": pe_c,
                               "dve": vones if (hp == 0 and kt == 0) else 0})
                vp_done = max(max(qk_free), tr_done)

                def flush_norm():
                    if pending_norm[0] is None:
                        return
                    hh_, b_, cpar_, dve_c_ = pending_norm[0]
                    pending_norm[0] = None
                    pe_c = emit("pe", lambda e: e.matmul(
                        qk_ps[0][0:64, :], onesb[0:1, 0:64], rrow[:, :], start=True, stop=True),
                        waits={"dve": max(dve_c_, qk_free[0])})
                    rb_pe[0] = pe_c
                    e0 = emit("dve", lambda e: e.tensor_copy(
                        rbb[:, :], qk_ps[0][0:64, :]), waits={"pe": pe_c})
                    qk_free[0] = e0
                    def fevac(e, hh_=hh_, b_=b_, cpar_=cpar_):
                        ctx = ctx0 if hh_ == 0 else ctx1
                        return e.tensor_tensor(ctx[:, b_ * 512:(b_ + 1) * 512],
                                               ctx_ps[cpar_][0:64, :], rbb[:, :], ALU.mult)
                    ctx_free[cpar_] = emit("dve", fevac, waits={"dve": e0})

                for hh in range(2):
                    for b in range(NB):
                        nkt = 4 * (b + 1)
                        cpar = it % 2
                        LKA = 3
                        pv_of = []

                        def emit_pv(kt):
                            slot = slot_of[kt]

                            def fpv(e, hh=hh, kt=kt, nkt=nkt, cpar=cpar, slot=slot):
                                return e.matmul(
                                    ctx_ps[cpar][0:65, :],
                                    Vp[:, kt * 130 + hh * 65:kt * 130 + hh * 65 + 65],
                                    Ppool[:, slot * 512:(slot + 1) * 512],
                                    start=(kt == 0), stop=(kt == nkt - 1))
                            w = {}
                            if kt in mask_at:
                                w["dve"] = mask_at[kt]
                            else:
                                w["act"] = exp_at[kt]
                            if kt == 0:
                                w["dve"] = max(w.get("dve", 0), ctx_free[cpar])
                            c = emit("pe", fpv, waits=w)
                            pslot_free[slot] = c
                            return c

                        exp_at = []
                        mask_at = {}
                        slot_of = {}
                        pv_q = []
                        pv_c = 0
                        for kt in range(nkt):
                            spar = (it * 16 + kt) % 3
                            pe_c = emit("pe", lambda e, hh=hh, b=b, kt=kt, spar=spar: e.matmul(
                                sc_ps[spar][:, :],
                                Kp[hh * 64:(hh + 1) * 64, kt * 128:(kt + 1) * 128],
                                Qp[hh * 64:(hh + 1) * 64, b * 512:(b + 1) * 512],
                                start=True, stop=True),
                                waits={"act": sc_free[spar],
                                       "dve": vp_done if (kt == 0 and hh == 0 and b == 0) else 0})
                            slot = pslot_rr[0] % NP
                            pslot_rr[0] += 1
                            slot_of[kt] = slot
                            col = (2 * hp + hh) * 64 + b * 16 + kt
                            act_c = emit("act", lambda e, kt=kt, spar=spar, col=col, slot=slot:
                                         e.activation(Ppool[:, slot * 512:(slot + 1) * 512],
                                                      sc_ps[spar][:, :], AF.Exp,
                                                      bias=alibi[:, col:col + 1], scale=0.125),
                                         waits={"pe": max(pe_c, pslot_free[slot])})
                            sc_free[spar] = act_c
                            exp_at.append(act_c)
                            r = kt - 4 * b
                            if r >= 0:
                                mask_at[kt] = emit("dve", lambda e, r=r, slot=slot:
                                                   e.tensor_tensor(
                                                       Ppool[:, slot * 512:(slot + 1) * 512],
                                                       Ppool[:, slot * 512:(slot + 1) * 512],
                                                       masks[:, r * 512:(r + 1) * 512], ALU.mult),
                                                   waits={"act": act_c})
                            pv_q.append(kt)
                            if kt >= LKA:
                                pv_c = emit_pv(pv_q.pop(0))
                        while pv_q:
                            pv_c = emit_pv(pv_q.pop(0))

                        flush_norm()
                        l0 = emit("dve", lambda e, cpar=cpar: e.reciprocal(
                            lrow[:, :], ctx_ps[cpar][64:65, :]),
                            waits={"pe": max(pv_c, rb_pe[0])})
                        dve_c = emit("dve", lambda e: e.tensor_copy(
                            rrow[:, :], lrow[:, :]), waits={"dve": l0})
                        pending_norm[0] = (hh, b, cpar, dve_c)
                        it += 1
                flush_norm()
                ctx_all = max(ctx_free)

                for dout in range(8):
                    for tt in range(4):
                        qpar = (dout * 4 + tt) % 2

                        def fp(e, hp=hp, dout=dout, tt=tt, qpar=qpar, par=par):
                            first = True
                            if hp == 0:
                                e.matmul(qk_ps[qpar][:, :],
                                         bp[:, dout * 128:(dout + 1) * 128],
                                         onesb[0:1, :], start=True, stop=False)
                                first = False
                            e.matmul(qk_ps[qpar][:, :],
                                     wp0[par][:, dout * 128:(dout + 1) * 128],
                                     ctx0[:, tt * 512:tt * 512 + 512],
                                     start=first, stop=False)
                            return e.matmul(qk_ps[qpar][:, :],
                                            wp1[par][:, dout * 128:(dout + 1) * 128],
                                            ctx1[:, tt * 512:tt * 512 + 512],
                                            start=False, stop=True)
                        pe_c = emit("pe", fp, waits={"dve": max(ctx_all, qk_free[qpar])})

                        qk_free[qpar] = emit("dve", lambda e, dout=dout, tt=tt, qpar=qpar:
                                             e.tensor_tensor(
                                                 resid[:, dout * T + tt * 512:dout * T + tt * 512 + 512],
                                                 resid[:, dout * T + tt * 512:dout * T + tt * 512 + 512],
                                                 qk_ps[qpar][:, :],
                                                 ALU.add),
                                             waits={"pe": pe_c})
                wbuf_free[par] = n["pe"]
                if hp + 2 < npair:
                    emit_pair_load(hp + 2)
            attn_done = max(qk_free)
            if debug:
                for nm, buf in [("xln", xln), ("Qp", Qp), ("Kp", Kp), ("Vp", Vp),
                                ("ctx0", ctx0), ("ctx1", ctx1), ("Pp", Ppool),
                                ("lr", lrow), ("rb", rbb),
                                ("wqs", wq_s[0]), ("wks", wk_s[0]), ("wvs", wv_s[0])]:
                    emit("pool", lambda e, nm=nm, buf=buf: e.dma_start(
                        out=d_dbg[nm][:, :], in_=buf[:, :]),
                        waits={"dve": n["dve"], "pe": n["pe"], "act": n["act"]}, dma="st")
            aes.close()
        else:
            attn_done = ln1_done


        if phases >= 3:
            sums_ps = [pb[0][0:1, :], pb[1][0:1, :]]
            bc_ps = [pb[2], pb[3], pb[4], pb[5]]
            ln2_nt, ln2_sums_last = emit_layernorm(sums_ps, bc_ps, msq, pre={"dve": attn_done})
            ln2_done = max(ln2_nt)
        else:
            ln2_done = attn_done


        if phases >= 4:
            fes = contextlib.ExitStack()
            fsb = lambda name, shp, dt: fes.enter_context(nc.sbuf_tensor("sb_" + name, shp, dt))
            w1g = fsb("w1g", [128, G * 1024], BF16)
            w2g = fsb("w2g", [128, G * 1024], BF16)
            hsb2 = fsb("hsb2", [128, 16 * 512], BF16)
            comb_fmb = fsb("comb_fmb", [16, T], BF16)
            comb_tm = fsb("comb_tm", [128, 16 * 9], BF16)
            comb_b = [fsb(f"comb_b{i}", [128, T], BF16) for i in range(2)]
            lg = fsb("lg", [128, 8], F32)
            g_m1 = fsb("g_m1", [128, 1], F32)
            g_m2 = fsb("g_m2", [128, 1], F32)
            g_eq1 = fsb("g_eq1", [128, 8], F32)
            g_eq2 = fsb("g_eq2", [128, 8], F32)
            g_msk = fsb("g_msk", [128, 8], F32)
            g_d = fsb("g_d", [128, 1], F32)
            g_e = fsb("g_e", [128, 1], F32)
            g_r = fsb("g_r", [128, 1], F32)
            g_gw2 = fsb("g_gw2", [128, 1], F32)
            g_c1 = fsb("g_c1", [128, 8], F32)
            g_c2 = fsb("g_c2", [128, 8], F32)
            g_ps = pb[6]
            tr_ps = pb7
            o_ps = [pb[2], pb[3], pb[4], pb[5]]
            h_ps = [pb[0], pb[1]]

            ctm_ones = emit("dve", lambda e: e.memset(
                comb_tm[:].rearrange("p (t e) -> p t e", e=9)[:, :, 8:9], 1.0),
                waits={"dve": ln2_done})
            pe_c = emit("pe", lambda e: e.matmul(g_ps[:, 0:8], onesf[:, :], gb_row[:, :],
                                                 start=True, stop=True),
                        waits={"ldc": LDS_CONST, "dve": ln2_done})
            gbb_done = emit("dve", lambda e: e.tensor_copy(gb_b[:, :], g_ps[:, 0:8]),
                            waits={"pe": pe_c})

            gate_st = {"g_ps_free": gbb_done, "tr_free": 0, "tr_evac": 0,
                       "comb_cast": 0, "steps": list(range(16))}

            def gate_tile(ttk):
                def f_lg(e, ttk=ttk):
                    for k in range(8):
                        last = e.matmul(g_ps[:, 0:8],
                                        xln[:, k * T + ttk * 128:k * T + (ttk + 1) * 128],
                                        gw[:, k * 8:(k + 1) * 8],
                                        start=(k == 0), stop=(k == 7))
                    return last
                pe_c = emit("pe", f_lg, waits={"dve": max(gate_st["g_ps_free"], ln2_nt[ttk // 4])})
                q0 = emit("dve", lambda e: e.tensor_tensor(
                    lg[:, :], g_ps[:, 0:8], gb_b[:, :], ALU.add), waits={"pe": pe_c})
                q1 = emit("dve", lambda e: e.tensor_reduce(
                    g_m1[:, :], lg[:, :], AX.X, ALU.max), waits={"dve": q0})
                q2 = emit("dve", lambda e: e.tensor_scalar(
                    g_eq1[:, :], lg[:, :], g_m1[:, 0:1], None, ALU.is_equal), waits={"dve": q1})
                q3 = emit("dve", lambda e: e.scalar_tensor_tensor(
                    g_msk[:, :], g_eq1[:, :], -1e30, lg[:, :], ALU.mult, ALU.add),
                    waits={"dve": q2})
                q4 = emit("dve", lambda e: e.tensor_reduce(
                    g_m2[:, :], g_msk[:, :], AX.X, ALU.max), waits={"dve": q3})
                q5 = emit("dve", lambda e: e.tensor_scalar(
                    g_eq2[:, :], g_msk[:, :], g_m2[:, 0:1], None, ALU.is_equal), waits={"dve": q4})
                d1 = emit("dve", lambda e: e.tensor_tensor(
                    g_d[:, :], g_m2[:, :], g_m1[:, :], ALU.subtract), waits={"dve": q5})
                a1 = emit("act", lambda e: e.activation(g_e[:, :], g_d[:, :], AF.Exp),
                          waits={"dve": d1})
                p0 = emit("dve", lambda e: e.tensor_scalar(
                    g_gw2[:, :], g_e[:, :], 1.0, None, ALU.add), waits={"act": a1})
                p1 = emit("dve", lambda e: e.reciprocal(
                    g_r[:, :], g_gw2[:, :]), waits={"dve": p0})
                p2 = emit("dve", lambda e: e.tensor_tensor(
                    g_gw2[:, :], g_e[:, :], g_r[:, :], ALU.mult), waits={"dve": p1})
                p3 = emit("dve", lambda e: e.tensor_scalar(
                    g_c1[:, :], g_eq1[:, :], g_r[:, 0:1], None, ALU.mult), waits={"dve": p2})
                p4 = emit("dve", lambda e: e.tensor_scalar(
                    g_c2[:, :], g_eq2[:, :], g_gw2[:, 0:1], None, ALU.mult), waits={"dve": p3})
                g_ps_free = emit("dve", lambda e, ttk=ttk: e.tensor_tensor(
                    comb_tm[:, ttk * 9:ttk * 9 + 8], g_c1[:, :], g_c2[:, :], ALU.add),
                    waits={"dve": p4})
                gate_st["g_ps_free"] = g_ps_free
                pe_c = emit("pe", lambda e, ttk=ttk: e.transpose(
                    tr_ps[0:9, 0:128], comb_tm[:, ttk * 9:ttk * 9 + 9], ident_bf[:, :]),
                    waits={"dve": max(g_ps_free, gate_st["tr_free"], ctm_ones),
                           "ldc": LDS_CONST})
                tr_evac = emit("dve", lambda e, ttk=ttk: e.tensor_copy(
                    comb_fmb[0:9, ttk * 128:(ttk + 1) * 128], tr_ps[0:9, 0:128]),
                    waits={"pe": pe_c})
                gate_st["tr_free"] = tr_evac
                gate_st["tr_evac"] = tr_evac
                gate_st["comb_cast"] = tr_evac

            gate_tick = [0]

            def maybe_gate():
                if gate_st["steps"]:
                    gate_tile(gate_st["steps"].pop(0))

            def drain_gate():
                while gate_st["steps"]:
                    maybe_gate()

            ffn_last = 0
            if phases >= 5:
                w1_ld = [0] * NEG
                w2_ld = [0] * NEG
                A_done = [0] * NEG
                C_done = [0] * NEG
                h_free = [0, 0]
                o_free = [ln2_nt[2], ln2_nt[2], ln2_nt[3], ln2_nt[3]]
                cb_done = [0, 0]
                cb_used = [0, 0]
                ffn_last = 0

                def emit_w_load(i):
                    o, g = divmod(i, NG)
                    e_idx = EXPERT_ORDER[o]
                    if e_idx == 8:
                        src1 = d_ws1[:, g * 1024:(g + 1) * 1024]
                        src2 = d_ws2[g * 1024:(g + 1) * 1024, :]
                    else:
                        src1 = d_w1[e_idx, :, g * 1024:(g + 1) * 1024]
                        src2 = d_w2[e_idx, g * 1024:(g + 1) * 1024, :]
                    w1_ld[i] = emit("pool", lambda e, src1=src1: e.dma_start(
                        out=w1g[:].rearrange("p (c m) -> p c m", c=8),
                        in_=src1.rearrange("(c p) m -> p c m", p=128)),
                        waits={"pe": A_done[i - 1]} if i else {"dve": attn_done}, dma="w1s")
                    w2_ld[i] = emit("pool", lambda e, src2=src2: e.dma_start(
                        out=w2g[:].rearrange("p (m d) -> p m d", m=8),
                        in_=src2.rearrange("(m p) d -> p m d", p=128)),
                        waits={"pe": C_done[i - 1]} if i else {"dve": attn_done}, dma="w2s")

                def emit_comb_b(e_idx):
                    par = e_idx % 2
                    evac = 0
                    for ntl in range(4):
                        bank = ntl % 2
                        pe_c = emit("pe", lambda e, e_idx=e_idx, ntl=ntl, bank=bank: e.matmul(
                            o_ps[bank][:, :], sel[0:9, e_idx * 128:(e_idx + 1) * 128],
                            comb_fmb[0:9, ntl * 512:(ntl + 1) * 512],
                            start=True, stop=True),
                            waits={"dve": max(gate_st["comb_cast"], o_free[bank],
                                              cb_used[par] if ntl == 0 else 0)})
                        evac = emit("dve", lambda e, par=par, ntl=ntl, bank=bank: e.tensor_copy(
                            comb_b[par][:, ntl * 512:(ntl + 1) * 512], o_ps[bank][:, :]),
                            waits={"pe": pe_c})
                        o_free[bank] = evac
                    cb_done[par] = evac

                emit_w_load(0)

                prev_B_pe = 0
                for o, e_idx in enumerate(EXPERT_ORDER):
                    if o + 1 < len(EXPERT_ORDER):
                        emit_comb_b(EXPERT_ORDER[o + 1])
                    par_e = e_idx % 2
                    for g in range(NG):
                        i = o * NG + g
                        if i >= ffn_groups:
                            continue
                        last_A = 0
                        last_B = 0
                        for half in range(2):
                            h_ready = []
                            for tl in range(2):
                                t = 2 * half + tl
                                for m in range(G):
                                    hpar = (t * G + m) % 2
                                    slot = tl * G + m

                                    def fA(e, t=t, m=m, hpar=hpar):
                                        for k in range(8):
                                            last = e.matmul(
                                                h_ps[hpar][:, :],
                                                w1g[:, k * 1024 + m * 128:k * 1024 + (m + 1) * 128],
                                                xln[:, k * T + t * 512:k * T + t * 512 + 512],
                                                start=(k == 0), stop=(k == 7))
                                        return last
                                    w = {"act": h_free[hpar]}
                                    if half == 0 and tl == 0 and m == 0:
                                        w["w1s"] = w1_ld[i]
                                        if i == 0:
                                            w["dve"] = ln2_done
                                    last_A = pe_c = emit("pe", fA, waits=w)
                                    if o == 0:
                                        gate_tick[0] += 1
                                        if gate_tick[0] % 6 == 0:
                                            maybe_gate()

                                    if ffn_parts < 2:
                                        h_ready.append(("act", 0))
                                        continue
                                    bcol = e_idx * 32 + g * 8 + m
                                    wg = {"pe": max(pe_c,
                                                    prev_B_pe if (tl == 0 and m == 0) else 0)}
                                    act_c = emit("act", lambda e, slot=slot, hpar=hpar, bcol=bcol:
                                                 e.activation(hsb2[:, slot * 512:(slot + 1) * 512],
                                                              h_ps[hpar][:, :], AF.Gelu,
                                                              bias=b1t[:, bcol:bcol + 1]),
                                                 waits=wg)
                                    h_free[hpar] = act_c
                                    if e_idx != 8:
                                        dve_c = emit("dve", lambda e, slot=slot, t=t, par_e=par_e:
                                                     e.tensor_tensor(
                                                         hsb2[:, slot * 512:(slot + 1) * 512],
                                                         hsb2[:, slot * 512:(slot + 1) * 512],
                                                         comb_b[par_e][:, t * 512:(t + 1) * 512],
                                                         ALU.mult),
                                                     waits={"act": act_c,
                                                            "dve": cb_done[par_e] if (half == 0 and tl == 0 and m == 0) else 0})
                                        cb_used[par_e] = dve_c
                                        h_ready.append(("dve", dve_c))
                                    else:
                                        h_ready.append(("act", act_c))
                            for tl in (range(2) if ffn_parts >= 3 else []):
                                t = 2 * half + tl
                                tslots = h_ready[tl * G:(tl + 1) * G]
                                h_act = max((c for k, c in tslots if k == "act"), default=0)
                                h_dve = max((c for k, c in tslots if k == "dve"), default=0)
                                for d in range(8):
                                    bank = d % 4
                                    inject_b2 = (e_idx == 7 and g == NG - 1)

                                    def fB(e, d=d, t=t, tl=tl, bank=bank, inject_b2=inject_b2):
                                        for m in range(G):
                                            last = e.matmul(
                                                o_ps[bank][:, :],
                                                w2g[:, m * 1024 + d * 128:m * 1024 + (d + 1) * 128],
                                                hsb2[:, (tl * G + m) * 512:(tl * G + m + 1) * 512],
                                                start=(m == 0), stop=(m == G - 1 and not inject_b2))
                                        if inject_b2:
                                            last = e.matmul(o_ps[bank][:, :],
                                                            b2c[0:9, d * 128:(d + 1) * 128],
                                                            comb_fmb[0:9, t * 512:(t + 1) * 512],
                                                            start=False, stop=True)
                                        return last
                                    w = {"act": h_act if d == 0 else 0,
                                         "dve": max(h_dve if d == 0 else 0, o_free[bank])}
                                    if half == 0 and tl == 0 and d == 0:
                                        w["w2s"] = w2_ld[i]
                                    last_B = pe_c = emit("pe", fB, waits=w)

                                    if ffn_parts >= 4:
                                        if ffn_parts == 5:
                                            fn_ev = (lambda e, d=d, t=t, bank=bank:
                                                     e.tensor_copy(tmp1[:, :], o_ps[bank][:, :]))
                                        elif ffn_parts == 6:
                                            fn_ev = (lambda e, d=d, t=t, bank=bank:
                                                     e.tensor_tensor(tmp1[:, :], o_ps[bank][:, :],
                                                                     resid[:, d * T + t * 512:d * T + t * 512 + 512],
                                                                     ALU.add))
                                        else:
                                            fn_ev = (lambda e, d=d, t=t, bank=bank:
                                                     e.tensor_tensor(resid[:, d * T + t * 512:d * T + t * 512 + 512],
                                                                     resid[:, d * T + t * 512:d * T + t * 512 + 512],
                                                                     o_ps[bank][:, :],
                                                                     ALU.add))
                                        o_free[bank] = ffn_last = emit(
                                            "dve", fn_ev, waits={"pe": pe_c})
                            prev_B_pe = last_B
                        A_done[i] = last_A
                        C_done[i] = last_B
                        if i + 1 < min(NEG, ffn_groups):
                            emit_w_load(i + 1)
                        if g == NG - 1:
                            drain_gate()
                            if o + 1 < len(EXPERT_ORDER):
                                emit_comb_b(EXPERT_ORDER[o + 1])
            fes.close()

        if debug and phases >= 4:
            for nm, buf in [("cfmb", comb_fmb), ("gbb", gb_b), ("ctm", comb_tm),
                            ("g_lg", lg), ("g_m1", g_m1), ("g_m2", g_m2), ("g_msk", g_msk),
                            ("g_eq1", g_eq1), ("g_eq2", g_eq2), ("g_d", g_d), ("g_e", g_e),
                            ("g_r", g_r), ("g_gw2", g_gw2), ("g_c1", g_c1), ("g_c2", g_c2)]:
                emit("pool", lambda e, nm=nm, buf=buf: e.dma_start(
                    out=d_dbg[nm][:, :], in_=buf[:, :]),
                    waits={"dve": n["dve"], "pe": n["pe"], "act": n["act"]}, dma="st")
        emit("sync", lambda e: e.dma_start(
            out=d_out[:, :].rearrange("(c p) t -> p c t", p=128),
            in_=resid[:].rearrange("p (c t) -> p c t", c=8)),
            waits={"dve": n["dve"], "act": n["act"], "pe": n["pe"]}, dma="st")
        emit("sync", lambda e: None, waits={"st": n["st"]}, inc=False)

        _check_schedule(ops, n)

        if True:
            _dispatch(nc, ops, s)
    return nc


def _check_schedule(ops, totals):
    sems = {k: 0 for k in totals}
    ptr = {k: 0 for k in ops}
    for k, v in totals.items():
        pass
    progressed = True
    while progressed:
        progressed = False
        for eng, lst in ops.items():
            while ptr[eng] < len(lst):
                w, fn, inc_sem, amt = lst[ptr[eng]]
                if all(sems[sk] >= sv for sk, sv in w.items()):
                    if inc_sem:
                        sems[inc_sem] += amt
                    ptr[eng] += 1
                    progressed = True
                else:
                    break
    stuck = {e: (ptr[e], len(l)) for e, l in ops.items() if ptr[e] < len(l)}
    if stuck:
        det = {}
        for eng, (p, ln) in stuck.items():
            det[eng] = (p, ln, ops[eng][p][0], dict(sems))
        raise RuntimeError(f"schedule deadlock: {det}")


def _dispatch(nc, ops, s):
    with nc.Block() as block:
        def body(eng_name):
            def run(engine):
                for w, fn, inc_sem, amt in ops[eng_name]:
                    for sk, sv in w.items():
                        engine.wait_ge(s[sk], sv)
                    last = fn(engine)
                    if inc_sem is not None and last is not None:
                        last.then_inc(s[inc_sem], amt)
            return run
        block.tensor(body("pe"))
        block.scalar(body("act"))
        block.vector(body("dve"))
        block.gpsimd(body("pool"))
        block.sync(body("sync"))


_NC_CACHE = []


def kernel(**inputs):
    x = np.ascontiguousarray(np.asarray(inputs["L0_attn_x"], dtype=np.float32))
    f32 = lambda k: np.ascontiguousarray(np.asarray(inputs[k], dtype=np.float32))
    wq, wk, wv, wp = f32("L0_attn_W_query"), f32("L0_attn_W_key"), f32("L0_attn_W_value"), f32("L0_attn_W_proj")
    bpv = f32("L0_attn_b_proj").reshape(1, D)
    gwv = f32("L0_ffn_gate_w")
    gbv = f32("L0_ffn_gate_b").reshape(1, 8)
    w1 = f32("L0_ffn_w1")
    b1 = f32("L0_ffn_b1")
    w2 = f32("L0_ffn_w2")
    b2 = f32("L0_ffn_b2")
    ws1 = f32("L0_ffn_ws1")
    bs1 = f32("L0_ffn_bs1")
    ws2 = f32("L0_ffn_ws2")
    bs2 = f32("L0_ffn_bs2")

    b1t = np.ascontiguousarray(
        np.concatenate([b1.reshape(8, 32, 128), bs1.reshape(1, 32, 128)], axis=0)
        .transpose(2, 0, 1).reshape(128, 288))
    b2c = np.ascontiguousarray(np.concatenate([b2, bs2.reshape(1, D)], axis=0))
    alibi, masksv, ones, identv, selv = _host_tables()

    if not _NC_CACHE:
        _NC_CACHE.append(build_program(phases=5))
    nc = _NC_CACHE[0]

    base = {"wq": wq, "wk": wk, "wv": wv, "wp": wp, "bp": bpv,
            "gw": gwv, "gb": gbv, "w1": w1, "w2": w2, "ws1": ws1, "ws2": ws2,
            "b1t": b1t, "b2c": b2c, "alibi": alibi, "masksd": masksv,
            "onesd": ones, "identd": identv, "seld": selv}
    in_maps = [dict(base, x=np.ascontiguousarray(x[c].T)) for c in range(8)]
    res = run_bass_kernel_spmd(nc, in_maps, core_ids=list(range(8)))
    out = np.stack([np.asarray(r["out"]).T for r in res.results], axis=0)
    return np.ascontiguousarray(out.astype(np.float32))


if __name__ == "__main__":
    import time
    t0 = time.time()
    nc = build_program()
    print(f"build ok {time.time()-t0:.1f}s n_inst={nc.n_instructions()}")


# version 10
# speedup vs baseline: 1.0638x; 1.0638x over previous
import contextlib

import numpy as np
import ml_dtypes

import concourse.bass as bass
import concourse.mybir as mybir
from concourse.bass_utils import run_bass_kernel_spmd

F32 = mybir.dt.float32
BF16 = mybir.dt.bfloat16
AF = mybir.ActivationFunctionType
ALU = mybir.AluOpType
AX = mybir.AxisListType

T = 2048
D = 1024
H = 16
NB = 4
G = 8
NG = 32 // G
EXPERT_ORDER = [8, 0, 1, 2, 3, 4, 5, 6, 7]
NEG = len(EXPERT_ORDER) * NG


def _host_tables():
    slopes = 2.0 ** (-8.0 * np.arange(1, H + 1) / H)
    alibi = np.zeros((128, 1024), np.float32)
    for h in range(H):
        for b in range(NB):
            for kt in range(4 * (b + 1)):
                k = 128 * kt + np.arange(128)
                alibi[:, h * 64 + b * 16 + kt] = slopes[h] * (k - (512 * (b + 1) - 1)) / 8.0
    masks = np.zeros((4, 128, 512), np.float32)
    p = np.arange(128)[:, None]
    j = np.arange(512)[None, :]
    for r in range(4):
        masks[r] = (j >= p + 128 * r).astype(np.float32)
    ones = np.ones((128, 512), np.float32)
    ident = np.eye(128, dtype=np.float32)
    sel = np.zeros((9, 8 * 128), np.float32)
    for e in range(8):
        sel[e, e * 128:(e + 1) * 128] = 1.0
    return alibi, masks, ones, ident, sel


def build_program(phases=5, npair=8, debug=False, ffn_groups=NEG, ffn_parts=4):
    nc = bass.Bass("TRN2", target_bir_lowering=False)

    d_x = nc.dram_tensor("x", [D, T], F32, kind="ExternalInput")
    d_wq = nc.dram_tensor("wq", [D, D], F32, kind="ExternalInput")
    d_wk = nc.dram_tensor("wk", [D, D], F32, kind="ExternalInput")
    d_wv = nc.dram_tensor("wv", [D, D], F32, kind="ExternalInput")
    d_wp = nc.dram_tensor("wp", [D, D], F32, kind="ExternalInput")
    d_bp = nc.dram_tensor("bp", [1, D], F32, kind="ExternalInput")
    d_gw = nc.dram_tensor("gw", [D, 8], F32, kind="ExternalInput")
    d_gb = nc.dram_tensor("gb", [1, 8], F32, kind="ExternalInput")
    d_w1 = nc.dram_tensor("w1", [8, D, 4096], F32, kind="ExternalInput")
    d_w2 = nc.dram_tensor("w2", [8, 4096, D], F32, kind="ExternalInput")
    d_ws1 = nc.dram_tensor("ws1", [D, 4096], F32, kind="ExternalInput")
    d_ws2 = nc.dram_tensor("ws2", [4096, D], F32, kind="ExternalInput")
    d_b1t = nc.dram_tensor("b1t", [128, 288], F32, kind="ExternalInput")
    d_b2c = nc.dram_tensor("b2c", [9, D], F32, kind="ExternalInput")
    d_alibi = nc.dram_tensor("alibi", [128, 1024], F32, kind="ExternalInput")
    d_masks = nc.dram_tensor("masksd", [4, 128, 512], F32, kind="ExternalInput")
    d_ones = nc.dram_tensor("onesd", [128, 512], F32, kind="ExternalInput")
    d_ident = nc.dram_tensor("identd", [128, 128], F32, kind="ExternalInput")
    d_sel = nc.dram_tensor("seld", [9, 8 * 128], F32, kind="ExternalInput")
    d_out = nc.dram_tensor("out", [D, T], F32, kind="ExternalOutput")
    if debug:
        d_dbg = {nm: nc.dram_tensor("dbg_" + nm, shp, F32, kind="ExternalOutput")
                 for nm, shp in [("xln", [128, 8 * T]), ("Qp", [128, T]), ("Kp", [128, T]),
                                 ("Vp", [128, 16 * 130]), ("ctx0", [64, T]), ("ctx1", [64, T]),
                                 ("Pp", [128, 8 * 512]), ("lr", [1, 512]), ("rb", [64, 512]),
                                 ("wqs", [128, 1024]), ("wks", [128, 1024]), ("wvs", [128, 1024]),
                                 ("cfmb", [16, T]), ("gbb", [128, 8]), ("ctm", [128, 16 * 9]),
                                 ("g_lg", [128, 8]), ("g_m1", [128, 1]), ("g_m2", [128, 1]),
                                 ("g_msk", [128, 8]), ("g_eq1", [128, 8]), ("g_eq2", [128, 8]),
                                 ("g_d", [128, 1]), ("g_e", [128, 1]), ("g_r", [128, 1]),
                                 ("g_gw2", [128, 1]), ("g_c1", [128, 8]), ("g_c2", [128, 8])]}

    es = contextlib.ExitStack()
    with es:
        sbt = lambda name, shp, dt: es.enter_context(nc.sbuf_tensor("sb_" + name, shp, dt))
        resid = sbt("resid", [128, 8 * T], F32)
        xln = sbt("xln", [128, 8 * T], BF16)
        b1t = sbt("b1t", [128, 288], F32)
        b2c = sbt("b2c", [9, D], BF16)
        bp = sbt("bp", [1, D], BF16)
        gw = sbt("gw", [128, 8 * 8], BF16)
        gb_row = sbt("gb_row", [1, 8], F32)
        gb_b = sbt("gb_b", [128, 8], F32)
        onesf = sbt("onesf", [1, 128], F32)
        onesb = sbt("onesb", [128, 512], BF16)
        alibi = sbt("alibi", [128, 1024], F32)
        masks = sbt("masks", [128, 4 * 512], BF16)
        ident = sbt("ident", [128, 128], F32)
        sel = sbt("sel", [9, 8 * 128], BF16)
        ident_bf = sbt("ident_bf", [128, 128], BF16)
        tmp1 = sbt("tmp1", [128, 512], F32)
        tmp2 = sbt("tmp2", [128, 512], F32)
        tmpd = [tmp1, tmp2]
        xsq = [sbt(f"xsq{i}", [128, 512], BF16) for i in range(2)]
        srow = sbt("srow", [1, 2 * T], F32)
        msq = sbt("msq", [1, T], F32)

        pb = [es.enter_context(nc.psum_tensor(f"pb{i}", [128, 512], F32)) for i in range(7)]
        pb7 = es.enter_context(nc.psum_tensor("pb7", [128, 1024], BF16))
        s = {k: es.enter_context(nc.semaphore("s_" + k))
             for k in ("pe", "act", "dve", "ldx", "ldc", "ldgc",
                       "wpA", "wpB", "w1s", "w2s", "st")}

        ops = {"pe": [], "act": [], "dve": [], "pool": [], "sync": []}
        n = {k: 0 for k in ("pe", "act", "dve", "ldx", "ldc", "ldgc",
                            "wpA", "wpB", "w1s", "w2s", "st")}
        _esem = {"pe": "pe", "act": "act", "dve": "dve"}

        def emit(eng, fn, waits=None, inc=True, dma=None):
            w = {k: v for k, v in (waits or {}).items() if v > 0}
            inc_sem = dma if dma else (_esem.get(eng) if inc else None)
            amt = 16 if dma else 1
            ops[eng].append((w, fn, inc_sem, amt))
            if inc_sem:
                n[inc_sem] += amt
                return n[inc_sem]
            return None

        for ch in range(2):
            emit("sync", lambda e, ch=ch: e.dma_start(
                out=resid[:].rearrange("p (c t) -> p c t", c=8)[:, 4 * ch:4 * ch + 4],
                in_=d_x[:, :].rearrange("(c p) t -> p c t", p=128)[:, 4 * ch:4 * ch + 4]),
                dma="ldx")
        LDS_X = n["ldx"]
        for fn in (lambda e: e.dma_start(out=alibi[:, :], in_=d_alibi[:, :]),
                   lambda e: e.dma_start(out=ident[:, :], in_=d_ident[:, :]),
                   lambda e: e.dma_start(out=b1t[:, :], in_=d_b1t[:, :]),
                   lambda e: e.dma_start(out=onesf[:, :], in_=d_ones[0:1, 0:128]),
                   lambda e: e.dma_start(out=gb_row[:, :], in_=d_gb[:, :])):
            emit("sync", fn, dma="ldc")
        LDS_CONST = n["ldc"]

        for fn in (lambda e: e.dma_start(out=onesb[:, :], in_=d_ones[:, :]),
                   lambda e: e.dma_start(
                       out=masks[:].rearrange("p (r j) -> p r j", r=4),
                       in_=d_masks[:, :, :].rearrange("r p j -> p r j")),
                   lambda e: e.dma_start(out=bp[:, :], in_=d_bp[:, :]),
                   lambda e: e.dma_start(out=b2c[:, :], in_=d_b2c[:, :]),
                   lambda e: e.dma_start(out=sel[:, :], in_=d_sel[:, :]),
                   lambda e: e.dma_start(out=ident_bf[:, :], in_=d_ident[:, :]),
                   lambda e: e.dma_start(
                       out=gw[:].rearrange("p (c e) -> p c e", c=8),
                       in_=d_gw[:, :].rearrange("(c p) e -> p c e", p=128))):
            emit("pool", fn, dma="ldgc")
        LDG_CONST = n["ldgc"]

        def emit_layernorm(sums_ps, bc_ps, msq, pre=None):
            pre = dict(pre or {})
            cast_done = []
            for c in range(8):
                w = dict(pre) if c == 0 else {}
                w["ldx"] = 16 * (c // 4 + 1)
                w["ldc"] = LDS_CONST
                w["ldgc"] = LDG_CONST
                cast_done.append(emit("act", lambda e, c=c: e.activation(
                    xln[:, c * T:(c + 1) * T], resid[:, c * T:(c + 1) * T], AF.Copy),
                    waits=w))
            sums_free = [0, 0]
            ev = [0, 0]
            for nt in range(4):
                def f_sum(e, nt=nt):
                    for c in range(8):
                        last = e.matmul(
                            sums_ps[0][:, :], onesb[:, 0:1],
                            xln[:, c * T + nt * 512:c * T + nt * 512 + 512],
                            start=(c == 0), stop=(c == 7))
                    return last
                pe_c = emit("pe", f_sum, waits={"act": cast_done[7],
                                                "dve": sums_free[0],
                                                "ldc": LDS_CONST,
                                                "ldgc": LDG_CONST})
                sums_free[0] = ev[0] = emit("dve", lambda e, nt=nt: e.tensor_copy(
                    srow[:, nt * 512:nt * 512 + 512], sums_ps[0][:, :]),
                    waits={"pe": pe_c})
            prev_mm = [0, 0]
            for nt in range(4):
                for c in range(8):
                    par = c % 2
                    sq_c = emit("act", lambda e, par=par, c=c, nt=nt: e.activation(
                        xsq[par][:, :], xln[:, c * T + nt * 512:c * T + nt * 512 + 512],
                        AF.Square), waits={"act": cast_done[c], "pe": prev_mm[par]})
                    pe_c = emit("pe", lambda e, par=par, c=c: e.matmul(
                        sums_ps[1][:, :], onesb[:, 0:1], xsq[par][:, :],
                        start=(c == 0), stop=(c == 7)),
                        waits={"act": sq_c,
                               "dve": sums_free[1] if c == 0 else 0})
                    prev_mm[par] = pe_c
                sums_free[1] = ev[1] = emit("dve", lambda e, nt=nt: e.tensor_copy(
                    srow[:, T + nt * 512:T + nt * 512 + 512], sums_ps[1][:, :]),
                    waits={"pe": pe_c})

            c0 = emit("dve", lambda e: e.tensor_scalar(
                srow[:, 0:T], srow[:, 0:T], 1.0 / 1024, None, ALU.mult),
                waits={"dve": max(ev)})
            c1 = emit("dve", lambda e: e.tensor_scalar(
                srow[:, T:2 * T], srow[:, T:2 * T], 1.0 / 1024, None, ALU.mult),
                waits={"dve": c0})
            c2 = emit("dve", lambda e: e.tensor_tensor(
                msq[:, :], srow[:, 0:T], srow[:, 0:T], ALU.mult), waits={"dve": c1})
            c3 = emit("dve", lambda e: e.tensor_tensor(
                srow[:, T:2 * T], srow[:, T:2 * T], msq[:, :], ALU.subtract),
                waits={"dve": c2})
            dve_st = emit("dve", lambda e: e.tensor_scalar(
                srow[:, T:2 * T], srow[:, T:2 * T], 1e-5, None, ALU.add),
                waits={"dve": c3})

            a0 = emit("act", lambda e: e.activation(
                srow[:, T:2 * T], srow[:, T:2 * T], AF.Ln), waits={"dve": dve_st})
            act_r = emit("act", lambda e: e.activation(
                srow[:, T:2 * T], srow[:, T:2 * T], AF.Exp, scale=-0.5),
                waits={"act": a0})

            norm_nt = [0, 0, 0, 0]
            bc_free = [0, 0]
            sub_done = [0, 0]
            for nt in range(4):
                bpar = nt % 2
                def f_bc(e, nt=nt, bpar=bpar):
                    e.matmul(bc_ps[2 * bpar][:, :], onesf[:, :],
                             srow[0:1, nt * 512:nt * 512 + 512],
                             start=True, stop=True)
                    return e.matmul(bc_ps[2 * bpar + 1][:, :], onesf[:, :],
                                    srow[0:1, T + nt * 512:T + nt * 512 + 512],
                                    start=True, stop=True)
                pe_c = emit("pe", f_bc, waits={"act": act_r, "dve": bc_free[bpar]})
                for c in range(8):
                    tpar = c % 2
                    s0 = emit("dve", lambda e, c=c, nt=nt, tpar=tpar, bpar=bpar:
                              e.tensor_tensor(
                                  tmpd[tpar][:, :],
                                  resid[:, c * T + nt * 512:c * T + nt * 512 + 512],
                                  bc_ps[2 * bpar][:, :], ALU.subtract),
                              waits={"pe": pe_c if c == 0 else 0,
                                     "dve": sub_done[tpar]})
                    sub_done[tpar] = emit("dve", lambda e, c=c, nt=nt, tpar=tpar, bpar=bpar:
                                          e.tensor_tensor(
                                              xln[:, c * T + nt * 512:c * T + nt * 512 + 512],
                                              tmpd[tpar][:, :], bc_ps[2 * bpar + 1][:, :],
                                              ALU.mult), waits={"dve": s0})
                bc_free[bpar] = norm_nt[nt] = max(sub_done)
            return norm_nt, (ev[0], ev[1])

        sums_ps = [pb[0][0:1, :], pb[1][0:1, :]]
        bc_ps = [pb[2], pb[3], pb[4], pb[5]]
        ln1_nt, ln1_sums_last = emit_layernorm(sums_ps, bc_ps, msq)
        ln1_done = max(ln1_nt)

        if phases >= 2:
            aes = contextlib.ExitStack()
            asb = lambda name, shp, dt: aes.enter_context(nc.sbuf_tensor("sb_" + name, shp, dt))
            wq_s = [asb(f"wq_s{i}", [128, 1024], BF16) for i in range(2)]
            wk_s = [asb(f"wk_s{i}", [128, 1024], BF16) for i in range(2)]
            wv_s = [asb(f"wv_s{i}", [128, 1024], BF16) for i in range(2)]
            wp0 = [asb(f"wp0_{i}", [64, 1024], BF16) for i in range(2)]
            wp1 = [asb(f"wp1_{i}", [64, 1024], BF16) for i in range(2)]
            Qp = asb("Qp", [128, T], BF16)
            Kp = asb("Kp", [128, T], BF16)
            Vp = asb("Vp", [128, 16 * 130], BF16)
            Vf = asb("Vf", [128, T], BF16)
            NP = 8
            Ppool = asb("Ppool", [128, NP * 512], BF16)
            ctx0 = asb("ctx0", [64, T], BF16)
            ctx1 = asb("ctx1", [64, T], BF16)
            lrow = asb("lrow", [1, 512], F32)
            rrow_h = [asb(f"rrow{i}", [1, 512], BF16) for i in range(2)]
            rbb_h = [asb(f"rbb{i}", [64, 512], BF16) for i in range(2)]
            sc_ps = [pb[2], pb[3], pb[4]]
            qk_ps = [pb[0], pb[1]]
            ctx_ps = [pb[5], pb[6]]
            tr2_ps = [pb7]

            def f_vones(e):
                V4 = Vp[:].rearrange("p (kt h d) -> p kt h d", kt=16, h=2)
                e.memset(V4[:, :, 0:1, 64:65], 1.0)
                return e.memset(V4[:, :, 1:2, 64:65], 1.0)
            vones = emit("dve", f_vones)

            wbuf_free = [0, 0]
            wload = [0] * 8

            def emit_pair_load(hp):
                par = hp % 2
                sem = "wpA" if par == 0 else "wpB"
                emit("pool", lambda e, hp=hp, par=par: e.dma_start(
                    out=wq_s[par][:].rearrange("p (c m) -> p c m", c=8),
                    in_=d_wq[:, hp * 128:(hp + 1) * 128].rearrange("(c p) m -> p c m", p=128)),
                    waits={"pe": wbuf_free[par]}, dma=sem)
                emit("pool", lambda e, hp=hp, par=par: e.dma_start(
                    out=wk_s[par][:].rearrange("p (c m) -> p c m", c=8),
                    in_=d_wk[:, hp * 128:(hp + 1) * 128].rearrange("(c p) m -> p c m", p=128)), dma=sem)
                emit("pool", lambda e, hp=hp, par=par: e.dma_start(
                    out=wv_s[par][:].rearrange("p (c m) -> p c m", c=8),
                    in_=d_wv[:, hp * 128:(hp + 1) * 128].rearrange("(c p) m -> p c m", p=128)), dma=sem)
                emit("pool", lambda e, hp=hp, par=par: e.dma_start(
                    out=wp0[par][:, :], in_=d_wp[hp * 128:hp * 128 + 64, :]), dma=sem)
                wload[hp] = emit("pool", lambda e, hp=hp, par=par: e.dma_start(
                    out=wp1[par][:, :], in_=d_wp[hp * 128 + 64:(hp + 1) * 128, :]), dma=sem)

            emit_pair_load(0)
            emit_pair_load(1)

            qk_free = list(ln1_sums_last)
            pending_norm = []
            sc_rr = [0]
            rb_pe = [0]
            pslot_free = [0] * 8
            pslot_rr = [0]
            sc_free = [0, 0, 0]
            ctx_free = [ln1_nt[3], 0]
            it = 0
            for hp in range(npair):
                par = hp % 2
                for which, (wslc, dst) in enumerate([(wq_s, Qp), (wk_s, Kp)]):
                    for ntl in range(4):
                        qpar = (which * 4 + ntl) % 2

                        def f(e, wslc=wslc, ntl=ntl, par=par, qpar=qpar):
                            for k in range(8):
                                last = e.matmul(
                                    qk_ps[qpar][:, :],
                                    wslc[par][:, k * 128:(k + 1) * 128],
                                    xln[:, k * T + ntl * 512:k * T + ntl * 512 + 512],
                                    start=(k == 0), stop=(k == 7))
                            return last
                        pe_c = emit("pe", f, waits={"dve": max(qk_free[qpar], ln1_nt[ntl]),
                                                    ("wpA" if par == 0 else "wpB"): wload[hp]})
                        qk_free[qpar] = emit("dve", lambda e, dst=dst, ntl=ntl, qpar=qpar:
                                             e.tensor_copy(dst[:, ntl * 512:ntl * 512 + 512],
                                                           qk_ps[qpar][:, :]),
                                             waits={"pe": pe_c})
                vf_done = [0, 0, 0, 0]
                for ntl in range(4):
                    qpar = ntl % 2

                    def f(e, ntl=ntl, par=par, qpar=qpar):
                        for k in range(8):
                            last = e.matmul(
                                qk_ps[qpar][:, :],
                                wv_s[par][:, k * 128:(k + 1) * 128],
                                xln[:, k * T + ntl * 512:k * T + ntl * 512 + 512],
                                start=(k == 0), stop=(k == 7))
                        return last
                    pe_c = emit("pe", f, waits={"dve": max(qk_free[qpar], ln1_nt[ntl])})
                    qk_free[qpar] = vf_done[ntl] = emit(
                        "dve", lambda e, ntl=ntl, qpar=qpar: e.tensor_copy(
                            Vf[:, ntl * 512:ntl * 512 + 512], qk_ps[qpar][:, :]),
                        waits={"pe": pe_c})
                tr_done = 0
                for kt in range(16):
                    pe_c = emit("pe", lambda e, kt=kt: e.transpose(
                        tr2_ps[0][:, 0:128], Vf[:, kt * 128:(kt + 1) * 128], ident_bf[:, :]),
                        waits={"dve": max(vf_done[kt // 4], tr_done),
                               "ldc": LDS_CONST})
                    tr_done = emit("dve", lambda e, kt=kt: e.tensor_copy(
                        Vp[:, kt * 130:(kt + 1) * 130].rearrange("p (h d) -> p h d", h=2)[:, :, 0:64],
                        tr2_ps[0][:, 0:128].rearrange("p (h d) -> p h d", h=2)),
                        waits={"pe": pe_c,
                               "dve": vones if (hp == 0 and kt == 0) else 0})
                vp_done = max(max(qk_free), tr_done)

                def flush_norms():
                    while pending_norm:
                        hh_, b_, dve_c_ = pending_norm.pop(0)
                        pe_c = emit("pe", lambda e, hh_=hh_: e.matmul(
                            qk_ps[0][0:64, :], onesb[0:1, 0:64], rrow_h[hh_][:, :],
                            start=True, stop=True),
                            waits={"dve": max(dve_c_, qk_free[0])})
                        rb_pe[0] = pe_c
                        e0 = emit("dve", lambda e, hh_=hh_: e.tensor_copy(
                            rbb_h[hh_][:, :], qk_ps[0][0:64, :]), waits={"pe": pe_c})
                        qk_free[0] = e0
                        def fevac(e, hh_=hh_, b_=b_):
                            ctx = ctx0 if hh_ == 0 else ctx1
                            return e.tensor_tensor(ctx[:, b_ * 512:(b_ + 1) * 512],
                                                   ctx_ps[hh_][0:64, :],
                                                   rbb_h[hh_][:, :], ALU.mult)
                        ctx_free[hh_] = emit("dve", fevac, waits={"dve": e0})

                for b in range(NB):
                    nkt = 4 * (b + 1)
                    LKA = 4
                    exp_at = {}
                    mask_at = {}
                    slot_of = {}
                    pv_q = []
                    pv_last = [0, 0]

                    def emit_pv(hh, kt):
                        slot = slot_of[(hh, kt)]

                        def fpv(e, hh=hh, kt=kt, nkt=nkt, slot=slot):
                            return e.matmul(
                                ctx_ps[hh][0:65, :],
                                Vp[:, kt * 130 + hh * 65:kt * 130 + hh * 65 + 65],
                                Ppool[:, slot * 512:(slot + 1) * 512],
                                start=(kt == 0), stop=(kt == nkt - 1))
                        w = {}
                        if (hh, kt) in mask_at:
                            w["dve"] = mask_at[(hh, kt)]
                        else:
                            w["act"] = exp_at[(hh, kt)]
                        if kt == 0:
                            w["dve"] = max(w.get("dve", 0), ctx_free[hh])
                        c = emit("pe", fpv, waits=w)
                        pslot_free[slot] = c
                        pv_last[hh] = c
                        return c

                    for kt in range(nkt):
                        while len(pv_q) > 2 * LKA - 2:
                            emit_pv(*pv_q.pop(0))
                        for hh in range(2):
                            spar = sc_rr[0] % 3
                            sc_rr[0] += 1
                            pe_c = emit("pe", lambda e, hh=hh, b=b, kt=kt, spar=spar: e.matmul(
                                sc_ps[spar][:, :],
                                Kp[hh * 64:(hh + 1) * 64, kt * 128:(kt + 1) * 128],
                                Qp[hh * 64:(hh + 1) * 64, b * 512:(b + 1) * 512],
                                start=True, stop=True),
                                waits={"act": sc_free[spar],
                                       "dve": vp_done if (kt == 0 and hh == 0 and b == 0) else 0})
                            slot = pslot_rr[0] % NP
                            pslot_rr[0] += 1
                            slot_of[(hh, kt)] = slot
                            col = (2 * hp + hh) * 64 + b * 16 + kt
                            act_c = emit("act", lambda e, kt=kt, spar=spar, col=col, slot=slot:
                                         e.activation(Ppool[:, slot * 512:(slot + 1) * 512],
                                                      sc_ps[spar][:, :], AF.Exp,
                                                      bias=alibi[:, col:col + 1], scale=0.125),
                                         waits={"pe": max(pe_c, pslot_free[slot])})
                            sc_free[spar] = act_c
                            exp_at[(hh, kt)] = act_c
                            r = kt - 4 * b
                            if r >= 0:
                                mask_at[(hh, kt)] = emit("dve", lambda e, kt=kt, r=r, slot=slot:
                                                         e.tensor_tensor(
                                                             Ppool[:, slot * 512:(slot + 1) * 512],
                                                             Ppool[:, slot * 512:(slot + 1) * 512],
                                                             masks[:, r * 512:(r + 1) * 512], ALU.mult),
                                                         waits={"act": act_c})
                            pv_q.append((hh, kt))
                        if kt == 1:
                            flush_norms()
                    while pv_q:
                        emit_pv(*pv_q.pop(0))

                    prev_tail = 0
                    for hh in range(2):
                        l0 = emit("dve", lambda e, hh=hh: e.reciprocal(
                            lrow[:, :], ctx_ps[hh][64:65, :]),
                            waits={"pe": max(pv_last[hh], rb_pe[0]),
                                   "dve": prev_tail})
                        dve_c = emit("dve", lambda e, hh=hh: e.tensor_copy(
                            rrow_h[hh][:, :], lrow[:, :]), waits={"dve": l0})
                        prev_tail = dve_c
                        pending_norm.append((hh, b, dve_c))
                flush_norms()
                ctx_all = max(ctx_free)

                for dout in range(8):
                    for tt in range(4):
                        qpar = (dout * 4 + tt) % 2

                        def fp(e, hp=hp, dout=dout, tt=tt, qpar=qpar, par=par):
                            first = True
                            if hp == 0:
                                e.matmul(qk_ps[qpar][:, :],
                                         bp[:, dout * 128:(dout + 1) * 128],
                                         onesb[0:1, :], start=True, stop=False)
                                first = False
                            e.matmul(qk_ps[qpar][:, :],
                                     wp0[par][:, dout * 128:(dout + 1) * 128],
                                     ctx0[:, tt * 512:tt * 512 + 512],
                                     start=first, stop=False)
                            return e.matmul(qk_ps[qpar][:, :],
                                            wp1[par][:, dout * 128:(dout + 1) * 128],
                                            ctx1[:, tt * 512:tt * 512 + 512],
                                            start=False, stop=True)
                        pe_c = emit("pe", fp, waits={"dve": max(ctx_all, qk_free[qpar])})

                        qk_free[qpar] = emit("dve", lambda e, dout=dout, tt=tt, qpar=qpar:
                                             e.tensor_tensor(
                                                 resid[:, dout * T + tt * 512:dout * T + tt * 512 + 512],
                                                 resid[:, dout * T + tt * 512:dout * T + tt * 512 + 512],
                                                 qk_ps[qpar][:, :],
                                                 ALU.add),
                                             waits={"pe": pe_c})
                wbuf_free[par] = n["pe"]
                if hp + 2 < npair:
                    emit_pair_load(hp + 2)
            attn_done = max(qk_free)
            if debug:
                for nm, buf in [("xln", xln), ("Qp", Qp), ("Kp", Kp), ("Vp", Vp),
                                ("ctx0", ctx0), ("ctx1", ctx1), ("Pp", Ppool),
                                ("lr", lrow), ("rb", rbb_h[0]),
                                ("wqs", wq_s[0]), ("wks", wk_s[0]), ("wvs", wv_s[0])]:
                    emit("pool", lambda e, nm=nm, buf=buf: e.dma_start(
                        out=d_dbg[nm][:, :], in_=buf[:, :]),
                        waits={"dve": n["dve"], "pe": n["pe"], "act": n["act"]}, dma="st")
            aes.close()
        else:
            attn_done = ln1_done


        if phases >= 3:
            sums_ps = [pb[0][0:1, :], pb[1][0:1, :]]
            bc_ps = [pb[2], pb[3], pb[4], pb[5]]
            ln2_nt, ln2_sums_last = emit_layernorm(sums_ps, bc_ps, msq, pre={"dve": attn_done})
            ln2_done = max(ln2_nt)
        else:
            ln2_done = attn_done


        if phases >= 4:
            fes = contextlib.ExitStack()
            fsb = lambda name, shp, dt: fes.enter_context(nc.sbuf_tensor("sb_" + name, shp, dt))
            w1g = fsb("w1g", [128, G * 1024], BF16)
            w2g = fsb("w2g", [128, G * 1024], BF16)
            hsb2 = fsb("hsb2", [128, 16 * 512], BF16)
            comb_fmb = fsb("comb_fmb", [16, T], BF16)
            comb_tm = fsb("comb_tm", [128, 16 * 9], BF16)
            comb_b = [fsb(f"comb_b{i}", [128, T], BF16) for i in range(2)]
            lg = fsb("lg", [128, 8], F32)
            g_m1 = fsb("g_m1", [128, 1], F32)
            g_m2 = fsb("g_m2", [128, 1], F32)
            g_eq1 = fsb("g_eq1", [128, 8], F32)
            g_eq2 = fsb("g_eq2", [128, 8], F32)
            g_msk = fsb("g_msk", [128, 8], F32)
            g_d = fsb("g_d", [128, 1], F32)
            g_e = fsb("g_e", [128, 1], F32)
            g_r = fsb("g_r", [128, 1], F32)
            g_gw2 = fsb("g_gw2", [128, 1], F32)
            g_c1 = fsb("g_c1", [128, 8], F32)
            g_c2 = fsb("g_c2", [128, 8], F32)
            g_ps = pb[6]
            tr_ps = pb7
            o_ps = [pb[2], pb[3], pb[4], pb[5]]
            h_ps = [pb[0], pb[1]]

            ctm_ones = emit("dve", lambda e: e.memset(
                comb_tm[:].rearrange("p (t e) -> p t e", e=9)[:, :, 8:9], 1.0),
                waits={"dve": ln2_done})
            pe_c = emit("pe", lambda e: e.matmul(g_ps[:, 0:8], onesf[:, :], gb_row[:, :],
                                                 start=True, stop=True),
                        waits={"ldc": LDS_CONST, "dve": ln2_done})
            gbb_done = emit("dve", lambda e: e.tensor_copy(gb_b[:, :], g_ps[:, 0:8]),
                            waits={"pe": pe_c})

            gate_st = {"g_ps_free": gbb_done, "tr_free": 0, "tr_evac": 0,
                       "comb_cast": 0, "steps": list(range(16))}

            def gate_tile(ttk):
                def f_lg(e, ttk=ttk):
                    for k in range(8):
                        last = e.matmul(g_ps[:, 0:8],
                                        xln[:, k * T + ttk * 128:k * T + (ttk + 1) * 128],
                                        gw[:, k * 8:(k + 1) * 8],
                                        start=(k == 0), stop=(k == 7))
                    return last
                pe_c = emit("pe", f_lg, waits={"dve": max(gate_st["g_ps_free"], ln2_nt[ttk // 4])})
                q0 = emit("dve", lambda e: e.tensor_tensor(
                    lg[:, :], g_ps[:, 0:8], gb_b[:, :], ALU.add), waits={"pe": pe_c})
                q1 = emit("dve", lambda e: e.tensor_reduce(
                    g_m1[:, :], lg[:, :], AX.X, ALU.max), waits={"dve": q0})
                q2 = emit("dve", lambda e: e.tensor_scalar(
                    g_eq1[:, :], lg[:, :], g_m1[:, 0:1], None, ALU.is_equal), waits={"dve": q1})
                q3 = emit("dve", lambda e: e.scalar_tensor_tensor(
                    g_msk[:, :], g_eq1[:, :], -1e30, lg[:, :], ALU.mult, ALU.add),
                    waits={"dve": q2})
                q4 = emit("dve", lambda e: e.tensor_reduce(
                    g_m2[:, :], g_msk[:, :], AX.X, ALU.max), waits={"dve": q3})
                q5 = emit("dve", lambda e: e.tensor_scalar(
                    g_eq2[:, :], g_msk[:, :], g_m2[:, 0:1], None, ALU.is_equal), waits={"dve": q4})
                d1 = emit("dve", lambda e: e.tensor_tensor(
                    g_d[:, :], g_m2[:, :], g_m1[:, :], ALU.subtract), waits={"dve": q5})
                a1 = emit("act", lambda e: e.activation(g_e[:, :], g_d[:, :], AF.Exp),
                          waits={"dve": d1})
                p0 = emit("dve", lambda e: e.tensor_scalar(
                    g_gw2[:, :], g_e[:, :], 1.0, None, ALU.add), waits={"act": a1})
                p1 = emit("dve", lambda e: e.reciprocal(
                    g_r[:, :], g_gw2[:, :]), waits={"dve": p0})
                p2 = emit("dve", lambda e: e.tensor_tensor(
                    g_gw2[:, :], g_e[:, :], g_r[:, :], ALU.mult), waits={"dve": p1})
                p3 = emit("dve", lambda e: e.tensor_scalar(
                    g_c1[:, :], g_eq1[:, :], g_r[:, 0:1], None, ALU.mult), waits={"dve": p2})
                p4 = emit("dve", lambda e: e.tensor_scalar(
                    g_c2[:, :], g_eq2[:, :], g_gw2[:, 0:1], None, ALU.mult), waits={"dve": p3})
                g_ps_free = emit("dve", lambda e, ttk=ttk: e.tensor_tensor(
                    comb_tm[:, ttk * 9:ttk * 9 + 8], g_c1[:, :], g_c2[:, :], ALU.add),
                    waits={"dve": p4})
                gate_st["g_ps_free"] = g_ps_free
                pe_c = emit("pe", lambda e, ttk=ttk: e.transpose(
                    tr_ps[0:9, 0:128], comb_tm[:, ttk * 9:ttk * 9 + 9], ident_bf[:, :]),
                    waits={"dve": max(g_ps_free, gate_st["tr_free"], ctm_ones),
                           "ldc": LDS_CONST})
                tr_evac = emit("dve", lambda e, ttk=ttk: e.tensor_copy(
                    comb_fmb[0:9, ttk * 128:(ttk + 1) * 128], tr_ps[0:9, 0:128]),
                    waits={"pe": pe_c})
                gate_st["tr_free"] = tr_evac
                gate_st["tr_evac"] = tr_evac
                gate_st["comb_cast"] = tr_evac

            gate_tick = [0]

            def maybe_gate():
                if gate_st["steps"]:
                    gate_tile(gate_st["steps"].pop(0))

            def drain_gate():
                while gate_st["steps"]:
                    maybe_gate()

            ffn_last = 0
            if phases >= 5:
                w1_ld = [0] * NEG
                w2_ld = [0] * NEG
                A_done = [0] * NEG
                C_done = [0] * NEG
                h_free = [0, 0]
                o_free = [ln2_nt[2], ln2_nt[2], ln2_nt[3], ln2_nt[3]]
                cb_done = [0, 0]
                cb_used = [0, 0]
                ffn_last = 0

                def emit_w_load(i):
                    o, g = divmod(i, NG)
                    e_idx = EXPERT_ORDER[o]
                    if e_idx == 8:
                        src1 = d_ws1[:, g * 1024:(g + 1) * 1024]
                        src2 = d_ws2[g * 1024:(g + 1) * 1024, :]
                    else:
                        src1 = d_w1[e_idx, :, g * 1024:(g + 1) * 1024]
                        src2 = d_w2[e_idx, g * 1024:(g + 1) * 1024, :]
                    w1_ld[i] = emit("pool", lambda e, src1=src1: e.dma_start(
                        out=w1g[:].rearrange("p (c m) -> p c m", c=8),
                        in_=src1.rearrange("(c p) m -> p c m", p=128)),
                        waits={"pe": A_done[i - 1]} if i else {"dve": attn_done}, dma="w1s")
                    w2_ld[i] = emit("pool", lambda e, src2=src2: e.dma_start(
                        out=w2g[:].rearrange("p (m d) -> p m d", m=8),
                        in_=src2.rearrange("(m p) d -> p m d", p=128)),
                        waits={"pe": C_done[i - 1]} if i else {"dve": attn_done}, dma="w2s")

                def emit_comb_b(e_idx):
                    par = e_idx % 2
                    evac = 0
                    for ntl in range(4):
                        bank = ntl % 2
                        pe_c = emit("pe", lambda e, e_idx=e_idx, ntl=ntl, bank=bank: e.matmul(
                            o_ps[bank][:, :], sel[0:9, e_idx * 128:(e_idx + 1) * 128],
                            comb_fmb[0:9, ntl * 512:(ntl + 1) * 512],
                            start=True, stop=True),
                            waits={"dve": max(gate_st["comb_cast"], o_free[bank],
                                              cb_used[par] if ntl == 0 else 0)})
                        evac = emit("dve", lambda e, par=par, ntl=ntl, bank=bank: e.tensor_copy(
                            comb_b[par][:, ntl * 512:(ntl + 1) * 512], o_ps[bank][:, :]),
                            waits={"pe": pe_c})
                        o_free[bank] = evac
                    cb_done[par] = evac

                emit_w_load(0)

                prev_B_pe = 0
                for o, e_idx in enumerate(EXPERT_ORDER):
                    if o + 1 < len(EXPERT_ORDER):
                        emit_comb_b(EXPERT_ORDER[o + 1])
                    par_e = e_idx % 2
                    for g in range(NG):
                        i = o * NG + g
                        if i >= ffn_groups:
                            continue
                        last_A = 0
                        last_B = 0
                        for half in range(2):
                            h_ready = []
                            for tl in range(2):
                                t = 2 * half + tl
                                for m in range(G):
                                    hpar = (t * G + m) % 2
                                    slot = tl * G + m

                                    def fA(e, t=t, m=m, hpar=hpar):
                                        for k in range(8):
                                            last = e.matmul(
                                                h_ps[hpar][:, :],
                                                w1g[:, k * 1024 + m * 128:k * 1024 + (m + 1) * 128],
                                                xln[:, k * T + t * 512:k * T + t * 512 + 512],
                                                start=(k == 0), stop=(k == 7))
                                        return last
                                    w = {"act": h_free[hpar]}
                                    if half == 0 and tl == 0 and m == 0:
                                        w["w1s"] = w1_ld[i]
                                        if i == 0:
                                            w["dve"] = ln2_done
                                    last_A = pe_c = emit("pe", fA, waits=w)
                                    if o == 0:
                                        gate_tick[0] += 1
                                        if gate_tick[0] % 6 == 0:
                                            maybe_gate()

                                    if ffn_parts < 2:
                                        h_ready.append(("act", 0))
                                        continue
                                    bcol = e_idx * 32 + g * 8 + m
                                    wg = {"pe": max(pe_c,
                                                    prev_B_pe if (tl == 0 and m == 0) else 0)}
                                    act_c = emit("act", lambda e, slot=slot, hpar=hpar, bcol=bcol:
                                                 e.activation(hsb2[:, slot * 512:(slot + 1) * 512],
                                                              h_ps[hpar][:, :], AF.Gelu,
                                                              bias=b1t[:, bcol:bcol + 1]),
                                                 waits=wg)
                                    h_free[hpar] = act_c
                                    if e_idx != 8:
                                        dve_c = emit("dve", lambda e, slot=slot, t=t, par_e=par_e:
                                                     e.tensor_tensor(
                                                         hsb2[:, slot * 512:(slot + 1) * 512],
                                                         hsb2[:, slot * 512:(slot + 1) * 512],
                                                         comb_b[par_e][:, t * 512:(t + 1) * 512],
                                                         ALU.mult),
                                                     waits={"act": act_c,
                                                            "dve": cb_done[par_e] if (half == 0 and tl == 0 and m == 0) else 0})
                                        cb_used[par_e] = dve_c
                                        h_ready.append(("dve", dve_c))
                                    else:
                                        h_ready.append(("act", act_c))
                            for tl in (range(2) if ffn_parts >= 3 else []):
                                t = 2 * half + tl
                                tslots = h_ready[tl * G:(tl + 1) * G]
                                h_act = max((c for k, c in tslots if k == "act"), default=0)
                                h_dve = max((c for k, c in tslots if k == "dve"), default=0)
                                for d in range(8):
                                    bank = d % 4
                                    inject_b2 = (e_idx == 7 and g == NG - 1)

                                    def fB(e, d=d, t=t, tl=tl, bank=bank, inject_b2=inject_b2):
                                        for m in range(G):
                                            last = e.matmul(
                                                o_ps[bank][:, :],
                                                w2g[:, m * 1024 + d * 128:m * 1024 + (d + 1) * 128],
                                                hsb2[:, (tl * G + m) * 512:(tl * G + m + 1) * 512],
                                                start=(m == 0), stop=(m == G - 1 and not inject_b2))
                                        if inject_b2:
                                            last = e.matmul(o_ps[bank][:, :],
                                                            b2c[0:9, d * 128:(d + 1) * 128],
                                                            comb_fmb[0:9, t * 512:(t + 1) * 512],
                                                            start=False, stop=True)
                                        return last
                                    w = {"act": h_act if d == 0 else 0,
                                         "dve": max(h_dve if d == 0 else 0, o_free[bank])}
                                    if half == 0 and tl == 0 and d == 0:
                                        w["w2s"] = w2_ld[i]
                                    last_B = pe_c = emit("pe", fB, waits=w)

                                    if ffn_parts >= 4:
                                        if ffn_parts == 5:
                                            fn_ev = (lambda e, d=d, t=t, bank=bank:
                                                     e.tensor_copy(tmp1[:, :], o_ps[bank][:, :]))
                                        elif ffn_parts == 6:
                                            fn_ev = (lambda e, d=d, t=t, bank=bank:
                                                     e.tensor_tensor(tmp1[:, :], o_ps[bank][:, :],
                                                                     resid[:, d * T + t * 512:d * T + t * 512 + 512],
                                                                     ALU.add))
                                        else:
                                            fn_ev = (lambda e, d=d, t=t, bank=bank:
                                                     e.tensor_tensor(resid[:, d * T + t * 512:d * T + t * 512 + 512],
                                                                     resid[:, d * T + t * 512:d * T + t * 512 + 512],
                                                                     o_ps[bank][:, :],
                                                                     ALU.add))
                                        o_free[bank] = ffn_last = emit(
                                            "dve", fn_ev, waits={"pe": pe_c})
                            prev_B_pe = last_B
                        A_done[i] = last_A
                        C_done[i] = last_B
                        if i + 1 < min(NEG, ffn_groups):
                            emit_w_load(i + 1)
                        if g == NG - 1:
                            drain_gate()
                            if o + 1 < len(EXPERT_ORDER):
                                emit_comb_b(EXPERT_ORDER[o + 1])
            fes.close()

        if debug and phases >= 4:
            for nm, buf in [("cfmb", comb_fmb), ("gbb", gb_b), ("ctm", comb_tm),
                            ("g_lg", lg), ("g_m1", g_m1), ("g_m2", g_m2), ("g_msk", g_msk),
                            ("g_eq1", g_eq1), ("g_eq2", g_eq2), ("g_d", g_d), ("g_e", g_e),
                            ("g_r", g_r), ("g_gw2", g_gw2), ("g_c1", g_c1), ("g_c2", g_c2)]:
                emit("pool", lambda e, nm=nm, buf=buf: e.dma_start(
                    out=d_dbg[nm][:, :], in_=buf[:, :]),
                    waits={"dve": n["dve"], "pe": n["pe"], "act": n["act"]}, dma="st")
        emit("sync", lambda e: e.dma_start(
            out=d_out[:, :].rearrange("(c p) t -> p c t", p=128),
            in_=resid[:].rearrange("p (c t) -> p c t", c=8)),
            waits={"dve": n["dve"], "act": n["act"], "pe": n["pe"]}, dma="st")
        emit("sync", lambda e: None, waits={"st": n["st"]}, inc=False)

        _check_schedule(ops, n)

        if True:
            _dispatch(nc, ops, s)
    return nc


def _check_schedule(ops, totals):
    sems = {k: 0 for k in totals}
    ptr = {k: 0 for k in ops}
    for k, v in totals.items():
        pass
    progressed = True
    while progressed:
        progressed = False
        for eng, lst in ops.items():
            while ptr[eng] < len(lst):
                w, fn, inc_sem, amt = lst[ptr[eng]]
                if all(sems[sk] >= sv for sk, sv in w.items()):
                    if inc_sem:
                        sems[inc_sem] += amt
                    ptr[eng] += 1
                    progressed = True
                else:
                    break
    stuck = {e: (ptr[e], len(l)) for e, l in ops.items() if ptr[e] < len(l)}
    if stuck:
        det = {}
        for eng, (p, ln) in stuck.items():
            det[eng] = (p, ln, ops[eng][p][0], dict(sems))
        raise RuntimeError(f"schedule deadlock: {det}")


def _dispatch(nc, ops, s):
    with nc.Block() as block:
        def body(eng_name):
            def run(engine):
                for w, fn, inc_sem, amt in ops[eng_name]:
                    for sk, sv in w.items():
                        engine.wait_ge(s[sk], sv)
                    last = fn(engine)
                    if inc_sem is not None and last is not None:
                        last.then_inc(s[inc_sem], amt)
            return run
        block.tensor(body("pe"))
        block.scalar(body("act"))
        block.vector(body("dve"))
        block.gpsimd(body("pool"))
        block.sync(body("sync"))


_NC_CACHE = []


def kernel(**inputs):
    x = np.ascontiguousarray(np.asarray(inputs["L0_attn_x"], dtype=np.float32))
    f32 = lambda k: np.ascontiguousarray(np.asarray(inputs[k], dtype=np.float32))
    wq, wk, wv, wp = f32("L0_attn_W_query"), f32("L0_attn_W_key"), f32("L0_attn_W_value"), f32("L0_attn_W_proj")
    bpv = f32("L0_attn_b_proj").reshape(1, D)
    gwv = f32("L0_ffn_gate_w")
    gbv = f32("L0_ffn_gate_b").reshape(1, 8)
    w1 = f32("L0_ffn_w1")
    b1 = f32("L0_ffn_b1")
    w2 = f32("L0_ffn_w2")
    b2 = f32("L0_ffn_b2")
    ws1 = f32("L0_ffn_ws1")
    bs1 = f32("L0_ffn_bs1")
    ws2 = f32("L0_ffn_ws2")
    bs2 = f32("L0_ffn_bs2")

    b1t = np.ascontiguousarray(
        np.concatenate([b1.reshape(8, 32, 128), bs1.reshape(1, 32, 128)], axis=0)
        .transpose(2, 0, 1).reshape(128, 288))
    b2c = np.ascontiguousarray(np.concatenate([b2, bs2.reshape(1, D)], axis=0))
    alibi, masksv, ones, identv, selv = _host_tables()

    if not _NC_CACHE:
        _NC_CACHE.append(build_program(phases=5))
    nc = _NC_CACHE[0]

    base = {"wq": wq, "wk": wk, "wv": wv, "wp": wp, "bp": bpv,
            "gw": gwv, "gb": gbv, "w1": w1, "w2": w2, "ws1": ws1, "ws2": ws2,
            "b1t": b1t, "b2c": b2c, "alibi": alibi, "masksd": masksv,
            "onesd": ones, "identd": identv, "seld": selv}
    in_maps = [dict(base, x=np.ascontiguousarray(x[c].T)) for c in range(8)]
    res = run_bass_kernel_spmd(nc, in_maps, core_ids=list(range(8)))
    out = np.stack([np.asarray(r["out"]).T for r in res.results], axis=0)
    return np.ascontiguousarray(out.astype(np.float32))


if __name__ == "__main__":
    import time
    t0 = time.time()
    nc = build_program()
    print(f"build ok {time.time()-t0:.1f}s n_inst={nc.n_instructions()}")


# version 13
# speedup vs baseline: 1.0837x; 1.0186x over previous
import contextlib

import numpy as np
import ml_dtypes

import concourse.bass as bass
import concourse.mybir as mybir
from concourse.bass_utils import run_bass_kernel_spmd

F32 = mybir.dt.float32
BF16 = mybir.dt.bfloat16
AF = mybir.ActivationFunctionType
ALU = mybir.AluOpType
AX = mybir.AxisListType

T = 2048
D = 1024
H = 16
NB = 4
G = 8
NG = 32 // G
EXPERT_ORDER = [8, 0, 1, 2, 3, 4, 5, 6, 7]
NEG = len(EXPERT_ORDER) * NG


def _host_tables():
    slopes = 2.0 ** (-8.0 * np.arange(1, H + 1) / H)
    boff = [0, 4, 12, 24]
    alibi = np.zeros((128, 640), np.float32)
    for h in range(H):
        for b in range(NB):
            for kt in range(4 * (b + 1)):
                k = 128 * kt + np.arange(128)
                alibi[:, h * 40 + boff[b] + kt] = slopes[h] * (k - (512 * (b + 1) - 1)) / 8.0
    masks = np.zeros((4, 128, 512), np.float32)
    p = np.arange(128)[:, None]
    j = np.arange(512)[None, :]
    for r in range(4):
        masks[r] = (j >= p + 128 * r).astype(np.float32)
    ones = np.ones((128, 512), np.float32)
    ident = np.eye(128, dtype=np.float32)
    sel = np.zeros((9, 8 * 128), np.float32)
    for e in range(8):
        sel[e, e * 128:(e + 1) * 128] = 1.0
    return alibi, masks, ones, ident, sel


def build_program(phases=5, npair=8, debug=False, ffn_groups=NEG, ffn_parts=4):
    nc = bass.Bass("TRN2", target_bir_lowering=False)

    d_x = nc.dram_tensor("x", [D, T], F32, kind="ExternalInput")
    d_wq = nc.dram_tensor("wq", [D, D], F32, kind="ExternalInput")
    d_wk = nc.dram_tensor("wk", [D, D], F32, kind="ExternalInput")
    d_wv = nc.dram_tensor("wv", [D, D], F32, kind="ExternalInput")
    d_wp = nc.dram_tensor("wp", [D, D], F32, kind="ExternalInput")
    d_bp = nc.dram_tensor("bp", [1, D], F32, kind="ExternalInput")
    d_gw = nc.dram_tensor("gw", [D, 8], F32, kind="ExternalInput")
    d_gb = nc.dram_tensor("gb", [1, 8], F32, kind="ExternalInput")
    d_w1 = nc.dram_tensor("w1", [8, D, 4096], F32, kind="ExternalInput")
    d_w2 = nc.dram_tensor("w2", [8, 4096, D], F32, kind="ExternalInput")
    d_ws1 = nc.dram_tensor("ws1", [D, 4096], F32, kind="ExternalInput")
    d_ws2 = nc.dram_tensor("ws2", [4096, D], F32, kind="ExternalInput")
    d_b1t = nc.dram_tensor("b1t", [128, 288], F32, kind="ExternalInput")
    d_b2c = nc.dram_tensor("b2c", [9, D], F32, kind="ExternalInput")
    d_alibi = nc.dram_tensor("alibi", [128, 640], F32, kind="ExternalInput")
    d_masks = nc.dram_tensor("masksd", [4, 128, 512], F32, kind="ExternalInput")
    d_ones = nc.dram_tensor("onesd", [128, 512], F32, kind="ExternalInput")
    d_ident = nc.dram_tensor("identd", [128, 128], F32, kind="ExternalInput")
    d_sel = nc.dram_tensor("seld", [9, 8 * 128], F32, kind="ExternalInput")
    d_out = nc.dram_tensor("out", [D, T], F32, kind="ExternalOutput")
    if debug:
        d_dbg = {nm: nc.dram_tensor("dbg_" + nm, shp, F32, kind="ExternalOutput")
                 for nm, shp in [("xln", [128, 8 * T]), ("Qp", [128, T]), ("Kp", [128, T]),
                                 ("Vp", [128, 16 * 130]), ("ctx0", [64, T]), ("ctx1", [64, T]),
                                 ("Pp", [128, 6 * 512]), ("lr", [1, 512]), ("rb", [64, 512]),
                                 ("wqs", [128, 1024]), ("wks", [128, 1024]), ("wvs", [128, 1024]),
                                 ("cfmb", [16, T]), ("gbb", [128, 8]), ("ctm", [128, 16 * 9]),
                                 ("g_lg", [128, 8]), ("g_m1", [128, 1]), ("g_m2", [128, 1]),
                                 ("g_msk", [128, 8]), ("g_eq1", [128, 8]), ("g_eq2", [128, 8]),
                                 ("g_d", [128, 1]), ("g_e", [128, 1]), ("g_r", [128, 1]),
                                 ("g_gw2", [128, 1]), ("g_c1", [128, 8]), ("g_c2", [128, 8])]}

    es = contextlib.ExitStack()
    with es:
        sbt = lambda name, shp, dt: es.enter_context(nc.sbuf_tensor("sb_" + name, shp, dt))
        resid = sbt("resid", [128, 8 * T], F32)
        xln = sbt("xln", [128, 8 * T], BF16)
        b1t = sbt("b1t", [128, 288], F32)
        b2c = sbt("b2c", [9, D], BF16)
        bp = sbt("bp", [1, D], BF16)
        gw = sbt("gw", [128, 8 * 8], BF16)
        gb_row = sbt("gb_row", [1, 8], F32)
        gb_b = sbt("gb_b", [128, 8], F32)
        onesf = sbt("onesf", [1, 128], F32)
        onesb = sbt("onesb", [128, 512], BF16)
        alibi = sbt("alibi", [128, 640], F32)
        masks = sbt("masks", [128, 4 * 512], BF16)
        ident = sbt("ident", [128, 128], F32)
        sel = sbt("sel", [9, 8 * 128], BF16)
        ident_bf = sbt("ident_bf", [128, 128], BF16)
        tmp1 = sbt("tmp1", [128, 512], F32)
        tmp2 = sbt("tmp2", [128, 512], F32)
        tmpd = [tmp1, tmp2]
        xsq = [t[:, :].bitcast(BF16)[:, 0:512] for t in tmpd]
        srow = sbt("srow", [1, 2 * T], F32)
        msq = sbt("msq", [1, T], F32)

        pb = [es.enter_context(nc.psum_tensor(f"pb{i}", [128, 512], F32)) for i in range(7)]
        pb7 = es.enter_context(nc.psum_tensor("pb7", [128, 1024], BF16))
        s = {k: es.enter_context(nc.semaphore("s_" + k))
             for k in ("pe", "act", "dve", "ldx", "ldc", "ldgc",
                       "wpA", "wpB", "w1s", "w2s", "st")}

        ops = {"pe": [], "act": [], "dve": [], "pool": [], "sync": []}
        n = {k: 0 for k in ("pe", "act", "dve", "ldx", "ldc", "ldgc",
                            "wpA", "wpB", "w1s", "w2s", "st")}
        _esem = {"pe": "pe", "act": "act", "dve": "dve"}

        def emit(eng, fn, waits=None, inc=True, dma=None):
            w = {k: v for k, v in (waits or {}).items() if v > 0}
            inc_sem = dma if dma else (_esem.get(eng) if inc else None)
            amt = 16 if dma else 1
            ops[eng].append((w, fn, inc_sem, amt))
            if inc_sem:
                n[inc_sem] += amt
                return n[inc_sem]
            return None

        for ch in range(2):
            emit("sync", lambda e, ch=ch: e.dma_start(
                out=resid[:].rearrange("p (c t) -> p c t", c=8)[:, 4 * ch:4 * ch + 4],
                in_=d_x[:, :].rearrange("(c p) t -> p c t", p=128)[:, 4 * ch:4 * ch + 4]),
                dma="ldx")
        LDS_X = n["ldx"]
        for fn in (lambda e: e.dma_start(out=alibi[:, :], in_=d_alibi[:, :]),
                   lambda e: e.dma_start(out=ident[:, :], in_=d_ident[:, :]),
                   lambda e: e.dma_start(out=b1t[:, :], in_=d_b1t[:, :]),
                   lambda e: e.dma_start(out=onesf[:, :], in_=d_ones[0:1, 0:128]),
                   lambda e: e.dma_start(out=gb_row[:, :], in_=d_gb[:, :])):
            emit("sync", fn, dma="ldc")
        LDS_CONST = n["ldc"]

        for fn in (lambda e: e.dma_start(out=onesb[:, :], in_=d_ones[:, :]),
                   lambda e: e.dma_start(
                       out=masks[:].rearrange("p (r j) -> p r j", r=4),
                       in_=d_masks[:, :, :].rearrange("r p j -> p r j")),
                   lambda e: e.dma_start(out=bp[:, :], in_=d_bp[:, :]),
                   lambda e: e.dma_start(out=b2c[:, :], in_=d_b2c[:, :]),
                   lambda e: e.dma_start(out=sel[:, :], in_=d_sel[:, :]),
                   lambda e: e.dma_start(out=ident_bf[:, :], in_=d_ident[:, :]),
                   lambda e: e.dma_start(
                       out=gw[:].rearrange("p (c e) -> p c e", c=8),
                       in_=d_gw[:, :].rearrange("(c p) e -> p c e", p=128))):
            emit("pool", fn, dma="ldgc")
        LDG_CONST = n["ldgc"]

        def emit_layernorm(sums_ps, bc_ps, msq, pre=None):
            pre = dict(pre or {})
            cast_done = []
            for c in range(8):
                w = dict(pre) if c == 0 else {}
                w["ldx"] = 16 * (c // 4 + 1)
                w["ldc"] = LDS_CONST
                w["ldgc"] = LDG_CONST
                cast_done.append(emit("act", lambda e, c=c: e.activation(
                    xln[:, c * T:(c + 1) * T], resid[:, c * T:(c + 1) * T], AF.Copy),
                    waits=w))
            sums_free = [0, 0]
            ev = [0, 0]
            for nt in range(4):
                def f_sum(e, nt=nt):
                    for c in range(8):
                        last = e.matmul(
                            sums_ps[0][:, :], onesb[:, 0:1],
                            xln[:, c * T + nt * 512:c * T + nt * 512 + 512],
                            start=(c == 0), stop=(c == 7))
                    return last
                pe_c = emit("pe", f_sum, waits={"act": cast_done[7],
                                                "dve": sums_free[0],
                                                "ldc": LDS_CONST,
                                                "ldgc": LDG_CONST})
                sums_free[0] = ev[0] = emit("dve", lambda e, nt=nt: e.tensor_copy(
                    srow[:, nt * 512:nt * 512 + 512], sums_ps[0][:, :]),
                    waits={"pe": pe_c})
            prev_mm = [0, 0]
            for nt in range(4):
                for c in range(8):
                    par = c % 2
                    sq_c = emit("act", lambda e, par=par, c=c, nt=nt: e.activation(
                        xsq[par][:, :], xln[:, c * T + nt * 512:c * T + nt * 512 + 512],
                        AF.Square), waits={"act": cast_done[c], "pe": prev_mm[par]})
                    pe_c = emit("pe", lambda e, par=par, c=c: e.matmul(
                        sums_ps[1][:, :], onesb[:, 0:1], xsq[par][:, :],
                        start=(c == 0), stop=(c == 7)),
                        waits={"act": sq_c,
                               "dve": sums_free[1] if c == 0 else 0})
                    prev_mm[par] = pe_c
                sums_free[1] = ev[1] = emit("dve", lambda e, nt=nt: e.tensor_copy(
                    srow[:, T + nt * 512:T + nt * 512 + 512], sums_ps[1][:, :]),
                    waits={"pe": pe_c})

            c0 = emit("dve", lambda e: e.tensor_scalar(
                srow[:, 0:T], srow[:, 0:T], 1.0 / 1024, None, ALU.mult),
                waits={"dve": max(ev)})
            c1 = emit("dve", lambda e: e.tensor_scalar(
                srow[:, T:2 * T], srow[:, T:2 * T], 1.0 / 1024, None, ALU.mult),
                waits={"dve": c0})
            c2 = emit("dve", lambda e: e.tensor_tensor(
                msq[:, :], srow[:, 0:T], srow[:, 0:T], ALU.mult), waits={"dve": c1})
            c3 = emit("dve", lambda e: e.tensor_tensor(
                srow[:, T:2 * T], srow[:, T:2 * T], msq[:, :], ALU.subtract),
                waits={"dve": c2})
            dve_st = emit("dve", lambda e: e.tensor_scalar(
                srow[:, T:2 * T], srow[:, T:2 * T], 1e-5, None, ALU.add),
                waits={"dve": c3})

            a0 = emit("act", lambda e: e.activation(
                srow[:, T:2 * T], srow[:, T:2 * T], AF.Ln), waits={"dve": dve_st})
            act_r = emit("act", lambda e: e.activation(
                srow[:, T:2 * T], srow[:, T:2 * T], AF.Exp, scale=-0.5),
                waits={"act": a0})

            norm_nt = [0, 0, 0, 0]
            bc_free = [0, 0]
            sub_done = [0, 0]
            for nt in range(4):
                bpar = nt % 2
                def f_bc(e, nt=nt, bpar=bpar):
                    e.matmul(bc_ps[2 * bpar][:, :], onesf[:, :],
                             srow[0:1, nt * 512:nt * 512 + 512],
                             start=True, stop=True)
                    return e.matmul(bc_ps[2 * bpar + 1][:, :], onesf[:, :],
                                    srow[0:1, T + nt * 512:T + nt * 512 + 512],
                                    start=True, stop=True)
                pe_c = emit("pe", f_bc, waits={"act": act_r, "dve": bc_free[bpar]})
                for c in range(8):
                    tpar = c % 2
                    s0 = emit("dve", lambda e, c=c, nt=nt, tpar=tpar, bpar=bpar:
                              e.tensor_tensor(
                                  tmpd[tpar][:, :],
                                  resid[:, c * T + nt * 512:c * T + nt * 512 + 512],
                                  bc_ps[2 * bpar][:, :], ALU.subtract),
                              waits={"pe": pe_c if c == 0 else 0,
                                     "dve": sub_done[tpar]})
                    sub_done[tpar] = emit("dve", lambda e, c=c, nt=nt, tpar=tpar, bpar=bpar:
                                          e.tensor_tensor(
                                              xln[:, c * T + nt * 512:c * T + nt * 512 + 512],
                                              tmpd[tpar][:, :], bc_ps[2 * bpar + 1][:, :],
                                              ALU.mult), waits={"dve": s0})
                bc_free[bpar] = norm_nt[nt] = max(sub_done)
            return norm_nt, (ev[0], ev[1])

        sums_ps = [pb[0][0:1, :], pb[1][0:1, :]]
        bc_ps = [pb[2], pb[3], pb[4], pb[5]]
        ln1_nt, ln1_sums_last = emit_layernorm(sums_ps, bc_ps, msq)
        ln1_done = max(ln1_nt)

        if phases >= 2:
            aes = contextlib.ExitStack()
            asb = lambda name, shp, dt: aes.enter_context(nc.sbuf_tensor("sb_" + name, shp, dt))
            wq_s = [asb(f"wq_s{i}", [128, 1024], BF16) for i in range(2)]
            wk_s = [asb(f"wk_s{i}", [128, 1024], BF16) for i in range(2)]
            wv_s = [asb(f"wv_s{i}", [128, 1024], BF16) for i in range(2)]
            wp0 = [asb(f"wp0_{i}", [64, 1024], BF16) for i in range(2)]
            wp1 = [asb(f"wp1_{i}", [64, 1024], BF16) for i in range(2)]
            Qp2 = [asb(f"Qp{i}", [128, T], BF16) for i in range(2)]
            Kp2 = [asb(f"Kp{i}", [128, T], BF16) for i in range(2)]
            Vp2 = [asb(f"Vp{i}", [128, 16 * 130], BF16) for i in range(2)]
            Vf = asb("Vf", [128, T], BF16)
            NP = 6
            Ppool = asb("Ppool", [128, NP * 512], BF16)
            ctx0 = asb("ctx0", [64, T], BF16)
            ctx1 = asb("ctx1", [64, T], BF16)
            lrow = asb("lrow", [1, 512], F32)
            rrow_h = [asb(f"rrow{i}", [1, 512], BF16) for i in range(2)]
            rbb1 = asb("rbb", [64, 512], BF16)
            rbb_h = [rbb1, rbb1]
            sc_ps = [pb[2], pb[3], pb[4]]
            qk_ps = [pb[0], pb[1]]
            ctx_ps = [pb[5], pb[6]]

            def f_vones(e):
                V4a = Vp2[0][:].rearrange("p (kt h d) -> p kt h d", kt=16, h=2)
                V4b = Vp2[1][:].rearrange("p (kt h d) -> p kt h d", kt=16, h=2)
                e.memset(V4a[:, :, 0:1, 64:65], 1.0)
                e.memset(V4a[:, :, 1:2, 64:65], 1.0)
                e.memset(V4b[:, :, 0:1, 64:65], 1.0)
                return e.memset(V4b[:, :, 1:2, 64:65], 1.0)
            vones = emit("dve", f_vones)

            wbuf_free = [0, 0]
            wload = [0] * 8

            def emit_pair_load(hp):
                par = hp % 2
                sem = "wpA" if par == 0 else "wpB"
                emit("pool", lambda e, hp=hp, par=par: e.dma_start(
                    out=wq_s[par][:].rearrange("p (c m) -> p c m", c=8),
                    in_=d_wq[:, hp * 128:(hp + 1) * 128].rearrange("(c p) m -> p c m", p=128)),
                    waits={"pe": wbuf_free[par]}, dma=sem)
                emit("pool", lambda e, hp=hp, par=par: e.dma_start(
                    out=wk_s[par][:].rearrange("p (c m) -> p c m", c=8),
                    in_=d_wk[:, hp * 128:(hp + 1) * 128].rearrange("(c p) m -> p c m", p=128)), dma=sem)
                emit("pool", lambda e, hp=hp, par=par: e.dma_start(
                    out=wv_s[par][:].rearrange("p (c m) -> p c m", c=8),
                    in_=d_wv[:, hp * 128:(hp + 1) * 128].rearrange("(c p) m -> p c m", p=128)), dma=sem)
                emit("pool", lambda e, hp=hp, par=par: e.dma_start(
                    out=wp0[par][:, :], in_=d_wp[hp * 128:hp * 128 + 64, :]), dma=sem)
                wload[hp] = emit("pool", lambda e, hp=hp, par=par: e.dma_start(
                    out=wp1[par][:, :], in_=d_wp[hp * 128 + 64:(hp + 1) * 128, :]), dma=sem)

            emit_pair_load(0)
            emit_pair_load(1)

            pj_free = [0]
            proj_done = [0] * 9
            vf_done = [0, 0, 0, 0]

            def make_proj_steps(hp):
                par = hp % 2
                sem = "wpA" if par == 0 else "wpB"
                steps = []

                def qk_step(wslc, dst, ntl, first, qpar):
                    def go():
                        def f(e, wslc=wslc, ntl=ntl, par=par, qpar=qpar):
                            for k in range(8):
                                last = e.matmul(
                                    qk_ps[qpar][:, :],
                                    wslc[par][:, k * 128:(k + 1) * 128],
                                    xln[:, k * T + ntl * 512:k * T + ntl * 512 + 512],
                                    start=(k == 0), stop=(k == 7))
                            return last
                        w = {"dve": max(qk_free[qpar], ln1_nt[ntl])}
                        if first:
                            w[sem] = wload[hp]
                        pe_c = emit("pe", f, waits=w)
                        qk_free[qpar] = emit("dve", lambda e, dst=dst, ntl=ntl, qpar=qpar:
                                             e.tensor_copy(
                                                 dst[:, ntl * 512:ntl * 512 + 512],
                                                 qk_ps[qpar][:, :]),
                                             waits={"pe": pe_c})
                    return go

                sidx = [0]
                for which, (wslc, dstl) in enumerate([(wq_s, Qp2), (wk_s, Kp2)]):
                    for ntl in range(4):
                        steps.append(qk_step(wslc, dstl[par], ntl,
                                             which == 0 and ntl == 0, sidx[0] % 2))
                        sidx[0] += 1

                def vf_step(ntl, qpar):
                    def go():
                        def f(e, ntl=ntl, par=par, qpar=qpar):
                            for k in range(8):
                                last = e.matmul(
                                    qk_ps[qpar][:, :],
                                    wv_s[par][:, k * 128:(k + 1) * 128],
                                    xln[:, k * T + ntl * 512:k * T + ntl * 512 + 512],
                                    start=(k == 0), stop=(k == 7))
                            return last
                        pe_c = emit("pe", f, waits={"dve": max(qk_free[qpar], ln1_nt[ntl])})
                        qk_free[qpar] = vf_done[ntl] = emit(
                            "dve", lambda e, ntl=ntl, qpar=qpar: e.tensor_copy(
                                Vf[:, ntl * 512:ntl * 512 + 512], qk_ps[qpar][:, :]),
                            waits={"pe": pe_c})
                    return go

                for ntl in range(4):
                    steps.append(vf_step(ntl, sidx[0] % 2))
                    sidx[0] += 1

                def vtr_step(kt, last_step):
                    def go():
                        pe_c = emit("pe", lambda e, kt=kt: e.transpose(
                            pb7[:, 0:128], Vf[:, kt * 128:(kt + 1) * 128], ident_bf[:, :]),
                            waits={"dve": max(vf_done[kt // 4], pj_free[0]),
                                   "ldc": LDS_CONST})
                        c = emit("dve", lambda e, kt=kt, par=par: e.tensor_copy(
                            Vp2[par][:, kt * 130:(kt + 1) * 130].rearrange(
                                "p (h d) -> p h d", h=2)[:, :, 0:64],
                            pb7[:, 0:128].rearrange("p (h d) -> p h d", h=2)),
                            waits={"pe": pe_c,
                                   "dve": vones if hp == 0 and kt == 0 else 0})
                        pj_free[0] = c
                        if last_step:
                            proj_done[hp] = c
                    return go

                for kt in range(16):
                    steps.append(vtr_step(kt, kt == 15))
                return steps

            qk_free = list(ln1_sums_last)
            pending_norm = []
            last_fevac = [0]
            sc_rr = [0]
            rb_pe = [0]
            pslot_free = [0] * 6
            pslot_rr = [0]
            sc_free = [0, 0, 0]
            ctx_free = [ln1_nt[3], 0]
            for st_ in make_proj_steps(0):
                st_()
            for hp in range(npair):
                par = hp % 2
                next_steps = make_proj_steps(hp + 1) if hp + 1 < npair else []

                band_fevac = {}
                proj_pending = []
                pq_rr = [0]

                def emit_proj_tile(dout, tt):
                    qpar = pq_rr[0] % 2
                    pq_rr[0] += 1

                    def fp(e, hp=hp, dout=dout, tt=tt, qpar=qpar, par=par):
                        first = True
                        if hp == 0:
                            e.matmul(qk_ps[qpar][:, :],
                                     bp[:, dout * 128:(dout + 1) * 128],
                                     onesb[0:1, :], start=True, stop=False)
                            first = False
                        e.matmul(qk_ps[qpar][:, :],
                                 wp0[par][:, dout * 128:(dout + 1) * 128],
                                 ctx0[:, tt * 512:tt * 512 + 512],
                                 start=first, stop=False)
                        return e.matmul(qk_ps[qpar][:, :],
                                        wp1[par][:, dout * 128:(dout + 1) * 128],
                                        ctx1[:, tt * 512:tt * 512 + 512],
                                        start=False, stop=True)
                    pe_c = emit("pe", fp,
                                waits={"dve": max(band_fevac[tt], qk_free[qpar])})
                    qk_free[qpar] = emit("dve", lambda e, dout=dout, tt=tt, qpar=qpar:
                                         e.tensor_tensor(
                                             resid[:, dout * T + tt * 512:dout * T + tt * 512 + 512],
                                             resid[:, dout * T + tt * 512:dout * T + tt * 512 + 512],
                                             qk_ps[qpar][:, :],
                                             ALU.add),
                                         waits={"pe": pe_c})

                def flush_norms():
                    while pending_norm:
                        hh_, b_, dve_c_ = pending_norm.pop(0)
                        pe_c = emit("pe", lambda e, hh_=hh_: e.matmul(
                            qk_ps[0][0:64, :], onesb[0:1, 0:64], rrow_h[hh_][:, :],
                            start=True, stop=True),
                            waits={"dve": max(dve_c_, qk_free[0])})
                        rb_pe[0] = pe_c
                        e0 = emit("dve", lambda e, hh_=hh_: e.tensor_copy(
                            rbb_h[hh_][:, :], qk_ps[0][0:64, :]),
                            waits={"pe": pe_c, "dve": last_fevac[0]})
                        qk_free[0] = e0
                        def fevac(e, hh_=hh_, b_=b_):
                            ctx = ctx0 if hh_ == 0 else ctx1
                            return e.tensor_tensor(ctx[:, b_ * 512:(b_ + 1) * 512],
                                                   ctx_ps[hh_][0:64, :],
                                                   rbb_h[hh_][:, :], ALU.mult)
                        ctx_free[hh_] = last_fevac[0] = emit("dve", fevac, waits={"dve": e0})
                        band_fevac[b_] = max(band_fevac.get(b_, 0), ctx_free[hh_])

                for b in range(NB):
                    nkt = 4 * (b + 1)
                    LKA = 3
                    exp_at = {}
                    mask_at = {}
                    slot_of = {}
                    pv_q = []
                    pv_last = [0, 0]

                    def emit_pv(hh, kt):
                        slot = slot_of[(hh, kt)]

                        def fpv(e, hh=hh, kt=kt, nkt=nkt, slot=slot, par=par):
                            return e.matmul(
                                ctx_ps[hh][0:65, :],
                                Vp2[par][:, kt * 130 + hh * 65:kt * 130 + hh * 65 + 65],
                                Ppool[:, slot * 512:(slot + 1) * 512],
                                start=(kt == 0), stop=(kt == nkt - 1))
                        w = {}
                        if (hh, kt) in mask_at:
                            w["dve"] = mask_at[(hh, kt)]
                        else:
                            w["act"] = exp_at[(hh, kt)]
                        if kt == 0:
                            w["dve"] = max(w.get("dve", 0), ctx_free[hh])
                        c = emit("pe", fpv, waits=w)
                        pslot_free[slot] = c
                        pv_last[hh] = c
                        return c

                    for kt in range(nkt):
                        while len(pv_q) > 2 * LKA - 2:
                            emit_pv(*pv_q.pop(0))
                        for hh in range(2):
                            spar = sc_rr[0] % 3
                            sc_rr[0] += 1
                            pe_c = emit("pe", lambda e, hh=hh, b=b, kt=kt, spar=spar, par=par:
                                        e.matmul(
                                            sc_ps[spar][:, :],
                                            Kp2[par][hh * 64:(hh + 1) * 64, kt * 128:(kt + 1) * 128],
                                            Qp2[par][hh * 64:(hh + 1) * 64, b * 512:(b + 1) * 512],
                                            start=True, stop=True),
                                        waits={"act": sc_free[spar],
                                               "dve": proj_done[hp] if (kt == 0 and hh == 0 and b == 0) else 0})
                            slot = pslot_rr[0] % NP
                            pslot_rr[0] += 1
                            slot_of[(hh, kt)] = slot
                            col = (2 * hp + hh) * 40 + [0, 4, 12, 24][b] + kt
                            act_c = emit("act", lambda e, kt=kt, spar=spar, col=col, slot=slot:
                                         e.activation(Ppool[:, slot * 512:(slot + 1) * 512],
                                                      sc_ps[spar][:, :], AF.Exp,
                                                      bias=alibi[:, col:col + 1], scale=0.125),
                                         waits={"pe": max(pe_c, pslot_free[slot])})
                            sc_free[spar] = act_c
                            exp_at[(hh, kt)] = act_c
                            r = kt - 4 * b
                            if r >= 0:
                                mask_at[(hh, kt)] = emit("dve", lambda e, kt=kt, r=r, slot=slot:
                                                         e.tensor_tensor(
                                                             Ppool[:, slot * 512:(slot + 1) * 512],
                                                             Ppool[:, slot * 512:(slot + 1) * 512],
                                                             masks[:, r * 512:(r + 1) * 512], ALU.mult),
                                                         waits={"act": act_c})
                            pv_q.append((hh, kt))
                        if next_steps:
                            next_steps.pop(0)()
                        if kt == 1:
                            flush_norms()
                            if b >= 1:
                                proj_pending.extend((dout, b - 1) for dout in range(8))
                        if proj_pending and kt >= 2:
                            emit_proj_tile(*proj_pending.pop(0))
                    while pv_q:
                        emit_pv(*pv_q.pop(0))

                    prev_tail = 0
                    for hh in range(2):
                        l0 = emit("dve", lambda e, hh=hh: e.reciprocal(
                            lrow[:, :], ctx_ps[hh][64:65, :]),
                            waits={"pe": max(pv_last[hh], rb_pe[0]),
                                   "dve": prev_tail})
                        dve_c = emit("dve", lambda e, hh=hh: e.tensor_copy(
                            rrow_h[hh][:, :], lrow[:, :]), waits={"dve": l0})
                        prev_tail = dve_c
                        pending_norm.append((hh, b, dve_c))
                flush_norms()
                proj_pending.extend((dout, 3) for dout in range(8))
                while proj_pending:
                    emit_proj_tile(*proj_pending.pop(0))
                while next_steps:
                    next_steps.pop(0)()
                wbuf_free[par] = n["pe"]
                if hp + 2 < npair:
                    emit_pair_load(hp + 2)
            attn_done = max(qk_free)
            if debug:
                for nm, buf in [("xln", xln), ("Qp", Qp2[0]), ("Kp", Kp2[0]), ("Vp", Vp2[0]),
                                ("ctx0", ctx0), ("ctx1", ctx1), ("Pp", Ppool),
                                ("lr", lrow), ("rb", rbb_h[0]),
                                ("wqs", wq_s[0]), ("wks", wk_s[0]), ("wvs", wv_s[0])]:
                    emit("pool", lambda e, nm=nm, buf=buf: e.dma_start(
                        out=d_dbg[nm][:, :], in_=buf[:, :]),
                        waits={"dve": n["dve"], "pe": n["pe"], "act": n["act"]}, dma="st")
            aes.close()
        else:
            attn_done = ln1_done


        if phases >= 3:
            sums_ps = [pb[0][0:1, :], pb[1][0:1, :]]
            bc_ps = [pb[2], pb[3], pb[4], pb[5]]
            ln2_nt, ln2_sums_last = emit_layernorm(sums_ps, bc_ps, msq, pre={"dve": attn_done})
            ln2_done = max(ln2_nt)
        else:
            ln2_done = attn_done


        if phases >= 4:
            fes = contextlib.ExitStack()
            fsb = lambda name, shp, dt: fes.enter_context(nc.sbuf_tensor("sb_" + name, shp, dt))
            w1g = fsb("w1g", [128, G * 1024], BF16)
            w2g = fsb("w2g", [128, G * 1024], BF16)
            hsb2 = fsb("hsb2", [128, 16 * 512], BF16)
            comb_fmb = fsb("comb_fmb", [16, T], BF16)
            comb_tm = fsb("comb_tm", [128, 16 * 9], BF16)
            comb_b = [fsb(f"comb_b{i}", [128, T], BF16) for i in range(2)]
            lg = fsb("lg", [128, 8], F32)
            g_m1 = fsb("g_m1", [128, 1], F32)
            g_m2 = fsb("g_m2", [128, 1], F32)
            g_eq1 = fsb("g_eq1", [128, 8], F32)
            g_eq2 = fsb("g_eq2", [128, 8], F32)
            g_msk = fsb("g_msk", [128, 8], F32)
            g_d = fsb("g_d", [128, 1], F32)
            g_e = fsb("g_e", [128, 1], F32)
            g_r = fsb("g_r", [128, 1], F32)
            g_gw2 = fsb("g_gw2", [128, 1], F32)
            g_c1 = fsb("g_c1", [128, 8], F32)
            g_c2 = fsb("g_c2", [128, 8], F32)
            g_ps = pb[6]
            tr_ps = pb7
            o_ps = [pb[2], pb[3], pb[4], pb[5]]
            h_ps = [pb[0], pb[1]]

            ctm_ones = emit("dve", lambda e: e.memset(
                comb_tm[:].rearrange("p (t e) -> p t e", e=9)[:, :, 8:9], 1.0),
                waits={"dve": ln2_done})
            pe_c = emit("pe", lambda e: e.matmul(g_ps[:, 0:8], onesf[:, :], gb_row[:, :],
                                                 start=True, stop=True),
                        waits={"ldc": LDS_CONST, "dve": ln2_done})
            gbb_done = emit("dve", lambda e: e.tensor_copy(gb_b[:, :], g_ps[:, 0:8]),
                            waits={"pe": pe_c})

            gate_st = {"g_ps_free": gbb_done, "tr_free": 0, "tr_evac": 0,
                       "comb_cast": 0, "steps": list(range(16))}

            def gate_tile(ttk):
                def f_lg(e, ttk=ttk):
                    for k in range(8):
                        last = e.matmul(g_ps[:, 0:8],
                                        xln[:, k * T + ttk * 128:k * T + (ttk + 1) * 128],
                                        gw[:, k * 8:(k + 1) * 8],
                                        start=(k == 0), stop=(k == 7))
                    return last
                pe_c = emit("pe", f_lg, waits={"dve": max(gate_st["g_ps_free"], ln2_nt[ttk // 4])})
                q0 = emit("dve", lambda e: e.tensor_tensor(
                    lg[:, :], g_ps[:, 0:8], gb_b[:, :], ALU.add), waits={"pe": pe_c})
                q1 = emit("dve", lambda e: e.tensor_reduce(
                    g_m1[:, :], lg[:, :], AX.X, ALU.max), waits={"dve": q0})
                q2 = emit("dve", lambda e: e.tensor_scalar(
                    g_eq1[:, :], lg[:, :], g_m1[:, 0:1], None, ALU.is_equal), waits={"dve": q1})
                q3 = emit("dve", lambda e: e.scalar_tensor_tensor(
                    g_msk[:, :], g_eq1[:, :], -1e30, lg[:, :], ALU.mult, ALU.add),
                    waits={"dve": q2})
                q4 = emit("dve", lambda e: e.tensor_reduce(
                    g_m2[:, :], g_msk[:, :], AX.X, ALU.max), waits={"dve": q3})
                q5 = emit("dve", lambda e: e.tensor_scalar(
                    g_eq2[:, :], g_msk[:, :], g_m2[:, 0:1], None, ALU.is_equal), waits={"dve": q4})
                d1 = emit("dve", lambda e: e.tensor_tensor(
                    g_d[:, :], g_m2[:, :], g_m1[:, :], ALU.subtract), waits={"dve": q5})
                a1 = emit("act", lambda e: e.activation(g_e[:, :], g_d[:, :], AF.Exp),
                          waits={"dve": d1})
                p0 = emit("dve", lambda e: e.tensor_scalar(
                    g_gw2[:, :], g_e[:, :], 1.0, None, ALU.add), waits={"act": a1})
                p1 = emit("dve", lambda e: e.reciprocal(
                    g_r[:, :], g_gw2[:, :]), waits={"dve": p0})
                p2 = emit("dve", lambda e: e.tensor_tensor(
                    g_gw2[:, :], g_e[:, :], g_r[:, :], ALU.mult), waits={"dve": p1})
                p3 = emit("dve", lambda e: e.tensor_scalar(
                    g_c1[:, :], g_eq1[:, :], g_r[:, 0:1], None, ALU.mult), waits={"dve": p2})
                p4 = emit("dve", lambda e: e.tensor_scalar(
                    g_c2[:, :], g_eq2[:, :], g_gw2[:, 0:1], None, ALU.mult), waits={"dve": p3})
                g_ps_free = emit("dve", lambda e, ttk=ttk: e.tensor_tensor(
                    comb_tm[:, ttk * 9:ttk * 9 + 8], g_c1[:, :], g_c2[:, :], ALU.add),
                    waits={"dve": p4})
                gate_st["g_ps_free"] = g_ps_free
                pe_c = emit("pe", lambda e, ttk=ttk: e.transpose(
                    tr_ps[0:9, 0:128], comb_tm[:, ttk * 9:ttk * 9 + 9], ident_bf[:, :]),
                    waits={"dve": max(g_ps_free, gate_st["tr_free"], ctm_ones),
                           "ldc": LDS_CONST})
                tr_evac = emit("dve", lambda e, ttk=ttk: e.tensor_copy(
                    comb_fmb[0:9, ttk * 128:(ttk + 1) * 128], tr_ps[0:9, 0:128]),
                    waits={"pe": pe_c})
                gate_st["tr_free"] = tr_evac
                gate_st["tr_evac"] = tr_evac
                gate_st["comb_cast"] = tr_evac

            gate_tick = [0]

            def maybe_gate():
                if gate_st["steps"]:
                    gate_tile(gate_st["steps"].pop(0))

            def drain_gate():
                while gate_st["steps"]:
                    maybe_gate()

            ffn_last = 0
            if phases >= 5:
                w1_ld = [0] * NEG
                w2_ld = [0] * NEG
                A_done = [0] * NEG
                C_done = [0] * NEG
                h_free = [0, 0]
                o_free = [ln2_nt[2], ln2_nt[2], ln2_nt[3], ln2_nt[3]]
                cb_done = [0, 0]
                cb_used = [0, 0]
                ffn_last = 0

                def emit_w_load(i):
                    o, g = divmod(i, NG)
                    e_idx = EXPERT_ORDER[o]
                    if e_idx == 8:
                        src1 = d_ws1[:, g * 1024:(g + 1) * 1024]
                        src2 = d_ws2[g * 1024:(g + 1) * 1024, :]
                    else:
                        src1 = d_w1[e_idx, :, g * 1024:(g + 1) * 1024]
                        src2 = d_w2[e_idx, g * 1024:(g + 1) * 1024, :]
                    w1_ld[i] = emit("pool", lambda e, src1=src1: e.dma_start(
                        out=w1g[:].rearrange("p (c m) -> p c m", c=8),
                        in_=src1.rearrange("(c p) m -> p c m", p=128)),
                        waits={"pe": A_done[i - 1]} if i else {"dve": attn_done}, dma="w1s")
                    w2_ld[i] = emit("pool", lambda e, src2=src2: e.dma_start(
                        out=w2g[:].rearrange("p (m d) -> p m d", m=8),
                        in_=src2.rearrange("(m p) d -> p m d", p=128)),
                        waits={"pe": C_done[i - 1]} if i else {"dve": attn_done}, dma="w2s")

                def emit_comb_b(e_idx):
                    par = e_idx % 2
                    evac = 0
                    for ntl in range(4):
                        bank = ntl % 2
                        pe_c = emit("pe", lambda e, e_idx=e_idx, ntl=ntl, bank=bank: e.matmul(
                            o_ps[bank][:, :], sel[0:9, e_idx * 128:(e_idx + 1) * 128],
                            comb_fmb[0:9, ntl * 512:(ntl + 1) * 512],
                            start=True, stop=True),
                            waits={"dve": max(gate_st["comb_cast"], o_free[bank],
                                              cb_used[par] if ntl == 0 else 0)})
                        evac = emit("dve", lambda e, par=par, ntl=ntl, bank=bank: e.tensor_copy(
                            comb_b[par][:, ntl * 512:(ntl + 1) * 512], o_ps[bank][:, :]),
                            waits={"pe": pe_c})
                        o_free[bank] = evac
                    cb_done[par] = evac

                emit_w_load(0)

                prev_B_pe = 0
                for o, e_idx in enumerate(EXPERT_ORDER):
                    if o + 1 < len(EXPERT_ORDER):
                        emit_comb_b(EXPERT_ORDER[o + 1])
                    par_e = e_idx % 2
                    for g in range(NG):
                        i = o * NG + g
                        if i >= ffn_groups:
                            continue
                        last_A = 0
                        last_B = 0
                        for half in range(2):
                            h_ready = []
                            for tl in range(2):
                                t = 2 * half + tl
                                for m in range(G):
                                    hpar = (t * G + m) % 2
                                    slot = tl * G + m

                                    def fA(e, t=t, m=m, hpar=hpar):
                                        for k in range(8):
                                            last = e.matmul(
                                                h_ps[hpar][:, :],
                                                w1g[:, k * 1024 + m * 128:k * 1024 + (m + 1) * 128],
                                                xln[:, k * T + t * 512:k * T + t * 512 + 512],
                                                start=(k == 0), stop=(k == 7))
                                        return last
                                    w = {"act": h_free[hpar]}
                                    if half == 0 and tl == 0 and m == 0:
                                        w["w1s"] = w1_ld[i]
                                        if i == 0:
                                            w["dve"] = ln2_done
                                    last_A = pe_c = emit("pe", fA, waits=w)
                                    if o == 0:
                                        gate_tick[0] += 1
                                        if gate_tick[0] % 6 == 0:
                                            maybe_gate()

                                    if ffn_parts < 2:
                                        h_ready.append(("act", 0))
                                        continue
                                    bcol = e_idx * 32 + g * 8 + m
                                    wg = {"pe": max(pe_c,
                                                    prev_B_pe if (tl == 0 and m == 0) else 0)}
                                    act_c = emit("act", lambda e, slot=slot, hpar=hpar, bcol=bcol:
                                                 e.activation(hsb2[:, slot * 512:(slot + 1) * 512],
                                                              h_ps[hpar][:, :], AF.Gelu,
                                                              bias=b1t[:, bcol:bcol + 1]),
                                                 waits=wg)
                                    h_free[hpar] = act_c
                                    if e_idx != 8:
                                        dve_c = emit("dve", lambda e, slot=slot, t=t, par_e=par_e:
                                                     e.tensor_tensor(
                                                         hsb2[:, slot * 512:(slot + 1) * 512],
                                                         hsb2[:, slot * 512:(slot + 1) * 512],
                                                         comb_b[par_e][:, t * 512:(t + 1) * 512],
                                                         ALU.mult),
                                                     waits={"act": act_c,
                                                            "dve": cb_done[par_e] if (half == 0 and tl == 0 and m == 0) else 0})
                                        cb_used[par_e] = dve_c
                                        h_ready.append(("dve", dve_c))
                                    else:
                                        h_ready.append(("act", act_c))
                            if ffn_parts >= 3:
                                h_act = max((c for k, c in h_ready if k == "act"), default=0)
                                h_dve = max((c for k, c in h_ready if k == "dve"), default=0)
                            for d in (range(8) if ffn_parts >= 3 else []):
                                dpar = d % 2
                                inject_b2 = (e_idx == 7 and g == NG - 1)

                                def fB(e, d=d, half=half, dpar=dpar, inject_b2=inject_b2):
                                    for m in range(G):
                                        for tl in range(2):
                                            last = e.matmul(
                                                o_ps[2 * dpar + tl][:, :],
                                                w2g[:, m * 1024 + d * 128:m * 1024 + (d + 1) * 128],
                                                hsb2[:, (tl * G + m) * 512:(tl * G + m + 1) * 512],
                                                start=(m == 0),
                                                stop=(m == G - 1 and not inject_b2))
                                    if inject_b2:
                                        for tl in range(2):
                                            t = 2 * half + tl
                                            last = e.matmul(o_ps[2 * dpar + tl][:, :],
                                                            b2c[0:9, d * 128:(d + 1) * 128],
                                                            comb_fmb[0:9, t * 512:(t + 1) * 512],
                                                            start=False, stop=True)
                                    return last
                                w = {"act": h_act if d == 0 else 0,
                                     "dve": max(h_dve if d == 0 else 0,
                                                o_free[2 * dpar], o_free[2 * dpar + 1])}
                                if half == 0 and d == 0:
                                    w["w2s"] = w2_ld[i]
                                last_B = pe_c = emit("pe", fB, waits=w)

                                if ffn_parts >= 4:
                                    for tl in range(2):
                                        t = 2 * half + tl
                                        bank = 2 * dpar + tl
                                        fn_ev = (lambda e, d=d, t=t, bank=bank:
                                                 e.tensor_tensor(resid[:, d * T + t * 512:d * T + t * 512 + 512],
                                                                 resid[:, d * T + t * 512:d * T + t * 512 + 512],
                                                                 o_ps[bank][:, :],
                                                                 ALU.add))
                                        o_free[bank] = ffn_last = emit(
                                            "dve", fn_ev, waits={"pe": pe_c})
                            prev_B_pe = last_B
                        A_done[i] = last_A
                        C_done[i] = last_B
                        if i + 1 < min(NEG, ffn_groups):
                            emit_w_load(i + 1)
                        if g == NG - 1:
                            drain_gate()
                            if o + 1 < len(EXPERT_ORDER):
                                emit_comb_b(EXPERT_ORDER[o + 1])
            fes.close()

        if debug and phases >= 4:
            for nm, buf in [("cfmb", comb_fmb), ("gbb", gb_b), ("ctm", comb_tm),
                            ("g_lg", lg), ("g_m1", g_m1), ("g_m2", g_m2), ("g_msk", g_msk),
                            ("g_eq1", g_eq1), ("g_eq2", g_eq2), ("g_d", g_d), ("g_e", g_e),
                            ("g_r", g_r), ("g_gw2", g_gw2), ("g_c1", g_c1), ("g_c2", g_c2)]:
                emit("pool", lambda e, nm=nm, buf=buf: e.dma_start(
                    out=d_dbg[nm][:, :], in_=buf[:, :]),
                    waits={"dve": n["dve"], "pe": n["pe"], "act": n["act"]}, dma="st")
        emit("sync", lambda e: e.dma_start(
            out=d_out[:, :].rearrange("(c p) t -> p c t", p=128),
            in_=resid[:].rearrange("p (c t) -> p c t", c=8)),
            waits={"dve": n["dve"], "act": n["act"], "pe": n["pe"]}, dma="st")
        emit("sync", lambda e: None, waits={"st": n["st"]}, inc=False)

        _check_schedule(ops, n)

        if True:
            _dispatch(nc, ops, s)
    return nc


def _check_schedule(ops, totals):
    sems = {k: 0 for k in totals}
    ptr = {k: 0 for k in ops}
    for k, v in totals.items():
        pass
    progressed = True
    while progressed:
        progressed = False
        for eng, lst in ops.items():
            while ptr[eng] < len(lst):
                w, fn, inc_sem, amt = lst[ptr[eng]]
                if all(sems[sk] >= sv for sk, sv in w.items()):
                    if inc_sem:
                        sems[inc_sem] += amt
                    ptr[eng] += 1
                    progressed = True
                else:
                    break
    stuck = {e: (ptr[e], len(l)) for e, l in ops.items() if ptr[e] < len(l)}
    if stuck:
        det = {}
        for eng, (p, ln) in stuck.items():
            det[eng] = (p, ln, ops[eng][p][0], dict(sems))
        raise RuntimeError(f"schedule deadlock: {det}")


def _dispatch(nc, ops, s):
    with nc.Block() as block:
        def body(eng_name):
            def run(engine):
                for w, fn, inc_sem, amt in ops[eng_name]:
                    for sk, sv in w.items():
                        engine.wait_ge(s[sk], sv)
                    last = fn(engine)
                    if inc_sem is not None and last is not None:
                        last.then_inc(s[inc_sem], amt)
            return run
        block.tensor(body("pe"))
        block.scalar(body("act"))
        block.vector(body("dve"))
        block.gpsimd(body("pool"))
        block.sync(body("sync"))


_NC_CACHE = []


def kernel(**inputs):
    x = np.ascontiguousarray(np.asarray(inputs["L0_attn_x"], dtype=np.float32))
    f32 = lambda k: np.ascontiguousarray(np.asarray(inputs[k], dtype=np.float32))
    wq, wk, wv, wp = f32("L0_attn_W_query"), f32("L0_attn_W_key"), f32("L0_attn_W_value"), f32("L0_attn_W_proj")
    bpv = f32("L0_attn_b_proj").reshape(1, D)
    gwv = f32("L0_ffn_gate_w")
    gbv = f32("L0_ffn_gate_b").reshape(1, 8)
    w1 = f32("L0_ffn_w1")
    b1 = f32("L0_ffn_b1")
    w2 = f32("L0_ffn_w2")
    b2 = f32("L0_ffn_b2")
    ws1 = f32("L0_ffn_ws1")
    bs1 = f32("L0_ffn_bs1")
    ws2 = f32("L0_ffn_ws2")
    bs2 = f32("L0_ffn_bs2")

    b1t = np.ascontiguousarray(
        np.concatenate([b1.reshape(8, 32, 128), bs1.reshape(1, 32, 128)], axis=0)
        .transpose(2, 0, 1).reshape(128, 288))
    b2c = np.ascontiguousarray(np.concatenate([b2, bs2.reshape(1, D)], axis=0))
    alibi, masksv, ones, identv, selv = _host_tables()

    if not _NC_CACHE:
        _NC_CACHE.append(build_program(phases=5))
    nc = _NC_CACHE[0]

    base = {"wq": wq, "wk": wk, "wv": wv, "wp": wp, "bp": bpv,
            "gw": gwv, "gb": gbv, "w1": w1, "w2": w2, "ws1": ws1, "ws2": ws2,
            "b1t": b1t, "b2c": b2c, "alibi": alibi, "masksd": masksv,
            "onesd": ones, "identd": identv, "seld": selv}
    in_maps = [dict(base, x=np.ascontiguousarray(x[c].T)) for c in range(8)]
    res = run_bass_kernel_spmd(nc, in_maps, core_ids=list(range(8)))
    out = np.stack([np.asarray(r["out"]).T for r in res.results], axis=0)
    return np.ascontiguousarray(out.astype(np.float32))


if __name__ == "__main__":
    import time
    t0 = time.time()
    nc = build_program()
    print(f"build ok {time.time()-t0:.1f}s n_inst={nc.n_instructions()}")


# version 15
# speedup vs baseline: 1.1076x; 1.0221x over previous
import contextlib

import numpy as np
import ml_dtypes

import concourse.bass as bass
import concourse.mybir as mybir
from concourse.bass_utils import run_bass_kernel_spmd

F32 = mybir.dt.float32
BF16 = mybir.dt.bfloat16
AF = mybir.ActivationFunctionType
ALU = mybir.AluOpType
AX = mybir.AxisListType

T = 2048
D = 1024
H = 16
NB = 4
G = 8
NG = 32 // G
EXPERT_ORDER = [8, 0, 1, 2, 3, 4, 5, 6, 7]
NEG = len(EXPERT_ORDER) * NG


def _host_tables():
    slopes = 2.0 ** (-8.0 * np.arange(1, H + 1) / H)
    boff = [0, 4, 12, 24]
    alibi = np.zeros((128, 640), np.float32)
    for h in range(H):
        for b in range(NB):
            for kt in range(4 * (b + 1)):
                k = 128 * kt + np.arange(128)
                alibi[:, h * 40 + boff[b] + kt] = slopes[h] * (k - (512 * (b + 1) - 1)) / 8.0
    masks = np.zeros((4, 128, 512), np.float32)
    p = np.arange(128)[:, None]
    j = np.arange(512)[None, :]
    for r in range(4):
        masks[r] = (j >= p + 128 * r).astype(np.float32)
    ones = np.ones((128, 512), np.float32)
    ident = np.eye(128, dtype=np.float32)
    sel = np.zeros((9, 8 * 128), np.float32)
    for e in range(8):
        sel[e, e * 128:(e + 1) * 128] = 1.0
    return alibi, masks, ones, ident, sel


def build_program(phases=5, npair=8, debug=False, ffn_groups=NEG, ffn_parts=4):
    nc = bass.Bass("TRN2", target_bir_lowering=False)

    d_x = nc.dram_tensor("x", [D, T], F32, kind="ExternalInput")
    d_wq = nc.dram_tensor("wq", [D, D], F32, kind="ExternalInput")
    d_wk = nc.dram_tensor("wk", [D, D], F32, kind="ExternalInput")
    d_wv = nc.dram_tensor("wv", [D, D], F32, kind="ExternalInput")
    d_wp = nc.dram_tensor("wp", [D, D], F32, kind="ExternalInput")
    d_bp = nc.dram_tensor("bp", [1, D], F32, kind="ExternalInput")
    d_gw = nc.dram_tensor("gw", [D, 8], F32, kind="ExternalInput")
    d_gb = nc.dram_tensor("gb", [1, 8], F32, kind="ExternalInput")
    d_w1 = nc.dram_tensor("w1", [8, D, 4096], F32, kind="ExternalInput")
    d_w2 = nc.dram_tensor("w2", [8, 4096, D], F32, kind="ExternalInput")
    d_ws1 = nc.dram_tensor("ws1", [D, 4096], F32, kind="ExternalInput")
    d_ws2 = nc.dram_tensor("ws2", [4096, D], F32, kind="ExternalInput")
    d_b1t = nc.dram_tensor("b1t", [128, 288], F32, kind="ExternalInput")
    d_b2c = nc.dram_tensor("b2c", [9, D], F32, kind="ExternalInput")
    d_alibi = nc.dram_tensor("alibi", [128, 640], F32, kind="ExternalInput")
    d_masks = nc.dram_tensor("masksd", [4, 128, 512], F32, kind="ExternalInput")
    d_ones = nc.dram_tensor("onesd", [128, 512], F32, kind="ExternalInput")
    d_ident = nc.dram_tensor("identd", [128, 128], F32, kind="ExternalInput")
    d_sel = nc.dram_tensor("seld", [9, 8 * 128], F32, kind="ExternalInput")
    d_out = nc.dram_tensor("out", [D, T], F32, kind="ExternalOutput")
    if debug:
        d_dbg = {nm: nc.dram_tensor("dbg_" + nm, shp, F32, kind="ExternalOutput")
                 for nm, shp in [("xln", [128, 8 * T]), ("Qp", [128, T]), ("Kp", [128, T]),
                                 ("Vp", [128, 16 * 130]), ("ctx0", [64, T]), ("ctx1", [64, T]),
                                 ("Pp", [128, 6 * 512]), ("lr", [1, 512]), ("rb", [64, 512]),
                                 ("wqs", [128, 1024]), ("wks", [128, 1024]), ("wvs", [128, 1024]),
                                 ("cfmb", [16, T]), ("gbb", [128, 8]), ("ctm", [128, 16 * 9]),
                                 ("g_lg", [128, 8]), ("g_m1", [128, 1]), ("g_m2", [128, 1]),
                                 ("g_msk", [128, 8]), ("g_eq1", [128, 8]), ("g_eq2", [128, 8]),
                                 ("g_d", [128, 1]), ("g_e", [128, 1]), ("g_r", [128, 1]),
                                 ("g_gw2", [128, 1]), ("g_c1", [128, 8]), ("g_c2", [128, 8])]}

    es = contextlib.ExitStack()
    with es:
        sbt = lambda name, shp, dt: es.enter_context(nc.sbuf_tensor("sb_" + name, shp, dt))
        resid = sbt("resid", [128, 8 * T], F32)
        xln = sbt("xln", [128, 8 * T], BF16)
        b1t = sbt("b1t", [128, 288], F32)
        b2c = sbt("b2c", [9, D], BF16)
        bp = sbt("bp", [1, D], BF16)
        gw = sbt("gw", [128, 8 * 8], BF16)
        gb_row = sbt("gb_row", [1, 8], F32)
        gb_b = sbt("gb_b", [128, 8], F32)
        onesf = sbt("onesf", [1, 128], F32)
        onesb = sbt("onesb", [128, 512], BF16)
        alibi = sbt("alibi", [128, 640], F32)
        masks = sbt("masks", [128, 4 * 512], BF16)
        ident = sbt("ident", [128, 128], F32)
        sel = sbt("sel", [9, 8 * 128], BF16)
        ident_bf = sbt("ident_bf", [128, 128], BF16)
        tmp1 = sbt("tmp1", [128, 512], F32)
        tmp2 = sbt("tmp2", [128, 512], F32)
        tmpd = [tmp1, tmp2]
        xsq = [t[:, :].bitcast(BF16)[:, 0:512] for t in tmpd]
        srow = sbt("srow", [1, 2 * T], F32)
        msq = sbt("msq", [1, T], F32)

        pb = [es.enter_context(nc.psum_tensor(f"pb{i}", [128, 512], F32)) for i in range(7)]
        pb7 = es.enter_context(nc.psum_tensor("pb7", [128, 1024], BF16))
        s = {k: es.enter_context(nc.semaphore("s_" + k))
             for k in ("pe", "act", "dve", "ldx", "ldx2", "ldc", "ldgc",
                       "wpA", "wpB", "w1s", "w2s", "st")}

        ops = {"pe": [], "act": [], "dve": [], "pool": [], "sync": []}
        n = {k: 0 for k in ("pe", "act", "dve", "ldx", "ldx2", "ldc", "ldgc",
                            "wpA", "wpB", "w1s", "w2s", "st")}
        _esem = {"pe": "pe", "act": "act", "dve": "dve"}

        def emit(eng, fn, waits=None, inc=True, dma=None):
            w = {k: v for k, v in (waits or {}).items() if v > 0}
            inc_sem = dma if dma else (_esem.get(eng) if inc else None)
            amt = 16 if dma else 1
            ops[eng].append((w, fn, inc_sem, amt))
            if inc_sem:
                n[inc_sem] += amt
                return n[inc_sem]
            return None

        for ch in range(2):
            emit("sync", lambda e, ch=ch: e.dma_start(
                out=resid[:].rearrange("p (c t) -> p c t", c=8)[:, 4 * ch:4 * ch + 4],
                in_=d_x[:, :].rearrange("(c p) t -> p c t", p=128)[:, 4 * ch:4 * ch + 4]),
                dma="ldx" if ch == 0 else "ldx2")
        LDS_X = n["ldx"]
        for fn in (lambda e: e.dma_start(out=alibi[:, :], in_=d_alibi[:, :]),
                   lambda e: e.dma_start(out=ident[:, :], in_=d_ident[:, :]),
                   lambda e: e.dma_start(out=b1t[:, :], in_=d_b1t[:, :]),
                   lambda e: e.dma_start(out=onesf[:, :], in_=d_ones[0:1, 0:128]),
                   lambda e: e.dma_start(out=gb_row[:, :], in_=d_gb[:, :])):
            emit("sync", fn, dma="ldc")
        LDS_CONST = n["ldc"]

        for fn in (lambda e: e.dma_start(out=onesb[:, :], in_=d_ones[:, :]),
                   lambda e: e.dma_start(
                       out=masks[:].rearrange("p (r j) -> p r j", r=4),
                       in_=d_masks[:, :, :].rearrange("r p j -> p r j")),
                   lambda e: e.dma_start(out=bp[:, :], in_=d_bp[:, :]),
                   lambda e: e.dma_start(out=b2c[:, :], in_=d_b2c[:, :]),
                   lambda e: e.dma_start(out=sel[:, :], in_=d_sel[:, :]),
                   lambda e: e.dma_start(out=ident_bf[:, :], in_=d_ident[:, :]),
                   lambda e: e.dma_start(
                       out=gw[:].rearrange("p (c e) -> p c e", c=8),
                       in_=d_gw[:, :].rearrange("(c p) e -> p c e", p=128))):
            emit("pool", fn, dma="ldgc")
        LDG_CONST = n["ldgc"]

        def emit_layernorm(sums_ps, bc_ps, msq, pre=None):
            pre = dict(pre or {})
            cast_done = []
            for c in range(8):
                w = dict(pre) if c == 0 else {}
                w["ldx" if c < 4 else "ldx2"] = 16
                w["ldc"] = LDS_CONST
                w["ldgc"] = LDG_CONST
                cast_done.append(emit("act", lambda e, c=c: e.activation(
                    xln[:, c * T:(c + 1) * T], resid[:, c * T:(c + 1) * T], AF.Copy),
                    waits=w))
            sums_free = [0, 0]
            ev = [0, 0]
            for nt in range(4):
                def f_sum(e, nt=nt):
                    for c in range(8):
                        last = e.matmul(
                            sums_ps[0][:, :], onesb[:, 0:1],
                            xln[:, c * T + nt * 512:c * T + nt * 512 + 512],
                            start=(c == 0), stop=(c == 7))
                    return last
                pe_c = emit("pe", f_sum, waits={"act": cast_done[7],
                                                "dve": sums_free[0],
                                                "ldc": LDS_CONST,
                                                "ldgc": LDG_CONST})
                sums_free[0] = ev[0] = emit("dve", lambda e, nt=nt: e.tensor_copy(
                    srow[:, nt * 512:nt * 512 + 512], sums_ps[0][:, :]),
                    waits={"pe": pe_c})
            prev_mm = [0, 0]
            for nt in range(4):
                for c in range(8):
                    par = c % 2
                    sq_c = emit("act", lambda e, par=par, c=c, nt=nt: e.activation(
                        xsq[par][:, :], xln[:, c * T + nt * 512:c * T + nt * 512 + 512],
                        AF.Square), waits={"act": cast_done[c], "pe": prev_mm[par]})
                    pe_c = emit("pe", lambda e, par=par, c=c: e.matmul(
                        sums_ps[1][:, :], onesb[:, 0:1], xsq[par][:, :],
                        start=(c == 0), stop=(c == 7)),
                        waits={"act": sq_c,
                               "dve": sums_free[1] if c == 0 else 0})
                    prev_mm[par] = pe_c
                sums_free[1] = ev[1] = emit("dve", lambda e, nt=nt: e.tensor_copy(
                    srow[:, T + nt * 512:T + nt * 512 + 512], sums_ps[1][:, :]),
                    waits={"pe": pe_c})

            c0 = emit("dve", lambda e: e.tensor_scalar(
                srow[:, 0:T], srow[:, 0:T], 1.0 / 1024, None, ALU.mult),
                waits={"dve": max(ev)})
            c1 = emit("dve", lambda e: e.tensor_scalar(
                srow[:, T:2 * T], srow[:, T:2 * T], 1.0 / 1024, None, ALU.mult),
                waits={"dve": c0})
            c2 = emit("dve", lambda e: e.tensor_tensor(
                msq[:, :], srow[:, 0:T], srow[:, 0:T], ALU.mult), waits={"dve": c1})
            c3 = emit("dve", lambda e: e.tensor_tensor(
                srow[:, T:2 * T], srow[:, T:2 * T], msq[:, :], ALU.subtract),
                waits={"dve": c2})
            dve_st = emit("dve", lambda e: e.tensor_scalar(
                srow[:, T:2 * T], srow[:, T:2 * T], 1e-5, None, ALU.add),
                waits={"dve": c3})

            a0 = emit("act", lambda e: e.activation(
                srow[:, T:2 * T], srow[:, T:2 * T], AF.Ln), waits={"dve": dve_st})
            act_r = emit("act", lambda e: e.activation(
                srow[:, T:2 * T], srow[:, T:2 * T], AF.Exp, scale=-0.5),
                waits={"act": a0})

            norm_nt = [0, 0, 0, 0]
            bc_free = [0, 0]
            sub_done = [0, 0]
            for nt in range(4):
                bpar = nt % 2
                def f_bc(e, nt=nt, bpar=bpar):
                    e.matmul(bc_ps[2 * bpar][:, :], onesf[:, :],
                             srow[0:1, nt * 512:nt * 512 + 512],
                             start=True, stop=True)
                    return e.matmul(bc_ps[2 * bpar + 1][:, :], onesf[:, :],
                                    srow[0:1, T + nt * 512:T + nt * 512 + 512],
                                    start=True, stop=True)
                pe_c = emit("pe", f_bc, waits={"act": act_r, "dve": bc_free[bpar]})
                for c in range(8):
                    tpar = c % 2
                    s0 = emit("dve", lambda e, c=c, nt=nt, tpar=tpar, bpar=bpar:
                              e.tensor_tensor(
                                  tmpd[tpar][:, :],
                                  resid[:, c * T + nt * 512:c * T + nt * 512 + 512],
                                  bc_ps[2 * bpar][:, :], ALU.subtract),
                              waits={"pe": pe_c if c == 0 else 0,
                                     "dve": sub_done[tpar]})
                    sub_done[tpar] = emit("dve", lambda e, c=c, nt=nt, tpar=tpar, bpar=bpar:
                                          e.tensor_tensor(
                                              xln[:, c * T + nt * 512:c * T + nt * 512 + 512],
                                              tmpd[tpar][:, :], bc_ps[2 * bpar + 1][:, :],
                                              ALU.mult), waits={"dve": s0})
                bc_free[bpar] = norm_nt[nt] = max(sub_done)
            return norm_nt, (ev[0], ev[1])

        sums_ps = [pb[0][0:1, :], pb[1][0:1, :]]
        bc_ps = [pb[2], pb[3], pb[4], pb[5]]
        ln1_nt, ln1_sums_last = emit_layernorm(sums_ps, bc_ps, msq)
        ln1_done = max(ln1_nt)

        if phases >= 2:
            aes = contextlib.ExitStack()
            asb = lambda name, shp, dt: aes.enter_context(nc.sbuf_tensor("sb_" + name, shp, dt))
            wq_s = [asb(f"wq_s{i}", [128, 1024], BF16) for i in range(2)]
            wk_s = [asb(f"wk_s{i}", [128, 1024], BF16) for i in range(2)]
            wv_s = [asb(f"wv_s{i}", [128, 1024], BF16) for i in range(2)]
            wp0 = [asb(f"wp0_{i}", [64, 1024], BF16) for i in range(2)]
            wp1 = [asb(f"wp1_{i}", [64, 1024], BF16) for i in range(2)]
            Qp2 = [asb(f"Qp{i}", [128, T], BF16) for i in range(2)]
            Kp2 = [asb(f"Kp{i}", [128, T], BF16) for i in range(2)]
            Vp2 = [asb(f"Vp{i}", [128, 16 * 130], BF16) for i in range(2)]
            Vf = asb("Vf", [128, T], BF16)
            NP = 6
            Ppool = asb("Ppool", [128, NP * 512], BF16)
            ctx0 = asb("ctx0", [64, T], BF16)
            ctx1 = asb("ctx1", [64, T], BF16)
            lrow = asb("lrow", [1, 512], F32)
            rrow_h = [asb(f"rrow{i}", [1, 512], BF16) for i in range(2)]
            rbb1 = asb("rbb", [64, 512], BF16)
            rbb_h = [rbb1, rbb1]
            sc_ps = [pb[2], pb[3], pb[4]]
            qk_ps = [pb[0], pb[1]]
            ctx_ps = [pb[5], pb[6]]

            def f_vones(e):
                V4a = Vp2[0][:].rearrange("p (kt h d) -> p kt h d", kt=16, h=2)
                V4b = Vp2[1][:].rearrange("p (kt h d) -> p kt h d", kt=16, h=2)
                e.memset(V4a[:, :, 0:1, 64:65], 1.0)
                e.memset(V4a[:, :, 1:2, 64:65], 1.0)
                e.memset(V4b[:, :, 0:1, 64:65], 1.0)
                return e.memset(V4b[:, :, 1:2, 64:65], 1.0)
            vones = emit("dve", f_vones)

            wbuf_free = [0, 0]
            wload = [0] * 8

            def emit_pair_load(hp):
                par = hp % 2
                sem = "wpA" if par == 0 else "wpB"
                emit("pool", lambda e, hp=hp, par=par: e.dma_start(
                    out=wq_s[par][:].rearrange("p (c m) -> p c m", c=8),
                    in_=d_wq[:, hp * 128:(hp + 1) * 128].rearrange("(c p) m -> p c m", p=128)),
                    waits={"pe": wbuf_free[par]}, dma=sem)
                emit("pool", lambda e, hp=hp, par=par: e.dma_start(
                    out=wk_s[par][:].rearrange("p (c m) -> p c m", c=8),
                    in_=d_wk[:, hp * 128:(hp + 1) * 128].rearrange("(c p) m -> p c m", p=128)), dma=sem)
                emit("pool", lambda e, hp=hp, par=par: e.dma_start(
                    out=wv_s[par][:].rearrange("p (c m) -> p c m", c=8),
                    in_=d_wv[:, hp * 128:(hp + 1) * 128].rearrange("(c p) m -> p c m", p=128)), dma=sem)
                emit("pool", lambda e, hp=hp, par=par: e.dma_start(
                    out=wp0[par][:, :], in_=d_wp[hp * 128:hp * 128 + 64, :]), dma=sem)
                wload[hp] = emit("pool", lambda e, hp=hp, par=par: e.dma_start(
                    out=wp1[par][:, :], in_=d_wp[hp * 128 + 64:(hp + 1) * 128, :]), dma=sem)

            emit_pair_load(0)
            emit_pair_load(1)

            pj_free = [0]
            proj_done = [0] * 9
            vf_done = [0, 0, 0, 0]

            def make_proj_steps(hp):
                par = hp % 2
                sem = "wpA" if par == 0 else "wpB"
                steps = []

                def qk_step(wslc, dst, ntl, first, qpar):
                    def go():
                        def f(e, wslc=wslc, ntl=ntl, par=par, qpar=qpar):
                            for k in range(8):
                                last = e.matmul(
                                    qk_ps[qpar][:, :],
                                    wslc[par][:, k * 128:(k + 1) * 128],
                                    xln[:, k * T + ntl * 512:k * T + ntl * 512 + 512],
                                    start=(k == 0), stop=(k == 7))
                            return last
                        w = {"dve": max(qk_free[qpar], ln1_nt[ntl])}
                        if first:
                            w[sem] = wload[hp]
                        pe_c = emit("pe", f, waits=w)
                        qk_free[qpar] = emit("dve", lambda e, dst=dst, ntl=ntl, qpar=qpar:
                                             e.tensor_copy(
                                                 dst[:, ntl * 512:ntl * 512 + 512],
                                                 qk_ps[qpar][:, :]),
                                             waits={"pe": pe_c})
                    return go

                sidx = [0]
                for which, (wslc, dstl) in enumerate([(wq_s, Qp2), (wk_s, Kp2)]):
                    for ntl in range(4):
                        steps.append(qk_step(wslc, dstl[par], ntl,
                                             which == 0 and ntl == 0, sidx[0] % 2))
                        sidx[0] += 1

                def vf_step(ntl, qpar):
                    def go():
                        def f(e, ntl=ntl, par=par, qpar=qpar):
                            for k in range(8):
                                last = e.matmul(
                                    qk_ps[qpar][:, :],
                                    wv_s[par][:, k * 128:(k + 1) * 128],
                                    xln[:, k * T + ntl * 512:k * T + ntl * 512 + 512],
                                    start=(k == 0), stop=(k == 7))
                            return last
                        pe_c = emit("pe", f, waits={"dve": max(qk_free[qpar], ln1_nt[ntl])})
                        qk_free[qpar] = vf_done[ntl] = emit(
                            "dve", lambda e, ntl=ntl, qpar=qpar: e.tensor_copy(
                                Vf[:, ntl * 512:ntl * 512 + 512], qk_ps[qpar][:, :]),
                            waits={"pe": pe_c})
                    return go

                for ntl in range(4):
                    steps.append(vf_step(ntl, sidx[0] % 2))
                    sidx[0] += 1

                def vtr_step(kt, last_step):
                    def go():
                        pe_c = emit("pe", lambda e, kt=kt: e.transpose(
                            pb7[:, 0:128], Vf[:, kt * 128:(kt + 1) * 128], ident_bf[:, :]),
                            waits={"dve": max(vf_done[kt // 4], pj_free[0]),
                                   "ldc": LDS_CONST})
                        c = emit("dve", lambda e, kt=kt, par=par: e.tensor_copy(
                            Vp2[par][:, kt * 130:(kt + 1) * 130].rearrange(
                                "p (h d) -> p h d", h=2)[:, :, 0:64],
                            pb7[:, 0:128].rearrange("p (h d) -> p h d", h=2)),
                            waits={"pe": pe_c,
                                   "dve": vones if hp == 0 and kt == 0 else 0})
                        pj_free[0] = c
                        if last_step:
                            proj_done[hp] = c
                    return go

                for kt in range(16):
                    steps.append(vtr_step(kt, kt == 15))
                return steps

            qk_free = list(ln1_sums_last)
            pending_norm = []
            last_fevac = [0]
            sc_rr = [0]
            rb_pe = [0]
            pslot_free = [0] * 6
            pslot_rr = [0]
            sc_free = [0, 0, 0]
            ctx_free = [ln1_nt[3], 0]
            for st_ in make_proj_steps(0):
                st_()
            for hp in range(npair):
                par = hp % 2
                next_steps = make_proj_steps(hp + 1) if hp + 1 < npair else []

                band_fevac = {}
                proj_pending = []
                pq_rr = [0]

                def emit_proj_tile(dout, tt):
                    qpar = pq_rr[0] % 2
                    pq_rr[0] += 1

                    def fp(e, hp=hp, dout=dout, tt=tt, qpar=qpar, par=par):
                        first = True
                        if hp == 0:
                            e.matmul(qk_ps[qpar][:, :],
                                     bp[:, dout * 128:(dout + 1) * 128],
                                     onesb[0:1, :], start=True, stop=False)
                            first = False
                        e.matmul(qk_ps[qpar][:, :],
                                 wp0[par][:, dout * 128:(dout + 1) * 128],
                                 ctx0[:, tt * 512:tt * 512 + 512],
                                 start=first, stop=False)
                        return e.matmul(qk_ps[qpar][:, :],
                                        wp1[par][:, dout * 128:(dout + 1) * 128],
                                        ctx1[:, tt * 512:tt * 512 + 512],
                                        start=False, stop=True)
                    pe_c = emit("pe", fp,
                                waits={"dve": max(band_fevac[tt], qk_free[qpar])})
                    qk_free[qpar] = emit("dve", lambda e, dout=dout, tt=tt, qpar=qpar:
                                         e.tensor_tensor(
                                             resid[:, dout * T + tt * 512:dout * T + tt * 512 + 512],
                                             resid[:, dout * T + tt * 512:dout * T + tt * 512 + 512],
                                             qk_ps[qpar][:, :],
                                             ALU.add),
                                         waits={"pe": pe_c})

                def flush_norms():
                    while pending_norm:
                        hh_, b_, dve_c_ = pending_norm.pop(0)
                        pe_c = emit("pe", lambda e, hh_=hh_: e.matmul(
                            qk_ps[0][0:64, :], onesb[0:1, 0:64], rrow_h[hh_][:, :],
                            start=True, stop=True),
                            waits={"dve": max(dve_c_, qk_free[0])})
                        rb_pe[0] = pe_c
                        e0 = emit("dve", lambda e, hh_=hh_: e.tensor_copy(
                            rbb_h[hh_][:, :], qk_ps[0][0:64, :]),
                            waits={"pe": pe_c, "dve": last_fevac[0]})
                        qk_free[0] = e0
                        def fevac(e, hh_=hh_, b_=b_):
                            ctx = ctx0 if hh_ == 0 else ctx1
                            return e.tensor_tensor(ctx[:, b_ * 512:(b_ + 1) * 512],
                                                   ctx_ps[hh_][0:64, :],
                                                   rbb_h[hh_][:, :], ALU.mult)
                        ctx_free[hh_] = last_fevac[0] = emit("dve", fevac, waits={"dve": e0})
                        band_fevac[b_] = max(band_fevac.get(b_, 0), ctx_free[hh_])

                for b in range(NB):
                    nkt = 4 * (b + 1)
                    LKA = 3
                    exp_at = {}
                    mask_at = {}
                    slot_of = {}
                    pv_q = []
                    pv_last = [0, 0]

                    def emit_pv(hh, kt):
                        slot = slot_of[(hh, kt)]

                        def fpv(e, hh=hh, kt=kt, nkt=nkt, slot=slot, par=par):
                            return e.matmul(
                                ctx_ps[hh][0:65, :],
                                Vp2[par][:, kt * 130 + hh * 65:kt * 130 + hh * 65 + 65],
                                Ppool[:, slot * 512:(slot + 1) * 512],
                                start=(kt == 0), stop=(kt == nkt - 1))
                        w = {}
                        if (hh, kt) in mask_at:
                            w["dve"] = mask_at[(hh, kt)]
                        else:
                            w["act"] = exp_at[(hh, kt)]
                        if kt == 0:
                            w["dve"] = max(w.get("dve", 0), ctx_free[hh])
                        c = emit("pe", fpv, waits=w)
                        pslot_free[slot] = c
                        pv_last[hh] = c
                        return c

                    for kt in range(nkt):
                        while len(pv_q) > 2 * LKA - 2:
                            emit_pv(*pv_q.pop(0))
                        for hh in range(2):
                            spar = sc_rr[0] % 3
                            sc_rr[0] += 1
                            pe_c = emit("pe", lambda e, hh=hh, b=b, kt=kt, spar=spar, par=par:
                                        e.matmul(
                                            sc_ps[spar][:, :],
                                            Kp2[par][hh * 64:(hh + 1) * 64, kt * 128:(kt + 1) * 128],
                                            Qp2[par][hh * 64:(hh + 1) * 64, b * 512:(b + 1) * 512],
                                            start=True, stop=True),
                                        waits={"act": sc_free[spar],
                                               "dve": proj_done[hp] if (kt == 0 and hh == 0 and b == 0) else 0})
                            slot = pslot_rr[0] % NP
                            pslot_rr[0] += 1
                            slot_of[(hh, kt)] = slot
                            col = (2 * hp + hh) * 40 + [0, 4, 12, 24][b] + kt
                            act_c = emit("act", lambda e, kt=kt, spar=spar, col=col, slot=slot:
                                         e.activation(Ppool[:, slot * 512:(slot + 1) * 512],
                                                      sc_ps[spar][:, :], AF.Exp,
                                                      bias=alibi[:, col:col + 1], scale=0.125),
                                         waits={"pe": max(pe_c, pslot_free[slot])})
                            sc_free[spar] = act_c
                            exp_at[(hh, kt)] = act_c
                            r = kt - 4 * b
                            if r >= 0:
                                mask_at[(hh, kt)] = emit("dve", lambda e, kt=kt, r=r, slot=slot:
                                                         e.tensor_tensor(
                                                             Ppool[:, slot * 512:(slot + 1) * 512],
                                                             Ppool[:, slot * 512:(slot + 1) * 512],
                                                             masks[:, r * 512:(r + 1) * 512], ALU.mult),
                                                         waits={"act": act_c})
                            pv_q.append((hh, kt))
                        if next_steps:
                            next_steps.pop(0)()
                        if kt == 1:
                            flush_norms()
                            if b >= 1:
                                proj_pending.extend((dout, b - 1) for dout in range(8))
                        if proj_pending and kt >= 2:
                            emit_proj_tile(*proj_pending.pop(0))
                    while pv_q:
                        emit_pv(*pv_q.pop(0))

                    prev_tail = 0
                    for hh in range(2):
                        l0 = emit("dve", lambda e, hh=hh: e.reciprocal(
                            lrow[:, :], ctx_ps[hh][64:65, :]),
                            waits={"pe": max(pv_last[hh], rb_pe[0]),
                                   "dve": prev_tail})
                        dve_c = emit("dve", lambda e, hh=hh: e.tensor_copy(
                            rrow_h[hh][:, :], lrow[:, :]), waits={"dve": l0})
                        prev_tail = dve_c
                        pending_norm.append((hh, b, dve_c))
                flush_norms()
                proj_pending.extend((dout, 3) for dout in range(8))
                while proj_pending:
                    emit_proj_tile(*proj_pending.pop(0))
                while next_steps:
                    next_steps.pop(0)()
                wbuf_free[par] = n["pe"]
                if hp + 2 < npair:
                    emit_pair_load(hp + 2)
            attn_done = max(qk_free)
            if debug:
                for nm, buf in [("xln", xln), ("Qp", Qp2[0]), ("Kp", Kp2[0]), ("Vp", Vp2[0]),
                                ("ctx0", ctx0), ("ctx1", ctx1), ("Pp", Ppool),
                                ("lr", lrow), ("rb", rbb_h[0]),
                                ("wqs", wq_s[0]), ("wks", wk_s[0]), ("wvs", wv_s[0])]:
                    emit("pool", lambda e, nm=nm, buf=buf: e.dma_start(
                        out=d_dbg[nm][:, :], in_=buf[:, :]),
                        waits={"dve": n["dve"], "pe": n["pe"], "act": n["act"]}, dma="st")
            aes.close()
        else:
            attn_done = ln1_done


        if phases >= 3:
            sums_ps = [pb[0][0:1, :], pb[1][0:1, :]]
            bc_ps = [pb[2], pb[3], pb[4], pb[5]]
            ln2_nt, ln2_sums_last = emit_layernorm(sums_ps, bc_ps, msq, pre={"dve": attn_done})
            ln2_done = max(ln2_nt)
        else:
            ln2_done = attn_done


        if phases >= 4:
            fes = contextlib.ExitStack()
            fsb = lambda name, shp, dt: fes.enter_context(nc.sbuf_tensor("sb_" + name, shp, dt))
            w1g = fsb("w1g", [128, G * 1024], BF16)
            w2g = fsb("w2g", [128, G * 1024], BF16)
            hsb2 = fsb("hsb2", [128, 16 * 512], BF16)
            comb_fmb = fsb("comb_fmb", [16, T], BF16)
            comb_tm = fsb("comb_tm", [128, 16 * 9], BF16)
            comb_b = [fsb(f"comb_b{i}", [128, T], BF16) for i in range(2)]
            lg = fsb("lg", [128, 8], F32)
            g_m1 = fsb("g_m1", [128, 1], F32)
            g_m2 = fsb("g_m2", [128, 1], F32)
            g_eq1 = fsb("g_eq1", [128, 8], F32)
            g_eq2 = fsb("g_eq2", [128, 8], F32)
            g_msk = fsb("g_msk", [128, 8], F32)
            g_d = fsb("g_d", [128, 1], F32)
            g_e = fsb("g_e", [128, 1], F32)
            g_r = fsb("g_r", [128, 1], F32)
            g_gw2 = fsb("g_gw2", [128, 1], F32)
            g_c1 = fsb("g_c1", [128, 8], F32)
            g_c2 = fsb("g_c2", [128, 8], F32)
            g_ps = pb[6]
            tr_ps = pb7
            o_ps = [pb[2], pb[3], pb[4], pb[5]]
            h_ps = [pb[0], pb[1]]

            ctm_ones = emit("dve", lambda e: e.memset(
                comb_tm[:].rearrange("p (t e) -> p t e", e=9)[:, :, 8:9], 1.0),
                waits={"dve": ln2_done})
            pe_c = emit("pe", lambda e: e.matmul(g_ps[:, 0:8], onesf[:, :], gb_row[:, :],
                                                 start=True, stop=True),
                        waits={"ldc": LDS_CONST, "dve": ln2_done})
            gbb_done = emit("dve", lambda e: e.tensor_copy(gb_b[:, :], g_ps[:, 0:8]),
                            waits={"pe": pe_c})

            gate_st = {"g_ps_free": gbb_done, "tr_free": 0, "tr_evac": 0,
                       "comb_cast": 0, "steps": list(range(16))}

            def gate_tile(ttk):
                def f_lg(e, ttk=ttk):
                    for k in range(8):
                        last = e.matmul(g_ps[:, 0:8],
                                        xln[:, k * T + ttk * 128:k * T + (ttk + 1) * 128],
                                        gw[:, k * 8:(k + 1) * 8],
                                        start=(k == 0), stop=(k == 7))
                    return last
                pe_c = emit("pe", f_lg, waits={"dve": max(gate_st["g_ps_free"], ln2_nt[ttk // 4])})
                q0 = emit("dve", lambda e: e.tensor_tensor(
                    lg[:, :], g_ps[:, 0:8], gb_b[:, :], ALU.add), waits={"pe": pe_c})
                q1 = emit("dve", lambda e: e.tensor_reduce(
                    g_m1[:, :], lg[:, :], AX.X, ALU.max), waits={"dve": q0})
                q2 = emit("dve", lambda e: e.tensor_scalar(
                    g_eq1[:, :], lg[:, :], g_m1[:, 0:1], None, ALU.is_equal), waits={"dve": q1})
                q3 = emit("dve", lambda e: e.scalar_tensor_tensor(
                    g_msk[:, :], g_eq1[:, :], -1e30, lg[:, :], ALU.mult, ALU.add),
                    waits={"dve": q2})
                q4 = emit("dve", lambda e: e.tensor_reduce(
                    g_m2[:, :], g_msk[:, :], AX.X, ALU.max), waits={"dve": q3})
                q5 = emit("dve", lambda e: e.tensor_scalar(
                    g_eq2[:, :], g_msk[:, :], g_m2[:, 0:1], None, ALU.is_equal), waits={"dve": q4})
                d1 = emit("dve", lambda e: e.tensor_tensor(
                    g_d[:, :], g_m2[:, :], g_m1[:, :], ALU.subtract), waits={"dve": q5})
                a1 = emit("act", lambda e: e.activation(g_e[:, :], g_d[:, :], AF.Exp),
                          waits={"dve": d1})
                p0 = emit("dve", lambda e: e.tensor_scalar(
                    g_gw2[:, :], g_e[:, :], 1.0, None, ALU.add), waits={"act": a1})
                p1 = emit("dve", lambda e: e.reciprocal(
                    g_r[:, :], g_gw2[:, :]), waits={"dve": p0})
                p2 = emit("dve", lambda e: e.tensor_tensor(
                    g_gw2[:, :], g_e[:, :], g_r[:, :], ALU.mult), waits={"dve": p1})
                p3 = emit("dve", lambda e: e.tensor_scalar(
                    g_c1[:, :], g_eq1[:, :], g_r[:, 0:1], None, ALU.mult), waits={"dve": p2})
                p4 = emit("dve", lambda e: e.tensor_scalar(
                    g_c2[:, :], g_eq2[:, :], g_gw2[:, 0:1], None, ALU.mult), waits={"dve": p3})
                g_ps_free = emit("dve", lambda e, ttk=ttk: e.tensor_tensor(
                    comb_tm[:, ttk * 9:ttk * 9 + 8], g_c1[:, :], g_c2[:, :], ALU.add),
                    waits={"dve": p4})
                gate_st["g_ps_free"] = g_ps_free
                pe_c = emit("pe", lambda e, ttk=ttk: e.transpose(
                    tr_ps[0:9, 0:128], comb_tm[:, ttk * 9:ttk * 9 + 9], ident_bf[:, :]),
                    waits={"dve": max(g_ps_free, gate_st["tr_free"], ctm_ones),
                           "ldc": LDS_CONST})
                tr_evac = emit("dve", lambda e, ttk=ttk: e.tensor_copy(
                    comb_fmb[0:9, ttk * 128:(ttk + 1) * 128], tr_ps[0:9, 0:128]),
                    waits={"pe": pe_c})
                gate_st["tr_free"] = tr_evac
                gate_st["tr_evac"] = tr_evac
                gate_st["comb_cast"] = tr_evac

            gate_tick = [0]

            def maybe_gate():
                if gate_st["steps"]:
                    gate_tile(gate_st["steps"].pop(0))

            def drain_gate():
                while gate_st["steps"]:
                    maybe_gate()

            ffn_last = 0
            if phases >= 5:
                w1_ld = [0] * NEG
                w2_ld = [0] * NEG
                A_done = [0] * NEG
                C_done = [0] * NEG
                h_free = [0, 0]
                o_free = [ln2_nt[2], ln2_nt[2], ln2_nt[3], ln2_nt[3]]
                cb_done = [0, 0]
                cb_used = [0, 0]
                ffn_last = 0

                def emit_w_load(i):
                    o, g = divmod(i, NG)
                    e_idx = EXPERT_ORDER[o]
                    if e_idx == 8:
                        src1 = d_ws1[:, g * 1024:(g + 1) * 1024]
                        src2 = d_ws2[g * 1024:(g + 1) * 1024, :]
                    else:
                        src1 = d_w1[e_idx, :, g * 1024:(g + 1) * 1024]
                        src2 = d_w2[e_idx, g * 1024:(g + 1) * 1024, :]
                    w1_ld[i] = emit("pool", lambda e, src1=src1: e.dma_start(
                        out=w1g[:].rearrange("p (c m) -> p c m", c=8),
                        in_=src1.rearrange("(c p) m -> p c m", p=128)),
                        waits={"pe": A_done[i - 1]} if i else {"dve": attn_done}, dma="w1s")
                    w2_ld[i] = emit("pool", lambda e, src2=src2: e.dma_start(
                        out=w2g[:].rearrange("p (m d) -> p m d", m=8),
                        in_=src2.rearrange("(m p) d -> p m d", p=128)),
                        waits={"pe": C_done[i - 1]} if i else {"dve": attn_done}, dma="w2s")

                def emit_comb_b(e_idx):
                    par = e_idx % 2
                    evac = 0
                    for ntl in range(4):
                        bank = ntl % 2
                        pe_c = emit("pe", lambda e, e_idx=e_idx, ntl=ntl, bank=bank: e.matmul(
                            o_ps[bank][:, :], sel[0:9, e_idx * 128:(e_idx + 1) * 128],
                            comb_fmb[0:9, ntl * 512:(ntl + 1) * 512],
                            start=True, stop=True),
                            waits={"dve": max(gate_st["comb_cast"], o_free[bank],
                                              cb_used[par] if ntl == 0 else 0)})
                        evac = emit("dve", lambda e, par=par, ntl=ntl, bank=bank: e.tensor_copy(
                            comb_b[par][:, ntl * 512:(ntl + 1) * 512], o_ps[bank][:, :]),
                            waits={"pe": pe_c})
                        o_free[bank] = evac
                    cb_done[par] = evac

                emit_w_load(0)

                prev_B_pe = 0
                for o, e_idx in enumerate(EXPERT_ORDER):
                    if o + 1 < len(EXPERT_ORDER):
                        emit_comb_b(EXPERT_ORDER[o + 1])
                    par_e = e_idx % 2
                    for g in range(NG):
                        i = o * NG + g
                        if i >= ffn_groups:
                            continue
                        last_A = 0
                        last_B = 0
                        for half in range(2):
                            h_ready = []
                            for tl in range(2):
                                t = 2 * half + tl
                                for m in range(G):
                                    hpar = (t * G + m) % 2
                                    slot = tl * G + m

                                    def fA(e, t=t, m=m, hpar=hpar):
                                        for k in range(8):
                                            last = e.matmul(
                                                h_ps[hpar][:, :],
                                                w1g[:, k * 1024 + m * 128:k * 1024 + (m + 1) * 128],
                                                xln[:, k * T + t * 512:k * T + t * 512 + 512],
                                                start=(k == 0), stop=(k == 7))
                                        return last
                                    w = {"act": h_free[hpar]}
                                    if half == 0 and tl == 0 and m == 0:
                                        w["w1s"] = w1_ld[i]
                                        if i == 0:
                                            w["dve"] = ln2_done
                                    last_A = pe_c = emit("pe", fA, waits=w)
                                    if o == 0:
                                        gate_tick[0] += 1
                                        if gate_tick[0] % 6 == 0:
                                            maybe_gate()

                                    if ffn_parts < 2:
                                        h_ready.append(("act", 0))
                                        continue
                                    bcol = e_idx * 32 + g * 8 + m
                                    wg = {"pe": max(pe_c,
                                                    prev_B_pe if (tl == 0 and m == 0) else 0)}
                                    act_c = emit("act", lambda e, slot=slot, hpar=hpar, bcol=bcol:
                                                 e.activation(hsb2[:, slot * 512:(slot + 1) * 512],
                                                              h_ps[hpar][:, :], AF.Gelu,
                                                              bias=b1t[:, bcol:bcol + 1]),
                                                 waits=wg)
                                    h_free[hpar] = act_c
                                    if e_idx != 8:
                                        dve_c = emit("dve", lambda e, slot=slot, t=t, par_e=par_e:
                                                     e.tensor_tensor(
                                                         hsb2[:, slot * 512:(slot + 1) * 512],
                                                         hsb2[:, slot * 512:(slot + 1) * 512],
                                                         comb_b[par_e][:, t * 512:(t + 1) * 512],
                                                         ALU.mult),
                                                     waits={"act": act_c,
                                                            "dve": cb_done[par_e] if (half == 0 and tl == 0 and m == 0) else 0})
                                        cb_used[par_e] = dve_c
                                        h_ready.append(("dve", dve_c))
                                    else:
                                        h_ready.append(("act", act_c))
                            for tl in (range(2) if ffn_parts >= 3 else []):
                                t = 2 * half + tl
                                tslots = h_ready[tl * G:(tl + 1) * G]
                                h_act = max((c for k, c in tslots if k == "act"), default=0)
                                h_dve = max((c for k, c in tslots if k == "dve"), default=0)
                                for d in range(8):
                                    bank = d % 4
                                    inject_b2 = (e_idx == 7 and g == NG - 1)

                                    def fB(e, d=d, t=t, tl=tl, bank=bank, inject_b2=inject_b2):
                                        for m in range(G):
                                            last = e.matmul(
                                                o_ps[bank][:, :],
                                                w2g[:, m * 1024 + d * 128:m * 1024 + (d + 1) * 128],
                                                hsb2[:, (tl * G + m) * 512:(tl * G + m + 1) * 512],
                                                start=(m == 0), stop=(m == G - 1 and not inject_b2))
                                        if inject_b2:
                                            last = e.matmul(o_ps[bank][:, :],
                                                            b2c[0:9, d * 128:(d + 1) * 128],
                                                            comb_fmb[0:9, t * 512:(t + 1) * 512],
                                                            start=False, stop=True)
                                        return last
                                    w = {"act": h_act if d == 0 else 0,
                                         "dve": max(h_dve if d == 0 else 0, o_free[bank])}
                                    if half == 0 and tl == 0 and d == 0:
                                        w["w2s"] = w2_ld[i]
                                    last_B = pe_c = emit("pe", fB, waits=w)

                                    if ffn_parts >= 4:
                                        fn_ev = (lambda e, d=d, t=t, bank=bank:
                                                 e.tensor_tensor(resid[:, d * T + t * 512:d * T + t * 512 + 512],
                                                                 resid[:, d * T + t * 512:d * T + t * 512 + 512],
                                                                 o_ps[bank][:, :],
                                                                 ALU.add))
                                        o_free[bank] = ffn_last = emit(
                                            "dve", fn_ev, waits={"pe": pe_c})
                            prev_B_pe = last_B
                        A_done[i] = last_A
                        C_done[i] = last_B
                        if i + 1 < min(NEG, ffn_groups):
                            emit_w_load(i + 1)
                        if g == NG - 1:
                            drain_gate()
                            if o + 1 < len(EXPERT_ORDER):
                                emit_comb_b(EXPERT_ORDER[o + 1])
            fes.close()

        if debug and phases >= 4:
            for nm, buf in [("cfmb", comb_fmb), ("gbb", gb_b), ("ctm", comb_tm),
                            ("g_lg", lg), ("g_m1", g_m1), ("g_m2", g_m2), ("g_msk", g_msk),
                            ("g_eq1", g_eq1), ("g_eq2", g_eq2), ("g_d", g_d), ("g_e", g_e),
                            ("g_r", g_r), ("g_gw2", g_gw2), ("g_c1", g_c1), ("g_c2", g_c2)]:
                emit("pool", lambda e, nm=nm, buf=buf: e.dma_start(
                    out=d_dbg[nm][:, :], in_=buf[:, :]),
                    waits={"dve": n["dve"], "pe": n["pe"], "act": n["act"]}, dma="st")
        emit("sync", lambda e: e.dma_start(
            out=d_out[:, :].rearrange("(c p) t -> p c t", p=128),
            in_=resid[:].rearrange("p (c t) -> p c t", c=8)),
            waits={"dve": n["dve"], "act": n["act"], "pe": n["pe"]}, dma="st")
        emit("sync", lambda e: None, waits={"st": n["st"]}, inc=False)

        _check_schedule(ops, n)

        if True:
            _dispatch(nc, ops, s)
    return nc


def _check_schedule(ops, totals):
    sems = {k: 0 for k in totals}
    ptr = {k: 0 for k in ops}
    for k, v in totals.items():
        pass
    progressed = True
    while progressed:
        progressed = False
        for eng, lst in ops.items():
            while ptr[eng] < len(lst):
                w, fn, inc_sem, amt = lst[ptr[eng]]
                if all(sems[sk] >= sv for sk, sv in w.items()):
                    if inc_sem:
                        sems[inc_sem] += amt
                    ptr[eng] += 1
                    progressed = True
                else:
                    break
    stuck = {e: (ptr[e], len(l)) for e, l in ops.items() if ptr[e] < len(l)}
    if stuck:
        det = {}
        for eng, (p, ln) in stuck.items():
            det[eng] = (p, ln, ops[eng][p][0], dict(sems))
        raise RuntimeError(f"schedule deadlock: {det}")


def _dispatch(nc, ops, s):
    with nc.Block() as block:
        def body(eng_name):
            def run(engine):
                for w, fn, inc_sem, amt in ops[eng_name]:
                    for sk, sv in w.items():
                        engine.wait_ge(s[sk], sv)
                    last = fn(engine)
                    if inc_sem is not None and last is not None:
                        last.then_inc(s[inc_sem], amt)
            return run
        block.tensor(body("pe"))
        block.scalar(body("act"))
        block.vector(body("dve"))
        block.gpsimd(body("pool"))
        block.sync(body("sync"))


_NC_CACHE = []


def kernel(**inputs):
    x = np.ascontiguousarray(np.asarray(inputs["L0_attn_x"], dtype=np.float32))
    f32 = lambda k: np.ascontiguousarray(np.asarray(inputs[k], dtype=np.float32))
    wq, wk, wv, wp = f32("L0_attn_W_query"), f32("L0_attn_W_key"), f32("L0_attn_W_value"), f32("L0_attn_W_proj")
    bpv = f32("L0_attn_b_proj").reshape(1, D)
    gwv = f32("L0_ffn_gate_w")
    gbv = f32("L0_ffn_gate_b").reshape(1, 8)
    w1 = f32("L0_ffn_w1")
    b1 = f32("L0_ffn_b1")
    w2 = f32("L0_ffn_w2")
    b2 = f32("L0_ffn_b2")
    ws1 = f32("L0_ffn_ws1")
    bs1 = f32("L0_ffn_bs1")
    ws2 = f32("L0_ffn_ws2")
    bs2 = f32("L0_ffn_bs2")

    b1t = np.ascontiguousarray(
        np.concatenate([b1.reshape(8, 32, 128), bs1.reshape(1, 32, 128)], axis=0)
        .transpose(2, 0, 1).reshape(128, 288))
    b2c = np.ascontiguousarray(np.concatenate([b2, bs2.reshape(1, D)], axis=0))
    alibi, masksv, ones, identv, selv = _host_tables()

    if not _NC_CACHE:
        _NC_CACHE.append(build_program(phases=5))
    nc = _NC_CACHE[0]

    base = {"wq": wq, "wk": wk, "wv": wv, "wp": wp, "bp": bpv,
            "gw": gwv, "gb": gbv, "w1": w1, "w2": w2, "ws1": ws1, "ws2": ws2,
            "b1t": b1t, "b2c": b2c, "alibi": alibi, "masksd": masksv,
            "onesd": ones, "identd": identv, "seld": selv}
    in_maps = [dict(base, x=np.ascontiguousarray(x[c].T)) for c in range(8)]
    res = run_bass_kernel_spmd(nc, in_maps, core_ids=list(range(8)))
    out = np.stack([np.asarray(r["out"]).T for r in res.results], axis=0)
    return np.ascontiguousarray(out.astype(np.float32))


if __name__ == "__main__":
    import time
    t0 = time.time()
    nc = build_program()
    print(f"build ok {time.time()-t0:.1f}s n_inst={nc.n_instructions()}")


# version 17
# speedup vs baseline: 1.1087x; 1.0010x over previous
import contextlib

import numpy as np
import ml_dtypes

import concourse.bass as bass
import concourse.mybir as mybir
from concourse.bass_utils import run_bass_kernel_spmd

F32 = mybir.dt.float32
BF16 = mybir.dt.bfloat16
AF = mybir.ActivationFunctionType
ALU = mybir.AluOpType
AX = mybir.AxisListType

T = 2048
D = 1024
H = 16
NB = 4
G = 8
NG = 32 // G
EXPERT_ORDER = [8, 0, 1, 2, 3, 4, 5, 6, 7]
NEG = len(EXPERT_ORDER) * NG


def _host_tables():
    slopes = 2.0 ** (-8.0 * np.arange(1, H + 1) / H)
    boff = [0, 4, 12, 24]
    alibi = np.zeros((128, 640), np.float32)
    for h in range(H):
        for b in range(NB):
            for kt in range(4 * (b + 1)):
                k = 128 * kt + np.arange(128)
                alibi[:, h * 40 + boff[b] + kt] = slopes[h] * (k - (512 * (b + 1) - 1)) / 8.0
    masks = np.zeros((4, 128, 512), np.float32)
    p = np.arange(128)[:, None]
    j = np.arange(512)[None, :]
    for r in range(4):
        masks[r] = (j >= p + 128 * r).astype(np.float32)
    ones = np.ones((128, 512), np.float32)
    ident = np.eye(128, dtype=np.float32)
    sel = np.zeros((9, 8 * 128), np.float32)
    for e in range(8):
        sel[e, e * 128:(e + 1) * 128] = 1.0
    return alibi, masks, ones, ident, sel


def build_program(phases=5, npair=8, debug=False, ffn_groups=NEG, ffn_parts=4):
    nc = bass.Bass("TRN2", target_bir_lowering=False)

    d_x = nc.dram_tensor("x", [D, T], F32, kind="ExternalInput")
    d_wq = nc.dram_tensor("wq", [D, D], F32, kind="ExternalInput")
    d_wk = nc.dram_tensor("wk", [D, D], F32, kind="ExternalInput")
    d_wv = nc.dram_tensor("wv", [D, D], F32, kind="ExternalInput")
    d_wp = nc.dram_tensor("wp", [D, D], F32, kind="ExternalInput")
    d_bp = nc.dram_tensor("bp", [1, D], F32, kind="ExternalInput")
    d_gw = nc.dram_tensor("gw", [D, 8], F32, kind="ExternalInput")
    d_gb = nc.dram_tensor("gb", [1, 8], F32, kind="ExternalInput")
    d_w1 = nc.dram_tensor("w1", [8, D, 4096], F32, kind="ExternalInput")
    d_w2 = nc.dram_tensor("w2", [8, 4096, D], F32, kind="ExternalInput")
    d_ws1 = nc.dram_tensor("ws1", [D, 4096], F32, kind="ExternalInput")
    d_ws2 = nc.dram_tensor("ws2", [4096, D], F32, kind="ExternalInput")
    d_b1t = nc.dram_tensor("b1t", [128, 288], F32, kind="ExternalInput")
    d_b2c = nc.dram_tensor("b2c", [9, D], F32, kind="ExternalInput")
    d_alibi = nc.dram_tensor("alibi", [128, 640], F32, kind="ExternalInput")
    d_masks = nc.dram_tensor("masksd", [4, 128, 512], F32, kind="ExternalInput")
    d_ones = nc.dram_tensor("onesd", [128, 512], F32, kind="ExternalInput")
    d_ident = nc.dram_tensor("identd", [128, 128], F32, kind="ExternalInput")
    d_sel = nc.dram_tensor("seld", [9, 8 * 128], F32, kind="ExternalInput")
    d_out = nc.dram_tensor("out", [D, T], F32, kind="ExternalOutput")
    if debug:
        d_dbg = {nm: nc.dram_tensor("dbg_" + nm, shp, F32, kind="ExternalOutput")
                 for nm, shp in [("xln", [128, 8 * T]), ("Qp", [128, T]), ("Kp", [128, T]),
                                 ("Vp", [128, 16 * 130]), ("ctx0", [64, T]), ("ctx1", [64, T]),
                                 ("Pp", [128, 6 * 512]), ("lr", [1, 512]), ("rb", [64, 512]),
                                 ("wqs", [128, 1024]), ("wks", [128, 1024]), ("wvs", [128, 1024]),
                                 ("cfmb", [16, T]), ("gbb", [128, 8]), ("ctm", [128, 16 * 9]),
                                 ("g_lg", [128, 8]), ("g_m1", [128, 1]), ("g_m2", [128, 1]),
                                 ("g_msk", [128, 8]), ("g_eq1", [128, 8]), ("g_eq2", [128, 8]),
                                 ("g_d", [128, 1]), ("g_e", [128, 1]), ("g_r", [128, 1]),
                                 ("g_gw2", [128, 1]), ("g_c1", [128, 8]), ("g_c2", [128, 8])]}

    es = contextlib.ExitStack()
    with es:
        sbt = lambda name, shp, dt: es.enter_context(nc.sbuf_tensor("sb_" + name, shp, dt))
        resid = sbt("resid", [128, 8 * T], F32)
        xln = sbt("xln", [128, 8 * T], BF16)
        b1t = sbt("b1t", [128, 288], F32)
        b2c = sbt("b2c", [9, D], BF16)
        bp = sbt("bp", [1, D], BF16)
        gw = sbt("gw", [128, 8 * 8], BF16)
        gb_row = sbt("gb_row", [1, 8], F32)
        gb_b = sbt("gb_b", [128, 8], F32)
        onesf = sbt("onesf", [1, 128], F32)
        onesb = sbt("onesb", [128, 512], BF16)
        alibi = sbt("alibi", [128, 640], F32)
        masks = sbt("masks", [128, 4 * 512], BF16)
        sel = sbt("sel", [9, 8 * 128], BF16)
        ident_bf = sbt("ident_bf", [128, 128], BF16)
        arena = sbt("arena", [128, 34816], BF16)
        arena32 = arena[:, :].bitcast(F32)
        lg = sbt("lg", [128, 8], F32)
        g_m1 = sbt("g_m1", [128, 1], F32)
        g_m2 = sbt("g_m2", [128, 1], F32)
        g_eq1 = sbt("g_eq1", [128, 8], F32)
        g_eq2 = sbt("g_eq2", [128, 8], F32)
        g_msk = sbt("g_msk", [128, 8], F32)
        g_d = sbt("g_d", [128, 1], F32)
        g_e = sbt("g_e", [128, 1], F32)
        g_r = sbt("g_r", [128, 1], F32)
        g_gw2 = sbt("g_gw2", [128, 1], F32)
        g_c1 = sbt("g_c1", [128, 8], F32)
        g_c2 = sbt("g_c2", [128, 8], F32)
        comb_tm = sbt("comb_tm", [128, 16 * 9], BF16)
        tmp1 = sbt("tmp1", [128, 512], F32)
        tmp2 = sbt("tmp2", [128, 512], F32)
        tmpd = [tmp1, tmp2]
        xsq = [t[:, :].bitcast(BF16)[:, 0:512] for t in tmpd]
        srow = sbt("srow", [1, 2 * T], F32)
        msq = sbt("msq", [1, T], BF16)

        pb = [es.enter_context(nc.psum_tensor(f"pb{i}", [128, 512], F32)) for i in range(7)]
        pb7 = es.enter_context(nc.psum_tensor("pb7", [128, 1024], BF16))
        s = {k: es.enter_context(nc.semaphore("s_" + k))
             for k in ("pe", "act", "dve", "ldx", "ldx2", "ldc", "ldgc",
                       "wpA", "wpB", "w1s", "w2s", "st")}

        ops = {"pe": [], "act": [], "dve": [], "pool": [], "sync": []}
        n = {k: 0 for k in ("pe", "act", "dve", "ldx", "ldx2", "ldc", "ldgc",
                            "wpA", "wpB", "w1s", "w2s", "st")}
        _esem = {"pe": "pe", "act": "act", "dve": "dve"}

        def emit(eng, fn, waits=None, inc=True, dma=None):
            w = {k: v for k, v in (waits or {}).items() if v > 0}
            inc_sem = dma if dma else (_esem.get(eng) if inc else None)
            amt = 16 if dma else 1
            ops[eng].append((w, fn, inc_sem, amt))
            if inc_sem:
                n[inc_sem] += amt
                return n[inc_sem]
            return None

        for ch in range(2):
            emit("sync", lambda e, ch=ch: e.dma_start(
                out=resid[:].rearrange("p (c t) -> p c t", c=8)[:, 4 * ch:4 * ch + 4],
                in_=d_x[:, :].rearrange("(c p) t -> p c t", p=128)[:, 4 * ch:4 * ch + 4]),
                dma="ldx" if ch == 0 else "ldx2")
        LDS_X = n["ldx"]
        for fn in (lambda e: e.dma_start(out=alibi[:, :], in_=d_alibi[:, :]),
                   lambda e: e.dma_start(out=b1t[:, :], in_=d_b1t[:, :]),
                   lambda e: e.dma_start(out=onesf[:, :], in_=d_ones[0:1, 0:128]),
                   lambda e: e.dma_start(out=gb_row[:, :], in_=d_gb[:, :])):
            emit("sync", fn, dma="ldc")
        LDS_CONST = n["ldc"]

        for fn in (lambda e: e.dma_start(out=onesb[:, :], in_=d_ones[:, :]),
                   lambda e: e.dma_start(
                       out=masks[:].rearrange("p (r j) -> p r j", r=4),
                       in_=d_masks[:, :, :].rearrange("r p j -> p r j")),
                   lambda e: e.dma_start(out=bp[:, :], in_=d_bp[:, :]),
                   lambda e: e.dma_start(out=b2c[:, :], in_=d_b2c[:, :]),
                   lambda e: e.dma_start(out=sel[:, :], in_=d_sel[:, :]),
                   lambda e: e.dma_start(out=ident_bf[:, :], in_=d_ident[:, :]),
                   lambda e: e.dma_start(
                       out=gw[:].rearrange("p (c e) -> p c e", c=8),
                       in_=d_gw[:, :].rearrange("(c p) e -> p c e", p=128))):
            emit("pool", fn, dma="ldgc")
        LDG_CONST = n["ldgc"]

        def emit_layernorm(sums_ps, bc_ps, msq, pre=None):
            pre = dict(pre or {})
            cast_done = []
            for c in range(8):
                w = dict(pre) if c == 0 else {}
                w["ldx" if c < 4 else "ldx2"] = 16
                w["ldc"] = LDS_CONST
                w["ldgc"] = LDG_CONST
                cast_done.append(emit("act", lambda e, c=c: e.activation(
                    xln[:, c * T:(c + 1) * T], resid[:, c * T:(c + 1) * T], AF.Copy),
                    waits=w))
            sums_free = [0, 0]
            ev = [0, 0]
            for nt in range(4):
                def f_sum(e, nt=nt):
                    for c in range(8):
                        last = e.matmul(
                            sums_ps[0][:, :], onesb[:, 0:1],
                            xln[:, c * T + nt * 512:c * T + nt * 512 + 512],
                            start=(c == 0), stop=(c == 7))
                    return last
                pe_c = emit("pe", f_sum, waits={"act": cast_done[7],
                                                "dve": sums_free[0],
                                                "ldc": LDS_CONST,
                                                "ldgc": LDG_CONST})
                sums_free[0] = ev[0] = emit("dve", lambda e, nt=nt: e.tensor_copy(
                    srow[:, nt * 512:nt * 512 + 512], sums_ps[0][:, :]),
                    waits={"pe": pe_c})
            prev_mm = [0, 0]
            for nt in range(4):
                for c in range(8):
                    par = c % 2
                    sq_c = emit("act", lambda e, par=par, c=c, nt=nt: e.activation(
                        xsq[par][:, :], xln[:, c * T + nt * 512:c * T + nt * 512 + 512],
                        AF.Square), waits={"act": cast_done[c], "pe": prev_mm[par]})
                    pe_c = emit("pe", lambda e, par=par, c=c: e.matmul(
                        sums_ps[1][:, :], onesb[:, 0:1], xsq[par][:, :],
                        start=(c == 0), stop=(c == 7)),
                        waits={"act": sq_c,
                               "dve": sums_free[1] if c == 0 else 0})
                    prev_mm[par] = pe_c
                sums_free[1] = ev[1] = emit("dve", lambda e, nt=nt: e.tensor_copy(
                    srow[:, T + nt * 512:T + nt * 512 + 512], sums_ps[1][:, :]),
                    waits={"pe": pe_c})

            c0 = emit("dve", lambda e: e.tensor_scalar(
                srow[:, 0:T], srow[:, 0:T], 1.0 / 1024, None, ALU.mult),
                waits={"dve": max(ev)})
            c1 = emit("dve", lambda e: e.tensor_scalar(
                srow[:, T:2 * T], srow[:, T:2 * T], 1.0 / 1024, None, ALU.mult),
                waits={"dve": c0})
            c2 = emit("dve", lambda e: e.tensor_tensor(
                msq[:, :], srow[:, 0:T], srow[:, 0:T], ALU.mult), waits={"dve": c1})
            c3 = emit("dve", lambda e: e.tensor_tensor(
                srow[:, T:2 * T], srow[:, T:2 * T], msq[:, :], ALU.subtract),
                waits={"dve": c2})
            dve_st = emit("dve", lambda e: e.tensor_scalar(
                srow[:, T:2 * T], srow[:, T:2 * T], 1e-5, None, ALU.add),
                waits={"dve": c3})

            a0 = emit("act", lambda e: e.activation(
                srow[:, T:2 * T], srow[:, T:2 * T], AF.Ln), waits={"dve": dve_st})
            act_r = emit("act", lambda e: e.activation(
                srow[:, T:2 * T], srow[:, T:2 * T], AF.Exp, scale=-0.5),
                waits={"act": a0})

            norm_nt = [0, 0, 0, 0]
            bc_free = [0, 0]
            sub_done = [0, 0]
            for nt in range(4):
                bpar = nt % 2
                def f_bc(e, nt=nt, bpar=bpar):
                    e.matmul(bc_ps[2 * bpar][:, :], onesf[:, :],
                             srow[0:1, nt * 512:nt * 512 + 512],
                             start=True, stop=True)
                    return e.matmul(bc_ps[2 * bpar + 1][:, :], onesf[:, :],
                                    srow[0:1, T + nt * 512:T + nt * 512 + 512],
                                    start=True, stop=True)
                pe_c = emit("pe", f_bc, waits={"act": act_r, "dve": bc_free[bpar]})
                for c in range(8):
                    tpar = c % 2
                    s0 = emit("dve", lambda e, c=c, nt=nt, tpar=tpar, bpar=bpar:
                              e.tensor_tensor(
                                  tmpd[tpar][:, :],
                                  resid[:, c * T + nt * 512:c * T + nt * 512 + 512],
                                  bc_ps[2 * bpar][:, :], ALU.subtract),
                              waits={"pe": pe_c if c == 0 else 0,
                                     "dve": sub_done[tpar]})
                    sub_done[tpar] = emit("dve", lambda e, c=c, nt=nt, tpar=tpar, bpar=bpar:
                                          e.tensor_tensor(
                                              xln[:, c * T + nt * 512:c * T + nt * 512 + 512],
                                              tmpd[tpar][:, :], bc_ps[2 * bpar + 1][:, :],
                                              ALU.mult), waits={"dve": s0})
                bc_free[bpar] = norm_nt[nt] = max(sub_done)
            return norm_nt, (ev[0], ev[1])

        sums_ps = [pb[0][0:1, :], pb[1][0:1, :]]
        bc_ps = [pb[2], pb[3], pb[4], pb[5]]
        ln1_nt, ln1_sums_last = emit_layernorm(sums_ps, bc_ps, msq)
        ln1_done = max(ln1_nt)

        if phases >= 2:
            wq_s = [arena[:, 0:1024], arena[:, 1024:2048]]
            wk_s = [arena[:, 2048:3072], arena[:, 3072:4096]]
            wv_s = [arena[:, 4096:5120], arena[:, 5120:6144]]
            wp0 = [arena[0:64, 6144:7168], arena[0:64, 7168:8192]]
            wp1 = [arena[0:64, 8192:9216], arena[0:64, 9216:10240]]
            Qp2 = [arena[:, 10240:12288], arena[:, 12288:14336]]
            Kp2 = [arena[:, 14336:16384], arena[:, 16384:18432]]
            Vp2 = [arena[:, 18432:20512], arena[:, 20512:22592]]
            Vf = arena[:, 22592:24640]
            NP = 6
            Ppool = arena[:, 24640:27712]
            ctx0 = arena[0:64, 27712:29760]
            ctx1 = arena[0:64, 32320:34368]
            lrow = arena32[0:1, 14880:15392]
            rrow_h = [arena[0:1, 30784:31296], arena[0:1, 31296:31808]]
            rbb1 = arena[0:64, 31808:32320]
            rbb_h = [rbb1, rbb1]
            sc_ps = [pb[2], pb[3], pb[4]]
            qk_ps = [pb[0], pb[1]]
            ctx_ps = [pb[5], pb[6]]

            def f_vones(e):
                V4a = Vp2[0][:].rearrange("p (kt h d) -> p kt h d", kt=16, h=2)
                V4b = Vp2[1][:].rearrange("p (kt h d) -> p kt h d", kt=16, h=2)
                e.memset(V4a[:, :, 0:1, 64:65], 1.0)
                e.memset(V4a[:, :, 1:2, 64:65], 1.0)
                e.memset(V4b[:, :, 0:1, 64:65], 1.0)
                return e.memset(V4b[:, :, 1:2, 64:65], 1.0)
            vones = emit("dve", f_vones)

            wbuf_free = [0, 0]
            wload = [0] * 8

            def emit_pair_load(hp):
                par = hp % 2
                sem = "wpA" if par == 0 else "wpB"
                emit("pool", lambda e, hp=hp, par=par: e.dma_start(
                    out=wq_s[par][:].rearrange("p (c m) -> p c m", c=8),
                    in_=d_wq[:, hp * 128:(hp + 1) * 128].rearrange("(c p) m -> p c m", p=128)),
                    waits={"pe": wbuf_free[par]}, dma=sem)
                emit("pool", lambda e, hp=hp, par=par: e.dma_start(
                    out=wk_s[par][:].rearrange("p (c m) -> p c m", c=8),
                    in_=d_wk[:, hp * 128:(hp + 1) * 128].rearrange("(c p) m -> p c m", p=128)), dma=sem)
                emit("pool", lambda e, hp=hp, par=par: e.dma_start(
                    out=wv_s[par][:].rearrange("p (c m) -> p c m", c=8),
                    in_=d_wv[:, hp * 128:(hp + 1) * 128].rearrange("(c p) m -> p c m", p=128)), dma=sem)
                emit("pool", lambda e, hp=hp, par=par: e.dma_start(
                    out=wp0[par][:, :], in_=d_wp[hp * 128:hp * 128 + 64, :]), dma=sem)
                wload[hp] = emit("pool", lambda e, hp=hp, par=par: e.dma_start(
                    out=wp1[par][:, :], in_=d_wp[hp * 128 + 64:(hp + 1) * 128, :]), dma=sem)

            emit_pair_load(0)
            emit_pair_load(1)

            pj_free = [0]
            proj_done = [0] * 9
            vf_done = [0, 0, 0, 0]

            def make_proj_steps(hp):
                par = hp % 2
                sem = "wpA" if par == 0 else "wpB"
                steps = []

                def qk_step(wslc, dst, ntl, first, qpar):
                    def go():
                        def f(e, wslc=wslc, ntl=ntl, par=par, qpar=qpar):
                            for k in range(8):
                                last = e.matmul(
                                    qk_ps[qpar][:, :],
                                    wslc[par][:, k * 128:(k + 1) * 128],
                                    xln[:, k * T + ntl * 512:k * T + ntl * 512 + 512],
                                    start=(k == 0), stop=(k == 7))
                            return last
                        w = {"dve": max(qk_free[qpar], ln1_nt[ntl])}
                        if first:
                            w[sem] = wload[hp]
                        pe_c = emit("pe", f, waits=w)
                        qk_free[qpar] = emit("dve", lambda e, dst=dst, ntl=ntl, qpar=qpar:
                                             e.tensor_copy(
                                                 dst[:, ntl * 512:ntl * 512 + 512],
                                                 qk_ps[qpar][:, :]),
                                             waits={"pe": pe_c})
                    return go

                sidx = [0]
                for which, (wslc, dstl) in enumerate([(wq_s, Qp2), (wk_s, Kp2)]):
                    for ntl in range(4):
                        steps.append(qk_step(wslc, dstl[par], ntl,
                                             which == 0 and ntl == 0, sidx[0] % 2))
                        sidx[0] += 1

                def vf_step(ntl, qpar):
                    def go():
                        def f(e, ntl=ntl, par=par, qpar=qpar):
                            for k in range(8):
                                last = e.matmul(
                                    qk_ps[qpar][:, :],
                                    wv_s[par][:, k * 128:(k + 1) * 128],
                                    xln[:, k * T + ntl * 512:k * T + ntl * 512 + 512],
                                    start=(k == 0), stop=(k == 7))
                            return last
                        pe_c = emit("pe", f, waits={"dve": max(qk_free[qpar], ln1_nt[ntl])})
                        qk_free[qpar] = vf_done[ntl] = emit(
                            "dve", lambda e, ntl=ntl, qpar=qpar: e.tensor_copy(
                                Vf[:, ntl * 512:ntl * 512 + 512], qk_ps[qpar][:, :]),
                            waits={"pe": pe_c})
                    return go

                for ntl in range(4):
                    steps.append(vf_step(ntl, sidx[0] % 2))
                    sidx[0] += 1

                def vtr_step(kt, last_step):
                    def go():
                        pe_c = emit("pe", lambda e, kt=kt: e.transpose(
                            pb7[:, 0:128], Vf[:, kt * 128:(kt + 1) * 128], ident_bf[:, :]),
                            waits={"dve": max(vf_done[kt // 4], pj_free[0]),
                                   "ldc": LDS_CONST})
                        c = emit("dve", lambda e, kt=kt, par=par: e.tensor_copy(
                            Vp2[par][:, kt * 130:(kt + 1) * 130].rearrange(
                                "p (h d) -> p h d", h=2)[:, :, 0:64],
                            pb7[:, 0:128].rearrange("p (h d) -> p h d", h=2)),
                            waits={"pe": pe_c,
                                   "dve": vones if hp == 0 and kt == 0 else 0})
                        pj_free[0] = c
                        if last_step:
                            proj_done[hp] = c
                    return go

                for kt in range(16):
                    steps.append(vtr_step(kt, kt == 15))
                return steps

            qk_free = list(ln1_sums_last)
            pending_norm = []
            last_fevac = [0]
            sc_rr = [0]
            rb_pe = [0]
            pslot_free = [0] * 6
            pslot_rr = [0]
            sc_free = [0, 0, 0]
            ctx_free = [ln1_nt[3], 0]
            for st_ in make_proj_steps(0):
                st_()
            for hp in range(npair):
                par = hp % 2
                next_steps = make_proj_steps(hp + 1) if hp + 1 < npair else []

                band_fevac = {}
                proj_pending = []
                pq_rr = [0]

                def emit_proj_tile(dout, tt):
                    qpar = pq_rr[0] % 2
                    pq_rr[0] += 1

                    def fp(e, hp=hp, dout=dout, tt=tt, qpar=qpar, par=par):
                        first = True
                        if hp == 0:
                            e.matmul(qk_ps[qpar][:, :],
                                     bp[:, dout * 128:(dout + 1) * 128],
                                     onesb[0:1, :], start=True, stop=False)
                            first = False
                        e.matmul(qk_ps[qpar][:, :],
                                 wp0[par][:, dout * 128:(dout + 1) * 128],
                                 ctx0[:, tt * 512:tt * 512 + 512],
                                 start=first, stop=False)
                        return e.matmul(qk_ps[qpar][:, :],
                                        wp1[par][:, dout * 128:(dout + 1) * 128],
                                        ctx1[:, tt * 512:tt * 512 + 512],
                                        start=False, stop=True)
                    pe_c = emit("pe", fp,
                                waits={"dve": max(band_fevac[tt], qk_free[qpar])})
                    qk_free[qpar] = emit("dve", lambda e, dout=dout, tt=tt, qpar=qpar:
                                         e.tensor_tensor(
                                             resid[:, dout * T + tt * 512:dout * T + tt * 512 + 512],
                                             resid[:, dout * T + tt * 512:dout * T + tt * 512 + 512],
                                             qk_ps[qpar][:, :],
                                             ALU.add),
                                         waits={"pe": pe_c})

                def flush_norms():
                    while pending_norm:
                        hh_, b_, dve_c_ = pending_norm.pop(0)
                        pe_c = emit("pe", lambda e, hh_=hh_: e.matmul(
                            qk_ps[0][0:64, :], onesb[0:1, 0:64], rrow_h[hh_][:, :],
                            start=True, stop=True),
                            waits={"dve": max(dve_c_, qk_free[0])})
                        rb_pe[0] = pe_c
                        e0 = emit("dve", lambda e, hh_=hh_: e.tensor_copy(
                            rbb_h[hh_][:, :], qk_ps[0][0:64, :]),
                            waits={"pe": pe_c, "dve": last_fevac[0]})
                        qk_free[0] = e0
                        def fevac(e, hh_=hh_, b_=b_):
                            ctx = ctx0 if hh_ == 0 else ctx1
                            return e.tensor_tensor(ctx[:, b_ * 512:(b_ + 1) * 512],
                                                   ctx_ps[hh_][0:64, :],
                                                   rbb_h[hh_][:, :], ALU.mult)
                        ctx_free[hh_] = last_fevac[0] = emit("dve", fevac, waits={"dve": e0})
                        band_fevac[b_] = max(band_fevac.get(b_, 0), ctx_free[hh_])

                for b in range(NB):
                    nkt = 4 * (b + 1)
                    LKA = 3
                    exp_at = {}
                    mask_at = {}
                    slot_of = {}
                    pv_q = []
                    pv_last = [0, 0]

                    def emit_pv(hh, kt):
                        slot = slot_of[(hh, kt)]

                        def fpv(e, hh=hh, kt=kt, nkt=nkt, slot=slot, par=par):
                            return e.matmul(
                                ctx_ps[hh][0:65, :],
                                Vp2[par][:, kt * 130 + hh * 65:kt * 130 + hh * 65 + 65],
                                Ppool[:, slot * 512:(slot + 1) * 512],
                                start=(kt == 0), stop=(kt == nkt - 1))
                        w = {}
                        if (hh, kt) in mask_at:
                            w["dve"] = mask_at[(hh, kt)]
                        else:
                            w["act"] = exp_at[(hh, kt)]
                        if kt == 0:
                            w["dve"] = max(w.get("dve", 0), ctx_free[hh])
                        c = emit("pe", fpv, waits=w)
                        pslot_free[slot] = c
                        pv_last[hh] = c
                        return c

                    for kt in range(nkt):
                        while len(pv_q) > 2 * LKA - 2:
                            emit_pv(*pv_q.pop(0))
                        for hh in range(2):
                            spar = sc_rr[0] % 3
                            sc_rr[0] += 1
                            pe_c = emit("pe", lambda e, hh=hh, b=b, kt=kt, spar=spar, par=par:
                                        e.matmul(
                                            sc_ps[spar][:, :],
                                            Kp2[par][hh * 64:(hh + 1) * 64, kt * 128:(kt + 1) * 128],
                                            Qp2[par][hh * 64:(hh + 1) * 64, b * 512:(b + 1) * 512],
                                            start=True, stop=True),
                                        waits={"act": sc_free[spar],
                                               "dve": proj_done[hp] if (kt == 0 and hh == 0 and b == 0) else 0})
                            slot = pslot_rr[0] % NP
                            pslot_rr[0] += 1
                            slot_of[(hh, kt)] = slot
                            col = (2 * hp + hh) * 40 + [0, 4, 12, 24][b] + kt
                            act_c = emit("act", lambda e, kt=kt, spar=spar, col=col, slot=slot:
                                         e.activation(Ppool[:, slot * 512:(slot + 1) * 512],
                                                      sc_ps[spar][:, :], AF.Exp,
                                                      bias=alibi[:, col:col + 1], scale=0.125),
                                         waits={"pe": max(pe_c, pslot_free[slot])})
                            sc_free[spar] = act_c
                            exp_at[(hh, kt)] = act_c
                            r = kt - 4 * b
                            if r >= 0:
                                mask_at[(hh, kt)] = emit("dve", lambda e, kt=kt, r=r, slot=slot:
                                                         e.tensor_tensor(
                                                             Ppool[:, slot * 512:(slot + 1) * 512],
                                                             Ppool[:, slot * 512:(slot + 1) * 512],
                                                             masks[:, r * 512:(r + 1) * 512], ALU.mult),
                                                         waits={"act": act_c})
                            pv_q.append((hh, kt))
                        if next_steps:
                            next_steps.pop(0)()
                        if kt == 1:
                            flush_norms()
                            if b >= 1:
                                proj_pending.extend((dout, b - 1) for dout in range(8))
                        if proj_pending and kt >= 2:
                            emit_proj_tile(*proj_pending.pop(0))
                    while pv_q:
                        emit_pv(*pv_q.pop(0))

                    prev_tail = 0
                    for hh in range(2):
                        l0 = emit("dve", lambda e, hh=hh: e.reciprocal(
                            lrow[:, :], ctx_ps[hh][64:65, :]),
                            waits={"pe": max(pv_last[hh], rb_pe[0]),
                                   "dve": prev_tail})
                        dve_c = emit("dve", lambda e, hh=hh: e.tensor_copy(
                            rrow_h[hh][:, :], lrow[:, :]), waits={"dve": l0})
                        prev_tail = dve_c
                        pending_norm.append((hh, b, dve_c))
                flush_norms()
                proj_pending.extend((dout, 3) for dout in range(8))
                while proj_pending:
                    emit_proj_tile(*proj_pending.pop(0))
                while next_steps:
                    next_steps.pop(0)()
                wbuf_free[par] = n["pe"]
                if hp + 2 < npair:
                    emit_pair_load(hp + 2)
            attn_done = max(qk_free)
            if debug:
                for nm, buf in [("xln", xln), ("Qp", Qp2[0]), ("Kp", Kp2[0]), ("Vp", Vp2[0]),
                                ("ctx0", ctx0), ("ctx1", ctx1), ("Pp", Ppool),
                                ("lr", lrow), ("rb", rbb_h[0]),
                                ("wqs", wq_s[0]), ("wks", wk_s[0]), ("wvs", wv_s[0])]:
                    emit("pool", lambda e, nm=nm, buf=buf: e.dma_start(
                        out=d_dbg[nm][:, :], in_=buf[:, :]),
                        waits={"dve": n["dve"], "pe": n["pe"], "act": n["act"]}, dma="st")
        else:
            attn_done = ln1_done


        if phases >= 3:
            sums_ps = [pb[0][0:1, :], pb[1][0:1, :]]
            bc_ps = [pb[2], pb[3], pb[4], pb[5]]
            ln2_nt, ln2_sums_last = emit_layernorm(sums_ps, bc_ps, msq, pre={"dve": attn_done})
            ln2_done = max(ln2_nt)
        else:
            ln2_done = attn_done


        if phases >= 4:
            w1g = arena[:, 0:8192]
            w2g = arena[:, 8192:16384]
            hsb2 = arena[:, 16384:24576]
            comb_fmb = arena[0:9, 24576:26624]
            comb_b = [arena[:, 26624:28672], arena[:, 28672:30720]]
            g_ps = pb[6]
            tr_ps = pb7
            o_ps = [pb[2], pb[3], pb[4], pb[5]]
            h_ps = [pb[0], pb[1]]

            ctm_ones = emit("dve", lambda e: e.memset(
                comb_tm[:].rearrange("p (t e) -> p t e", e=9)[:, :, 8:9], 1.0),
                waits={"dve": ln2_done})
            pe_c = emit("pe", lambda e: e.matmul(g_ps[:, 0:8], onesf[:, :], gb_row[:, :],
                                                 start=True, stop=True),
                        waits={"ldc": LDS_CONST, "dve": ln2_done})
            gbb_done = emit("dve", lambda e: e.tensor_copy(gb_b[:, :], g_ps[:, 0:8]),
                            waits={"pe": pe_c})

            gate_st = {"g_ps_free": gbb_done, "tr_free": 0, "tr_evac": 0,
                       "comb_cast": 0, "steps": list(range(16))}

            def gate_tile(ttk):
                def f_lg(e, ttk=ttk):
                    for k in range(8):
                        last = e.matmul(g_ps[:, 0:8],
                                        xln[:, k * T + ttk * 128:k * T + (ttk + 1) * 128],
                                        gw[:, k * 8:(k + 1) * 8],
                                        start=(k == 0), stop=(k == 7))
                    return last
                pe_c = emit("pe", f_lg, waits={"dve": max(gate_st["g_ps_free"], ln2_nt[ttk // 4])})
                q0 = emit("dve", lambda e: e.tensor_tensor(
                    lg[:, :], g_ps[:, 0:8], gb_b[:, :], ALU.add), waits={"pe": pe_c})
                q1 = emit("dve", lambda e: e.tensor_reduce(
                    g_m1[:, :], lg[:, :], AX.X, ALU.max), waits={"dve": q0})
                q2 = emit("dve", lambda e: e.tensor_scalar(
                    g_eq1[:, :], lg[:, :], g_m1[:, 0:1], None, ALU.is_equal), waits={"dve": q1})
                q3 = emit("dve", lambda e: e.scalar_tensor_tensor(
                    g_msk[:, :], g_eq1[:, :], -1e30, lg[:, :], ALU.mult, ALU.add),
                    waits={"dve": q2})
                q4 = emit("dve", lambda e: e.tensor_reduce(
                    g_m2[:, :], g_msk[:, :], AX.X, ALU.max), waits={"dve": q3})
                q5 = emit("dve", lambda e: e.tensor_scalar(
                    g_eq2[:, :], g_msk[:, :], g_m2[:, 0:1], None, ALU.is_equal), waits={"dve": q4})
                d1 = emit("dve", lambda e: e.tensor_tensor(
                    g_d[:, :], g_m2[:, :], g_m1[:, :], ALU.subtract), waits={"dve": q5})
                a1 = emit("act", lambda e: e.activation(g_e[:, :], g_d[:, :], AF.Exp),
                          waits={"dve": d1})
                p0 = emit("dve", lambda e: e.tensor_scalar(
                    g_gw2[:, :], g_e[:, :], 1.0, None, ALU.add), waits={"act": a1})
                p1 = emit("dve", lambda e: e.reciprocal(
                    g_r[:, :], g_gw2[:, :]), waits={"dve": p0})
                p2 = emit("dve", lambda e: e.tensor_tensor(
                    g_gw2[:, :], g_e[:, :], g_r[:, :], ALU.mult), waits={"dve": p1})
                p3 = emit("dve", lambda e: e.tensor_scalar(
                    g_c1[:, :], g_eq1[:, :], g_r[:, 0:1], None, ALU.mult), waits={"dve": p2})
                p4 = emit("dve", lambda e: e.tensor_scalar(
                    g_c2[:, :], g_eq2[:, :], g_gw2[:, 0:1], None, ALU.mult), waits={"dve": p3})
                g_ps_free = emit("dve", lambda e, ttk=ttk: e.tensor_tensor(
                    comb_tm[:, ttk * 9:ttk * 9 + 8], g_c1[:, :], g_c2[:, :], ALU.add),
                    waits={"dve": p4})
                gate_st["g_ps_free"] = g_ps_free
                pe_c = emit("pe", lambda e, ttk=ttk: e.transpose(
                    tr_ps[0:9, 0:128], comb_tm[:, ttk * 9:ttk * 9 + 9], ident_bf[:, :]),
                    waits={"dve": max(g_ps_free, gate_st["tr_free"], ctm_ones),
                           "ldc": LDS_CONST})
                tr_evac = emit("dve", lambda e, ttk=ttk: e.tensor_copy(
                    comb_fmb[0:9, ttk * 128:(ttk + 1) * 128], tr_ps[0:9, 0:128]),
                    waits={"pe": pe_c})
                gate_st["tr_free"] = tr_evac
                gate_st["tr_evac"] = tr_evac
                gate_st["comb_cast"] = tr_evac

            gate_tick = [0]

            def maybe_gate():
                if gate_st["steps"]:
                    gate_tile(gate_st["steps"].pop(0))

            def drain_gate():
                while gate_st["steps"]:
                    maybe_gate()

            ffn_last = 0
            if phases >= 5:
                w1_ld = [0] * NEG
                w2_ld = [0] * NEG
                A_done = [0] * NEG
                C_done = [0] * NEG
                h_free = [0, 0]
                o_free = [ln2_nt[2], ln2_nt[2], ln2_nt[3], ln2_nt[3]]
                cb_done = [0, 0]
                cb_used = [0, 0]
                ffn_last = 0

                def emit_w_load(i):
                    o, g = divmod(i, NG)
                    e_idx = EXPERT_ORDER[o]
                    if e_idx == 8:
                        src1 = d_ws1[:, g * 1024:(g + 1) * 1024]
                        src2 = d_ws2[g * 1024:(g + 1) * 1024, :]
                    else:
                        src1 = d_w1[e_idx, :, g * 1024:(g + 1) * 1024]
                        src2 = d_w2[e_idx, g * 1024:(g + 1) * 1024, :]
                    w1_ld[i] = emit("pool", lambda e, src1=src1: e.dma_start(
                        out=w1g[:].rearrange("p (c m) -> p c m", c=8),
                        in_=src1.rearrange("(c p) m -> p c m", p=128)),
                        waits={"pe": A_done[i - 1]} if i else {"dve": attn_done}, dma="w1s")
                    w2_ld[i] = emit("pool", lambda e, src2=src2: e.dma_start(
                        out=w2g[:].rearrange("p (m d) -> p m d", m=8),
                        in_=src2.rearrange("(m p) d -> p m d", p=128)),
                        waits={"pe": C_done[i - 1]} if i else {"dve": attn_done}, dma="w2s")

                def emit_comb_b(e_idx):
                    par = e_idx % 2
                    evac = 0
                    for ntl in range(4):
                        bank = ntl % 2
                        pe_c = emit("pe", lambda e, e_idx=e_idx, ntl=ntl, bank=bank: e.matmul(
                            o_ps[bank][:, :], sel[0:9, e_idx * 128:(e_idx + 1) * 128],
                            comb_fmb[0:9, ntl * 512:(ntl + 1) * 512],
                            start=True, stop=True),
                            waits={"dve": max(gate_st["comb_cast"], o_free[bank],
                                              cb_used[par] if ntl == 0 else 0)})
                        evac = emit("dve", lambda e, par=par, ntl=ntl, bank=bank: e.tensor_copy(
                            comb_b[par][:, ntl * 512:(ntl + 1) * 512], o_ps[bank][:, :]),
                            waits={"pe": pe_c})
                        o_free[bank] = evac
                    cb_done[par] = evac

                emit_w_load(0)

                prev_B_pe = 0
                for o, e_idx in enumerate(EXPERT_ORDER):
                    if o + 1 < len(EXPERT_ORDER):
                        emit_comb_b(EXPERT_ORDER[o + 1])
                    par_e = e_idx % 2
                    for g in range(NG):
                        i = o * NG + g
                        if i >= ffn_groups:
                            continue
                        last_A = 0
                        last_B = 0
                        for half in range(2):
                            h_ready = []
                            for tl in range(2):
                                t = 2 * half + tl
                                for m in range(G):
                                    hpar = (t * G + m) % 2
                                    slot = tl * G + m

                                    def fA(e, t=t, m=m, hpar=hpar):
                                        for k in range(8):
                                            last = e.matmul(
                                                h_ps[hpar][:, :],
                                                w1g[:, k * 1024 + m * 128:k * 1024 + (m + 1) * 128],
                                                xln[:, k * T + t * 512:k * T + t * 512 + 512],
                                                start=(k == 0), stop=(k == 7))
                                        return last
                                    w = {"act": h_free[hpar]}
                                    if half == 0 and tl == 0 and m == 0:
                                        w["w1s"] = w1_ld[i]
                                        if i == 0:
                                            w["dve"] = ln2_done
                                    last_A = pe_c = emit("pe", fA, waits=w)
                                    if o == 0:
                                        gate_tick[0] += 1
                                        if gate_tick[0] % 6 == 0:
                                            maybe_gate()

                                    if ffn_parts < 2:
                                        h_ready.append(("act", 0))
                                        continue
                                    bcol = e_idx * 32 + g * 8 + m
                                    wg = {"pe": max(pe_c,
                                                    prev_B_pe if (tl == 0 and m == 0) else 0)}
                                    act_c = emit("act", lambda e, slot=slot, hpar=hpar, bcol=bcol:
                                                 e.activation(hsb2[:, slot * 512:(slot + 1) * 512],
                                                              h_ps[hpar][:, :], AF.Gelu,
                                                              bias=b1t[:, bcol:bcol + 1]),
                                                 waits=wg)
                                    h_free[hpar] = act_c
                                    if e_idx != 8:
                                        dve_c = emit("dve", lambda e, slot=slot, t=t, par_e=par_e:
                                                     e.tensor_tensor(
                                                         hsb2[:, slot * 512:(slot + 1) * 512],
                                                         hsb2[:, slot * 512:(slot + 1) * 512],
                                                         comb_b[par_e][:, t * 512:(t + 1) * 512],
                                                         ALU.mult),
                                                     waits={"act": act_c,
                                                            "dve": cb_done[par_e] if (half == 0 and tl == 0 and m == 0) else 0})
                                        cb_used[par_e] = dve_c
                                        h_ready.append(("dve", dve_c))
                                    else:
                                        h_ready.append(("act", act_c))
                            for tl in (range(2) if ffn_parts >= 3 else []):
                                t = 2 * half + tl
                                tslots = h_ready[tl * G:(tl + 1) * G]
                                h_act = max((c for k, c in tslots if k == "act"), default=0)
                                h_dve = max((c for k, c in tslots if k == "dve"), default=0)
                                for d in range(8):
                                    bank = d % 4
                                    inject_b2 = (e_idx == 7 and g == NG - 1)

                                    def fB(e, d=d, t=t, tl=tl, bank=bank, inject_b2=inject_b2):
                                        for m in range(G):
                                            last = e.matmul(
                                                o_ps[bank][:, :],
                                                w2g[:, m * 1024 + d * 128:m * 1024 + (d + 1) * 128],
                                                hsb2[:, (tl * G + m) * 512:(tl * G + m + 1) * 512],
                                                start=(m == 0), stop=(m == G - 1 and not inject_b2))
                                        if inject_b2:
                                            last = e.matmul(o_ps[bank][:, :],
                                                            b2c[0:9, d * 128:(d + 1) * 128],
                                                            comb_fmb[0:9, t * 512:(t + 1) * 512],
                                                            start=False, stop=True)
                                        return last
                                    w = {"act": h_act if d == 0 else 0,
                                         "dve": max(h_dve if d == 0 else 0, o_free[bank])}
                                    if half == 0 and tl == 0 and d == 0:
                                        w["w2s"] = w2_ld[i]
                                    last_B = pe_c = emit("pe", fB, waits=w)

                                    if ffn_parts >= 4:
                                        fn_ev = (lambda e, d=d, t=t, bank=bank:
                                                 e.tensor_tensor(resid[:, d * T + t * 512:d * T + t * 512 + 512],
                                                                 resid[:, d * T + t * 512:d * T + t * 512 + 512],
                                                                 o_ps[bank][:, :],
                                                                 ALU.add))
                                        o_free[bank] = ffn_last = emit(
                                            "dve", fn_ev, waits={"pe": pe_c})
                            prev_B_pe = last_B
                        A_done[i] = last_A
                        C_done[i] = last_B
                        if i + 1 < min(NEG, ffn_groups):
                            emit_w_load(i + 1)
                        if g == NG - 1:
                            drain_gate()
                            if o + 1 < len(EXPERT_ORDER):
                                emit_comb_b(EXPERT_ORDER[o + 1])

        if debug and phases >= 4:
            for nm, buf in [("cfmb", comb_fmb), ("gbb", gb_b), ("ctm", comb_tm),
                            ("g_lg", lg), ("g_m1", g_m1), ("g_m2", g_m2), ("g_msk", g_msk),
                            ("g_eq1", g_eq1), ("g_eq2", g_eq2), ("g_d", g_d), ("g_e", g_e),
                            ("g_r", g_r), ("g_gw2", g_gw2), ("g_c1", g_c1), ("g_c2", g_c2)]:
                emit("pool", lambda e, nm=nm, buf=buf: e.dma_start(
                    out=d_dbg[nm][:, :], in_=buf[:, :]),
                    waits={"dve": n["dve"], "pe": n["pe"], "act": n["act"]}, dma="st")
        emit("sync", lambda e: e.dma_start(
            out=d_out[:, :].rearrange("(c p) t -> p c t", p=128),
            in_=resid[:].rearrange("p (c t) -> p c t", c=8)),
            waits={"dve": n["dve"], "act": n["act"], "pe": n["pe"]}, dma="st")
        emit("sync", lambda e: None, waits={"st": n["st"]}, inc=False)

        _check_schedule(ops, n)

        if True:
            _dispatch(nc, ops, s)
    return nc


def _check_schedule(ops, totals):
    sems = {k: 0 for k in totals}
    ptr = {k: 0 for k in ops}
    for k, v in totals.items():
        pass
    progressed = True
    while progressed:
        progressed = False
        for eng, lst in ops.items():
            while ptr[eng] < len(lst):
                w, fn, inc_sem, amt = lst[ptr[eng]]
                if all(sems[sk] >= sv for sk, sv in w.items()):
                    if inc_sem:
                        sems[inc_sem] += amt
                    ptr[eng] += 1
                    progressed = True
                else:
                    break
    stuck = {e: (ptr[e], len(l)) for e, l in ops.items() if ptr[e] < len(l)}
    if stuck:
        det = {}
        for eng, (p, ln) in stuck.items():
            det[eng] = (p, ln, ops[eng][p][0], dict(sems))
        raise RuntimeError(f"schedule deadlock: {det}")


def _dispatch(nc, ops, s):
    with nc.Block() as block:
        def body(eng_name):
            def run(engine):
                for w, fn, inc_sem, amt in ops[eng_name]:
                    for sk, sv in w.items():
                        engine.wait_ge(s[sk], sv)
                    last = fn(engine)
                    if inc_sem is not None and last is not None:
                        last.then_inc(s[inc_sem], amt)
            return run
        block.tensor(body("pe"))
        block.scalar(body("act"))
        block.vector(body("dve"))
        block.gpsimd(body("pool"))
        block.sync(body("sync"))


_NC_CACHE = []


def kernel(**inputs):
    x = np.ascontiguousarray(np.asarray(inputs["L0_attn_x"], dtype=np.float32))
    f32 = lambda k: np.ascontiguousarray(np.asarray(inputs[k], dtype=np.float32))
    wq, wk, wv, wp = f32("L0_attn_W_query"), f32("L0_attn_W_key"), f32("L0_attn_W_value"), f32("L0_attn_W_proj")
    bpv = f32("L0_attn_b_proj").reshape(1, D)
    gwv = f32("L0_ffn_gate_w")
    gbv = f32("L0_ffn_gate_b").reshape(1, 8)
    w1 = f32("L0_ffn_w1")
    b1 = f32("L0_ffn_b1")
    w2 = f32("L0_ffn_w2")
    b2 = f32("L0_ffn_b2")
    ws1 = f32("L0_ffn_ws1")
    bs1 = f32("L0_ffn_bs1")
    ws2 = f32("L0_ffn_ws2")
    bs2 = f32("L0_ffn_bs2")

    b1t = np.ascontiguousarray(
        np.concatenate([b1.reshape(8, 32, 128), bs1.reshape(1, 32, 128)], axis=0)
        .transpose(2, 0, 1).reshape(128, 288))
    b2c = np.ascontiguousarray(np.concatenate([b2, bs2.reshape(1, D)], axis=0))
    alibi, masksv, ones, identv, selv = _host_tables()

    if not _NC_CACHE:
        _NC_CACHE.append(build_program(phases=5))
    nc = _NC_CACHE[0]

    base = {"wq": wq, "wk": wk, "wv": wv, "wp": wp, "bp": bpv,
            "gw": gwv, "gb": gbv, "w1": w1, "w2": w2, "ws1": ws1, "ws2": ws2,
            "b1t": b1t, "b2c": b2c, "alibi": alibi, "masksd": masksv,
            "onesd": ones, "identd": identv, "seld": selv}
    in_maps = [dict(base, x=np.ascontiguousarray(x[c].T)) for c in range(8)]
    res = run_bass_kernel_spmd(nc, in_maps, core_ids=list(range(8)))
    out = np.stack([np.asarray(r["out"]).T for r in res.results], axis=0)
    return np.ascontiguousarray(out.astype(np.float32))


if __name__ == "__main__":
    import time
    t0 = time.time()
    nc = build_program()
    print(f"build ok {time.time()-t0:.1f}s n_inst={nc.n_instructions()}")
